# version 18
# baseline (speedup 1.0000x reference)
import contextlib

import numpy as np
import ml_dtypes

import concourse.bass as bass
import concourse.mybir as mybir
from concourse.bass_utils import run_bass_kernel_spmd

F32 = mybir.dt.float32
BF16 = mybir.dt.bfloat16
I32 = mybir.dt.int32
AF = mybir.ActivationFunctionType
ALU = mybir.AluOpType
AX = mybir.AxisListType

B, T, D = 8, 2048, 1024
NCORES = 8
TCH = T // 128
DCH = D // 128
PI = float(np.pi)
TWO_PI = float(2 * np.pi)

ENGINES = ("tensor", "vector", "scalar", "gpsimd", "sync")
SERIAL = ("vector", "scalar", "gpsimd")

DEBUG_OUTS = True


class Sem:
    __slots__ = ("h", "n", "name")

    def __init__(self, h, name):
        self.h = h
        self.n = 0
        self.name = name


class KB:

    def __init__(self):
        self.nc = bass.Bass("TRN2", target_bir_lowering=False)
        self.ctx = contextlib.ExitStack()
        self.queues = {e: [] for e in ENGINES}
        self.tiles = {}
        self.sems = {}
        self.dram = {}

    def din(self, name, shape, dtype=F32):
        ap = self.nc.dram_tensor(name, list(shape), dtype,
                                 kind="ExternalInput").ap()
        self.dram[name] = ap
        return ap

    def dout(self, name, shape, dtype=F32):
        ap = self.nc.dram_tensor(name, list(shape), dtype,
                                 kind="ExternalOutput").ap()
        self.dram[name] = ap
        return ap

    def dscratch(self, name, shape, dtype=F32):
        ap = self.nc.dram_tensor(name, list(shape), dtype,
                                 kind="Internal").ap()
        self.dram[name] = ap
        return ap

    def sb(self, name, shape, dtype=F32):
        if name not in self.tiles:
            h = self.ctx.enter_context(
                self.nc.sbuf_tensor(name, list(shape), dtype))
            self.tiles[name] = h
        return self.tiles[name]

    def psum(self, name, shape, dtype=F32):
        if name not in self.tiles:
            h = self.ctx.enter_context(
                self.nc.psum_tensor(name, list(shape), dtype))
            self.tiles[name] = h
        return self.tiles[name]

    def sem(self, name):
        if name not in self.sems:
            h = self.ctx.enter_context(self.nc.semaphore(name))
            self.sems[name] = Sem(h, name)
        return self.sems[name]

    def on(self, eng, fn, waits=(), dma=False):
        wlist = []
        for w in waits:
            s, v = w
            if v > 0:
                wlist.append((s.h, v, s.name))
        if dma:
            s = self.sem(f"dma_{eng}_{dma}")
            k = 16
        else:
            s = self.sem("ser_" + eng)
            k = 1
            if eng in SERIAL and s.n > 0:
                wlist.append((s.h, s.n, s.name))
        s.n += k
        token = (s, s.n)

        def body(e, _w=tuple(wlist), _fn=fn, _sh=s.h, _k=k):
            for (h, v, _nm) in _w:
                e.wait_ge(h, v)
            inst = _fn(e)
            if inst is None:
                inst = e.nop()
            inst.then_inc(_sh, _k)

        self.queues[eng].append(body)
        return token

    def tok(self, eng, dma=None):
        s = self.sem(f"dma_{eng}_{dma}" if dma else "ser_" + eng)
        return (s, s.n)

    def view(self, src, shape, dtype, byte_off=0):
        h = self.tiles[src]
        ap = h[:].bitcast(dtype)
        esz = 2 if dtype == BF16 else 4
        off = byte_off // esz
        return ap[0:shape[0], off:off + shape[1]]

    def finish(self):
        block = self.ctx.enter_context(self.nc.Block())
        for eng in ENGINES:
            q = self.queues[eng]
            if not q:
                continue

            def run_all(e, _q=tuple(q)):
                for fn in _q:
                    fn(e)

            getattr(block, eng)(run_all)
        self.ctx.close()
        return self.nc


class Track:

    def __init__(self):
        self.d = {}

    def wait(self, key):
        return [self.d[key]] if key in self.d else []

    def set(self, key, token):
        self.d[key] = token



def emit_load_x(K: KB):
    x = K.dram["x"]
    x_tok = K.sb("x_tok", [128, TCH * D], F32)
    src = x.rearrange("(c p) d -> p c d", p=128)
    dst = x_tok[:].rearrange("p (c d) -> p c d", d=D)
    return K.on("sync", lambda e, s=src, d=dst: e.dma_start(out=d, in_=s),
                dma="x")


def emit_consts(K: KB):
    eye = K.sb("eye_bf16", [128, 128], BF16)
    t_eye = K.on("sync",
                 lambda e, d=eye: e.dma_start(out=d[:], in_=K.dram["c_eye"]),
                 dma="eye")
    K.sb("c_eps5", [128, 1], F32)
    K.sb("c_eps3", [128, 1], F32)
    K.on("vector", lambda e: e.memset(K.tiles["c_eps5"][:], 1e-5))
    t_ms = K.on("vector", lambda e: e.memset(K.tiles["c_eps3"][:], 1e-3))
    return t_eye, t_ms


def emit_layernorm(K: KB, tag, ready, t_eye=None, eps_name="c_eps5",
                   src="x_tok", stat_tag=None, store_xn=False):
    x_tok = K.tiles[src]
    stat_tag = tag if stat_tag is None else stat_tag
    mu = K.sb(f"mu{stat_tag}", [128, TCH], F32)
    nmr = K.sb(f"nmr{stat_tag}", [128, TCH], F32)
    rstd = K.sb(f"rstd{stat_tag}", [128, TCH], F32)
    ssq = K.sb(f"ssq{stat_tag}", [128, TCH], F32)
    sq_scr = K.sb("sq_scr", [128, D], F32)
    xn_chunk = K.sb("xn_chunk", [128, D], BF16)
    xnT = [K.sb(f"xnT{k}", [128, T], BF16) for k in range(DCH)]
    psA = K.psum("psA", [128, 2048], F32)
    pT2 = [psA[:, 0:512].bitcast(BF16), psA[:, 512:1024].bitcast(BF16)]
    eps = K.tiles[eps_name]
    eye = K.tiles["eye_bf16"]

    x3 = x_tok[:].rearrange("p (c d) -> p c d", d=D)
    K.on("vector",
         lambda e: e.reduce_sum(mu[:].rearrange("p (c o) -> p c o", o=1),
                                x3, axis=AX.X), waits=[ready])
    K.on("vector", lambda e: e.tensor_scalar_mul(mu[:], mu[:], 1.0 / D))
    for c in range(TCH):
        K.on("scalar",
             lambda e, c=c: e.activation(
                 sq_scr[:], x_tok[:, c * D:(c + 1) * D], AF.Square,
                 bias=mu[:, c:c + 1], scale=-1.0,
                 accum_out=ssq[:, c:c + 1]),
             waits=[K.tok("vector")] if c == 0 else [])
    K.on("scalar",
         lambda e: e.activation(rstd[:], ssq[:], AF.Ln,
                                bias=eps[:, 0:1], scale=1.0 / D))
    K.on("scalar",
         lambda e: e.activation(rstd[:], rstd[:], AF.Exp, scale=-0.5))
    K.on("vector",
         lambda e: e.tensor_tensor(nmr[:], mu[:], rstd[:], ALU.mult),
         waits=[K.tok("scalar")])
    t_stats = K.on("vector",
                   lambda e: e.tensor_scalar_mul(nmr[:], nmr[:], -1.0))

    tr_toks = {}
    ev_toks = {}
    norm_toks = {}
    st_toks = {}
    for c in range(TCH):
        pT = pT2[c % 2]
        w = [t_stats]
        if c >= 1:
            w.append(tr_toks[c - 1])
        if store_xn and c >= 1:
            w.append(st_toks[c - 1])
        norm_toks[c] = K.on(
            "scalar",
            lambda e, c=c: e.activation(
                xn_chunk[:], x_tok[:, c * D:(c + 1) * D], AF.Identity,
                bias=nmr[:, c:c + 1], scale=rstd[:, c:c + 1]),
            waits=w)
        if store_xn:
            st_toks[c] = K.on(
                "sync",
                lambda e, c=c: e.dma_start(
                    out=K.dram["xn_scr"][c * 128:(c + 1) * 128, :],
                    in_=xn_chunk[:]),
                waits=[norm_toks[c]], dma="xnst")
        for k in range(DCH):
            w = []
            if k == 0:
                w = [norm_toks[c]]
                if c == 0 and t_eye is not None:
                    w.append(t_eye)
                if c >= 2:
                    w.append(ev_toks[c - 2])
            tr_toks[c] = K.on(
                "tensor",
                lambda e, c=c, k=k, pT=pT: e.transpose(
                    pT[:, k * 128:(k + 1) * 128],
                    xn_chunk[:, k * 128:(k + 1) * 128], eye[:]),
                waits=w)
        for k in range(DCH):
            ev_toks[c] = K.on(
                "vector",
                lambda e, c=c, k=k, pT=pT: e.tensor_copy(
                    xnT[k][:, c * 128:(c + 1) * 128],
                    pT[:, k * 128:(k + 1) * 128]),
                waits=[tr_toks[c]] if k == 0 else [])
    return ev_toks[TCH - 1]


def emit_l0_attn(K: KB, xnT_ready, t_gpc, sq_guard=(), psA_guard=(),
                 psB_guard=(), dma_filler=None):
    H = 16
    xnT = [K.tiles[f"xnT{k}"] for k in range(DCH)]
    x_tok = K.tiles["x_tok"]
    psA = K.psum("psA", [128, 2048], F32)
    psB = K.psum("psB", [128, 2048], F32)

    v65 = K.sb("v65", [128, TCH * H * 65], BF16)
    v65v = v65[:].rearrange("p (c h o) -> p c h o", h=H, o=65)
    qT = K.sb("qT", [128, T], BF16)
    kTt = K.sb("kTt", [128, T], BF16)
    Ea = K.sb("Ea", [128, T], BF16)
    Eb = K.sb("Eb", [128, T], BF16)
    r4 = K.sb("r4", [128, 512], F32)
    ones64 = K.sb("ones64", [1, 64], F32)
    rrep = K.sb("rrep", [128, 512], F32)
    ctx_stg = K.sb("ctx_stg", [64, T], BF16)
    ctx_rd = [K.sb("ctx_rd0", [128, 1024], BF16),
              K.sb("ctx_rd1", [128, 1024], BF16)]
    wbig = K.sb("wbig", [128, DCH * 1024], BF16)
    wbigv = wbig[:].rearrange("p (kc o) -> p kc o", o=1024)
    wqp = K.sb("wqp", [128, 1024], BF16)
    wkp = K.sb("wkp", [128, 1024], BF16)
    xn0_scr = K.sb("xn0_scr", [128, D], F32)
    y_sb = K.sb("y_sb", [128, D], F32)
    yn_sb = y_sb
    a0_sb = xn0_scr
    sq_scr = K.sb("sq_scr", [128, D], F32)
    muy = K.sb("muy", [128, 1], F32)
    ssqy = K.sb("ssqy", [128, 1], F32)
    rstdy = K.sb("rstdy", [128, 1], F32)
    nmry = K.sb("nmry", [128, 1], F32)
    c_n16 = K.sb("c_n16", [128, 1], F32)
    ctx_scr = K.dscratch("ctx_scr", [H, 64, T], BF16)

    trk = Track()
    K.on("vector", lambda e: e.memset(ones64[:], 1.0))
    K.on("vector", lambda e: e.memset(c_n16[:], -16.0))
    t_ones = K.on("vector", lambda e: e.memset(v65v[:, :, :, 64:65], 1.0))

    wv = K.dram["wv"]
    t_wv = K.on("gpsimd",
                lambda e: e.dma_start(
                    out=wbigv, in_=wv.rearrange("(kc p) o -> p kc o", p=128)),
                dma="wbig")

    bv_rep = K.tiles["t_c_bv_rep"]
    bv3 = bv_rep[:].rearrange("p (h o) -> p h o", o=64)
    for c in range(TCH):
        for n in range(2):
            slot = (c * 2 + n) % 4
            pv = psA[:, slot * 512:(slot + 1) * 512]

            def mm(e, c=c, n=n, pv=pv):
                i = None
                for kc in range(DCH):
                    i = e.matmul(pv, xnT[kc][:, c * 128:(c + 1) * 128],
                                 wbigv[:, kc, n * 512:(n + 1) * 512],
                                 start=(kc == 0), stop=(kc == DCH - 1))
                return i
            w = list([t_wv, xnT_ready, t_ones] + list(psA_guard)) \
                if (c == 0 and n == 0) else []
            w = list(w) + trk.wait(("psA", slot))
            t_mm = K.on("tensor", mm, waits=w)

            def ev(e, c=c, n=n, pv=pv):
                return e.scalar_tensor_tensor(
                    v65v[:, c, n * 8:(n + 1) * 8, 0:64],
                    pv.rearrange("p (h o) -> p h o", o=64),
                    1.0, bv3[:, n * 8:(n + 1) * 8, :],
                    ALU.mult, ALU.add)
            wev = [t_mm]
            if c == 0 and n == 0:
                wev.append(t_gpc)
            t_ev = K.on("vector", ev, waits=wev)
            trk.set(("psA", slot), t_ev)
    v_prod = K.tok("tensor")

    wo = K.dram["wo"]
    t_wo = K.on("gpsimd",
                lambda e: e.dma_start(
                    out=wbigv, in_=wo.rearrange("(kc p) o -> p kc o", p=128)),
                waits=[v_prod], dma="wbig")

    bqc = K.tiles["t_c_bq_col"]
    bkc = K.tiles["t_c_bk_col"]

    def fill(n):
        if dma_filler is not None:
            for _ in range(n):
                if next(dma_filler, None) is None:
                    break

    fill(2)
    for p in range(8):
        fill(4)
        wq = K.dram["wq"]
        wk = K.dram["wk"]
        wqv = wq[:, p * 128:(p + 1) * 128].rearrange(
            "(kc q) o -> q kc o", q=128)
        wkv = wk[:, p * 128:(p + 1) * 128].rearrange(
            "(kc q) o -> q kc o", q=128)
        t_lq = K.on("gpsimd",
                    lambda e, s=wqv: e.dma_start(
                        out=wqp[:].rearrange("p (kc o) -> p kc o", o=128),
                        in_=s),
                    waits=trk.wait("wqp"), dma="wq")
        t_lk = K.on("gpsimd",
                    lambda e, s=wkv: e.dma_start(
                        out=wkp[:].rearrange("p (kc o) -> p kc o", o=128),
                        in_=s),
                    waits=trk.wait("wkp"), dma="wk")

        for (wtile, otile, bcol, t_l) in ((wqp, qT, bqc, t_lq),
                                          (wkp, kTt, bkc, t_lk)):
            for n in range(4):
                slot = n
                pv = psA[:, slot * 512:(slot + 1) * 512]

                def mm(e, wtile=wtile, n=n, pv=pv):
                    wv3 = wtile[:].rearrange("p (kc o) -> p kc o", o=128)
                    i = None
                    for kc in range(DCH):
                        i = e.matmul(pv, wv3[:, kc, :],
                                     xnT[kc][:, n * 512:(n + 1) * 512],
                                     start=(kc == 0), stop=(kc == DCH - 1))
                    return i
                w = [t_l] + trk.wait(("psA", slot)) + trk.wait("psA_free")
                t_mm = K.on("tensor", mm, waits=w)

                wev2 = [t_mm]
                if p == 0 and n == 0:
                    wev2.append(K.tok("sync", dma="bqk"))
                t_ev = K.on(
                    "scalar",
                    lambda e, otile=otile, bcol=bcol, n=n, pv=pv, p=p:
                    e.activation(otile[:, n * 512:(n + 1) * 512], pv,
                                 AF.Identity, bias=bcol[:, p:p + 1]),
                    waits=wev2)
                trk.set(("psA", slot), t_ev)
        trk.set("wqp", K.tok("scalar"))
        trk.set("wkp", K.tok("scalar"))
        qk_done = K.tok("scalar")

        for hh in range(2):
            h = 2 * p + hh
            sT = K.sem("ser_tensor")
            sA = K.sem("ser_scalar")
            t0 = sT.n
            a0 = sA.n
            for c in range(TCH):
                Ebuf = Ea if c % 2 == 0 else Eb
                for half in range(2):
                    def smm(e, c=c, hh=hh, half=half):
                        lhs = kTt[hh * 64:(hh + 1) * 64,
                                  c * 128:(c + 1) * 128]
                        i = None
                        for n in (2 * half, 2 * half + 1):
                            i = e.matmul(psA[:, n * 512:(n + 1) * 512], lhs,
                                         qT[hh * 64:(hh + 1) * 64,
                                            n * 512:(n + 1) * 512],
                                         start=True, stop=True)
                        return i
                    if c == 0:
                        w = ([qk_done] + trk.wait("psA_free")) \
                            if half == 0 else []
                    else:
                        w = [(sA, a0 + 2 * (c - 1) + half + 1)]
                    K.on("tensor", smm, waits=w)
                for half in range(2):
                    wex = [(sT, t0 + 4 * c + half + 1)]
                    K.on("scalar",
                         lambda e, half=half, Ebuf=Ebuf: e.activation(
                             Ebuf[:, half * 1024:(half + 1) * 1024],
                             psA[:, half * 1024:(half + 1) * 1024],
                             AF.Exp, bias=c_n16[:, 0:1], scale=0.125),
                         waits=wex)
                for half in range(2):
                    def cmm(e, c=c, h=h, half=half, Ebuf=Ebuf):
                        i = None
                        for n in (2 * half, 2 * half + 1):
                            i = e.matmul(psB[0:65, n * 512:(n + 1) * 512],
                                         v65v[:, c, h, :],
                                         Ebuf[:, n * 512:(n + 1) * 512],
                                         start=(c == 0), stop=(c == TCH - 1))
                        return i
                    w = [(sA, a0 + 2 * c + half + 1)]
                    if c == 0 and half == 0:
                        w += trk.wait("psB_free")
                        if p == 0 and hh == 0:
                            w += list(psB_guard)
                    K.on("tensor", cmm, waits=w)
            ctx_done = K.tok("tensor")

            bc_prev = None
            for n in range(4):
                w = [ctx_done] if n == 0 else [bc_prev]
                t_rc = K.on(
                    "vector",
                    lambda e, n=n: e.reciprocal(
                        r4[0:1, 0:512],
                        psB[64:65, n * 512:(n + 1) * 512]),
                    waits=w)
                w = [t_rc]
                if n == 0:
                    w.append((sA, a0 + 2 * TCH))
                bc_prev = K.on(
                    "tensor",
                    lambda e, n=n: e.matmul(
                        psA[0:64, n * 512:(n + 1) * 512], ones64[0:1, :],
                        r4[0:1, 0:512],
                        start=True, stop=True),
                    waits=w)
                t_cp = K.on(
                    "scalar",
                    lambda e, n=n: e.copy(rrep[0:64, 0:512],
                                          psA[0:64, n * 512:(n + 1) * 512]),
                    waits=[bc_prev] + trk.wait("rrep"))
                t_ml = K.on(
                    "vector",
                    lambda e, n=n: e.tensor_tensor(
                        ctx_stg[:, n * 512:(n + 1) * 512],
                        psB[0:64, n * 512:(n + 1) * 512],
                        rrep[0:64, 0:512], ALU.mult),
                    waits=[t_cp] + (trk.wait("ctx_stg") if n == 0 else []))
                trk.set("rrep", t_ml)
            trk.set("psB_free", K.tok("vector"))
            t_st = K.on("sync",
                        lambda e, h=h: e.dma_start(out=ctx_scr[h],
                                                   in_=ctx_stg[:]),
                        waits=[K.tok("vector")], dma="ctxst")
            trk.set("ctx_stg", t_st)
            trk.set("psA_free", K.tok("scalar"))

    fill(64)
    g_rep = K.tiles["t_c_gamma_rep"]
    be_rep = K.tiles["t_c_beta_rep"]
    bo_rep = K.tiles["t_c_bo_rep"]
    rstd0 = K.tiles["rstd0"]
    nmr0 = K.tiles["nmr0"]
    c_eps3 = K.tiles["c_eps3"]
    for c in range(TCH):
        buf = ctx_rd[c % 2]
        src = ctx_scr.rearrange("(kc h2) dv t -> (h2 dv) kc t", h2=2)[
            :, :, c * 128:(c + 1) * 128]
        dst = buf[:].rearrange("p (kc t) -> p kc t", t=128)
        t_ld = K.on("sync",
                    lambda e, s=src, d=dst: e.dma_start(out=d, in_=s),
                    waits=[K.tok("sync", dma="ctxst")] +
                    trk.wait(("ctx_rd", c % 2)), dma=f"ctxrd{c % 2}")

        pao = psB[:, (c % 2) * 1024:((c % 2) + 1) * 1024]

        def mm(e, buf=buf, pao=pao):
            b3 = buf[:].rearrange("p (kc t) -> p kc t", t=128)
            i = None
            for n in range(2):
                for kc in range(DCH):
                    i = e.matmul(pao[:, n * 512:(n + 1) * 512], b3[:, kc, :],
                                 wbigv[:, kc, n * 512:(n + 1) * 512],
                                 start=(kc == 0), stop=(kc == DCH - 1))
            return i
        w = [t_ld, t_wo] + trk.wait(("psB_ao", c % 2))
        if c < 2:
            w += trk.wait("psB_free")
        t_mm = K.on("tensor", mm, waits=w)
        trk.set(("ctx_rd", c % 2), t_mm)

        t_xn0 = K.on("scalar",
                     lambda e, c=c: e.activation(
                         xn0_scr[:], x_tok[:, c * D:(c + 1) * D], AF.Identity,
                         bias=nmr0[:, c:c + 1], scale=rstd0[:, c:c + 1]),
                     waits=trk.wait("finchunk"))
        wy = [t_mm, t_xn0]
        if c == 0:
            wy.append(t_gpc)
        t_y = K.on("vector",
                   lambda e, pao=pao: e.scalar_tensor_tensor(
                       y_sb[:], pao, 1.0, xn0_scr[:], ALU.mult, ALU.add),
                   waits=wy)
        trk.set(("psB_ao", c % 2), t_y)
        K.on("vector",
             lambda e: e.tensor_tensor(y_sb[:], y_sb[:], bo_rep[:], ALU.add))
        K.on("vector",
             lambda e: e.reduce_sum(muy[:].rearrange("p (c o) -> p c o", o=1),
                                    y_sb[:].rearrange("p (c d) -> p c d", c=1),
                                    axis=AX.X))
        t_mu = K.on("vector",
                    lambda e: e.tensor_scalar_mul(muy[:], muy[:], 1.0 / D))
        wsq = [t_mu]
        if c == 0:
            wsq += list(sq_guard)
        K.on("scalar",
             lambda e: e.activation(sq_scr[:], y_sb[:], AF.Square,
                                    bias=muy[:, 0:1], scale=-1.0,
                                    accum_out=ssqy[:, 0:1]),
             waits=wsq)
        K.on("scalar",
             lambda e: e.activation(rstdy[:], ssqy[:], AF.Ln,
                                    bias=c_eps3[:, 0:1], scale=1.0 / D))
        t_rs = K.on("scalar",
                    lambda e: e.activation(rstdy[:], rstdy[:], AF.Exp,
                                           scale=-0.5))
        K.on("vector",
             lambda e: e.tensor_tensor(nmry[:], muy[:], rstdy[:], ALU.mult),
             waits=[t_rs])
        t_nm = K.on("vector",
                    lambda e: e.tensor_scalar_mul(nmry[:], nmry[:], -1.0))
        t_yn = K.on("scalar",
                    lambda e: e.activation(yn_sb[:], y_sb[:], AF.Identity,
                                           bias=nmry[:, 0:1],
                                           scale=rstdy[:, 0:1]),
                    waits=[t_nm])
        K.on("vector",
             lambda e: e.tensor_tensor(a0_sb[:], yn_sb[:], g_rep[:],
                                       ALU.mult),
             waits=[t_yn])
        K.on("vector",
             lambda e: e.tensor_tensor(a0_sb[:], a0_sb[:], be_rep[:],
                                       ALU.add))
        t_fin = K.on("vector",
                     lambda e, c=c: e.tensor_tensor(
                         x_tok[:, c * D:(c + 1) * D],
                         x_tok[:, c * D:(c + 1) * D],
                         a0_sb[:], ALU.add))
        trk.set("finchunk", t_fin)
    return K.tok("vector")


def emit_l0_router(K: KB, xnT_ready):
    E = 32
    xnT = [K.tiles[f"xnT{k}"] for k in range(DCH)]
    psA = K.psum("psA", [128, 2048], F32)
    s_all = K.view("ctx_rd0", [128, 512], F32, 0)
    s3 = s_all.rearrange("p (c e) -> p c e", e=E)
    rt_scr = K.view("r4", [128, 512], F32, 0)
    rt3 = rt_scr.rearrange("p (c e) -> p c e", e=E)
    wrt = K.sb("wrt", [128, 512], F32)
    ge_bf = K.sb("ge_bf", [128, 512], BF16)
    ge3 = ge_bf[:].rearrange("p (c e) -> p c e", e=E)
    rt_r = K.sb("rt_r", [128, TCH], F32)
    rt_r3 = rt_r[:].rearrange("p (c o) -> p c o", o=1)
    rt_m = K.tiles["muy"]
    rt_sum = K.tiles["ssqy"]
    trk = Track()
    bA = K.tok("scalar")
    bV = K.tok("vector")

    wg = K.dram["wg"]
    wg_sb = K.sb("wrt_wg", [128, DCH * 32], BF16)
    wgv = wg_sb[:].rearrange("p (kc o) -> p kc o", o=32)
    t_wg = K.on("gpsimd",
                lambda e: e.dma_start(
                    out=wgv, in_=wg.rearrange("(kc p) o -> p kc o", p=128)),
                dma="wg")
    for c in range(TCH):
        slot = c % 4
        pv = psA[:, slot * 512:slot * 512 + 32]

        def mm(e, c=c, pv=pv):
            i = None
            for kc in range(DCH):
                i = e.matmul(pv, xnT[kc][:, c * 128:(c + 1) * 128],
                             wgv[:, kc, :],
                             start=(kc == 0), stop=(kc == DCH - 1))
            return i
        w = [t_wg, xnT_ready, bA, bV] if c == 0 else []
        w = list(w) + trk.wait(("rps", slot))
        t_mm = K.on("tensor", mm, waits=w)

        K.on("vector",
             lambda e, pv=pv: e.reduce_max(
                 rt_m[:].rearrange("p (c o) -> p c o", o=1),
                 pv.rearrange("p (c e) -> p c e", c=1), axis=AX.X),
             waits=[t_mm])
        K.on("vector",
             lambda e: e.tensor_scalar_mul(rt_m[:], rt_m[:], -1.0))
        t_exp = K.on("scalar",
                     lambda e, c=c, pv=pv: e.activation(
                         s_all[:, c * E:(c + 1) * E], pv, AF.Exp,
                         bias=rt_m[:, 0:1], accum_out=rt_sum[:, 0:1]),
                     waits=[K.tok("vector")])
        trk.set(("rps", slot), t_exp)
        K.on("vector", lambda e: e.reciprocal(rt_sum[:], rt_sum[:]),
             waits=[t_exp])
        K.on("vector",
             lambda e, c=c: e.tensor_scalar_mul(
                 s_all[:, c * E:(c + 1) * E], s_all[:, c * E:(c + 1) * E],
                 rt_sum[:, 0:1]))
    K.on("vector", lambda e: e.tensor_copy(rt_scr[:], s_all[:]))
    for rnd in range(3):
        K.on("vector",
             lambda e: e.reduce_max(rt_r3, rt3, axis=AX.X))
        K.on("vector",
             lambda e: e.tensor_tensor(
                 ge3, rt3, rt_r3.broadcast_to([128, TCH, E]), ALU.is_ge))
        K.on("vector",
             lambda e: e.scalar_tensor_tensor(
                 rt_scr[:], ge_bf[:], -1e30, rt_scr[:], ALU.mult, ALU.add))
    K.on("vector", lambda e: e.reduce_max(rt_r3, rt3, axis=AX.X))
    K.on("vector",
         lambda e: e.tensor_tensor(
             ge3, s3, rt_r3.broadcast_to([128, TCH, E]), ALU.is_ge))
    t_wrt = K.on("vector",
                 lambda e: e.tensor_tensor(wrt[:], s_all[:], ge_bf[:],
                                           ALU.mult))
    router_A = K.tok("scalar")
    return t_wrt, router_A


def ensure_shared_tiles(K: KB):
    K.sb("v65", [128, TCH * 16 * 65], BF16)
    K.sb("qT", [128, T], BF16)
    K.sb("kTt", [128, T], BF16)
    K.sb("Ea", [128, T], BF16)
    K.sb("Eb", [128, T], BF16)
    K.sb("r4", [128, 512], F32)
    K.sb("ones64", [1, 64], F32)
    K.sb("rrep", [128, 512], F32)
    K.sb("ctx_stg", [64, T], BF16)
    K.sb("ctx_rd0", [128, 1024], BF16)
    K.sb("ctx_rd1", [128, 1024], BF16)
    K.sb("wbig", [128, DCH * 1024], BF16)
    K.sb("wqp", [128, 1024], BF16)
    K.sb("wkp", [128, 1024], BF16)
    K.sb("xn0_scr", [128, D], F32)
    K.sb("y_sb", [128, D], F32)
    K.sb("muy", [128, 1], F32)
    K.sb("ssqy", [128, 1], F32)
    K.sb("rstdy", [128, 1], F32)
    K.sb("nmry", [128, 1], F32)
    K.sb("c_n16", [128, 1], F32)


def emit_l0_sparse_prep(K: KB, t_wrt):
    C = 512
    E = 32
    NR = E * C
    mask = K.tiles["ge_bf"]
    mask3 = mask[:].rearrange("p (c e) -> p c e", e=E)
    wrt = K.tiles["wrt"]
    wrt3 = wrt[:].rearrange("p (c e) -> p c e", e=E)
    psB = K.psum("psB", [128, 2048], F32)
    pTb16 = [psB[:, 0:512].bitcast(BF16), psB[:, 512:1024].bitcast(BF16)]
    pTf32 = [psB[:, 1024:1536], psB[:, 1536:2048]]
    eyeb = K.tiles["eye_bf16"]
    eye32 = K.view("wrt_wg", [128, 128], F32, 0)
    maskT = K.view("y_sb", [32, 2048], BF16, 0)
    pos_exp = K.view("v65", [32, 2048], F32, 0)
    pos_tok = K.view("xn0_scr", [128, 512], F32, 0)
    pos3 = pos_tok.rearrange("p (c e) -> p c e", e=E)
    cum_all = K.view("r4", [128, 512], F32, 0)
    cum3 = cum_all.rearrange("p (c e) -> p c e", e=E)
    sel = K.view("ctx_rd1", [128, 512], BF16, 0)
    sel3 = sel.rearrange("p (c e) -> p c e", e=E)
    tmul = K.view("sq_scr", [128, 512], F32, 0)
    tm3 = tmul.rearrange("p (c e) -> p c e", e=E)
    eio = K.sb("xn_chunk", [128, D], BF16)[:].bitcast(F32)[0:128, 0:32]
    spdst = K.view("xn_chunk", [128, 64], I32, 128)
    spdf = K.view("xn_chunk", [128, 64], F32, 384)
    spdf3 = spdf[:].rearrange("p (c k) -> p c k", k=4)
    spw = K.view("xn_chunk", [128, 64], F32, 640)
    spw3 = spw.rearrange("p (c k) -> p c k", k=4)
    spe = K.view("xn_chunk", [128, 64], F32, 896)
    spe3 = spe.rearrange("p (c k) -> p c k", k=4)
    ones1 = K.sb("ones1", [128, 1], F32)
    gx = K.dscratch("gx", [NR, 1040], BF16)
    xn_dram = K.dram["xn_scr"]
    xgw = K.view("sq_scr", [128, 1040], BF16, 0)
    xgw_w = K.view("sq_scr", [128, 8], F32, 2048)
    trk = Track()

    def mkbc(e):
        r = e.alloc_register("sp_bc")
        K.cache = getattr(K, "cache", {})
        K.cache["sp_bc"] = r
        return e.reg_mov(r, NR - 1)
    K.on("gpsimd", mkbc)
    t_xnst = K.tok("sync", dma="xnst")
    t_lnT = K.tok("tensor")
    t_eio = K.on("sync",
                 lambda e: e.dma_start(out=eio,
                                       in_=K.dram["c_iota512"][:, 0:32]),
                 waits=[t_wrt, t_xnst, t_lnT], dma="eio")
    t_e32 = K.on("sync",
                 lambda e: e.dma_start(out=eye32, in_=K.dram["c_eye32"]),
                 waits=[K.tok("tensor")], dma="e32")
    K.on("vector", lambda e: e.memset(ones1[:], 1.0),
         waits=[t_wrt, t_xnst, t_eio])

    ev_toks = []
    for c in range(TCH):
        pT = pTb16[c % 2]
        w = []
        if c < 2:
            w = [t_wrt]
        else:
            w = [ev_toks[c - 2]]
        t_tr = K.on("tensor",
                    lambda e, c=c, pT=pT: e.transpose(
                        pT[0:32, 0:128], mask[:, c * 32:(c + 1) * 32],
                        eyeb[:]),
                    waits=w)
        ev_toks.append(K.on("vector",
                            lambda e, c=c, pT=pT: e.tensor_copy(
                                maskT[:, c * 128:(c + 1) * 128],
                                pT[0:32, 0:128]),
                            waits=[t_tr]))
    t_pos = K.on("vector",
                 lambda e: e.tensor_tensor_scan(
                     pos_exp, ones1[0:32, 0:1].broadcast_to([32, 2048]),
                     maskT, 0.0, ALU.mult, ALU.add))
    ev2 = []
    for c in range(TCH):
        pT = pTf32[c % 2]
        w = [t_pos, t_e32] if c < 2 else [ev2[c - 2]]
        t_tr = K.on("tensor",
                    lambda e, c=c, pT=pT: e.transpose(
                        pT[0:128, 0:32],
                        pos_exp[:, c * 128:(c + 1) * 128],
                        eye32[0:32, 0:32]),
                    waits=w)
        ev2.append(K.on("vector",
                        lambda e, c=c, pT=pT: e.tensor_copy(
                            pos3[:, c, :], pT[0:128, 0:32]),
                        waits=[t_tr]))
    for c in range(TCH):
        K.on("vector",
             lambda e, c=c: e.tensor_tensor_scan(
                 cum3[:, c, :], ones1[:, 0:1].broadcast_to([128, 32]),
                 mask3[:, c, :], 0.0, ALU.mult, ALU.add),
             waits=[ev2[TCH - 1]] if c == 0 else [])
    for k in range(4):
        K.on("vector",
             lambda e, k=k: e.tensor_single_scalar(sel, cum_all,
                                                   float(k + 1),
                                                   ALU.is_equal))
        K.on("vector",
             lambda e: e.tensor_tensor(sel, sel, mask[:], ALU.mult))
        K.on("vector",
             lambda e: e.tensor_tensor(
                 tm3, sel3,
                 eio[:].unsqueeze(1).broadcast_to([128, TCH, 32]),
                 ALU.mult))
        K.on("vector",
             lambda e, k=k: e.reduce_sum(spe3[:, :, k:k + 1], tm3,
                                         axis=AX.X))
        K.on("vector",
             lambda e: e.tensor_tensor(tm3, sel3, pos3, ALU.mult))
        K.on("vector",
             lambda e, k=k: e.reduce_sum(spdf3[:, :, k:k + 1], tm3,
                                         axis=AX.X))
        K.on("vector",
             lambda e: e.tensor_tensor(tm3, sel3, wrt3, ALU.mult))
        K.on("vector",
             lambda e, k=k: e.reduce_sum(spw3[:, :, k:k + 1], tm3,
                                         axis=AX.X))
    K.on("vector",
         lambda e: e.scalar_tensor_tensor(spdf[:], spe[:], float(C),
                                          spdf[:], ALU.mult, ALU.add))
    K.on("vector",
         lambda e: e.tensor_scalar_add(spdf[:], spdf[:], -1.0))
    K.on("vector",
         lambda e: e.scalar_tensor_tensor(tmul[:, 64:128], spe[:],
                                          float(-C), spdf[:],
                                          ALU.mult, ALU.add))
    K.on("vector",
         lambda e: e.tensor_single_scalar(tmul[:, 0:64], tmul[:, 64:128],
                                          float(C - 1), ALU.is_gt))
    K.on("vector",
         lambda e: e.scalar_tensor_tensor(spdf[:], tmul[:, 0:64], 1e9,
                                          spdf[:], ALU.mult, ALU.add))
    t_dst = K.on("vector", lambda e: e.tensor_copy(spdst[:], spdf[:]))

    spdsti = spdst.rearrange("p (c k) -> p c k", k=4)
    idxp = [K.view("rt_r", [128, 8], I32, 0)[:, k:k + 1] for k in range(4)]
    t_sc_last = None
    for c in range(TCH):
        wx = sum((trk.wait(("xgw", k)) for k in range(4)), [])
        if c == 0:
            wx = wx + [t_dst]
        t_x = K.on("sync",
                   lambda e, c=c: e.dma_start(
                       out=xgw[:, 0:1024],
                       in_=xn_dram[c * 128:(c + 1) * 128, :]),
                   waits=wx, dma="xg")
        t_m1 = K.on("vector",
                    lambda e, c=c: e.tensor_copy(xgw_w[:, 0:4],
                                                 spw3[:, c, :]),
                    waits=[t_dst] + wx)
        t_m2 = K.on("vector",
                    lambda e, c=c: e.tensor_copy(xgw_w[:, 4:8],
                                                 spe3[:, c, :]))
        sc_toks = []
        for k in range(4):
            idxt = idxp[k]
            t_cp = K.on("vector",
                        lambda e, c=c, k=k, idxt=idxt: e.tensor_copy(
                            idxt, spdsti[:, c, k:k + 1]),
                        waits=trk.wait(("scidx", k)))
            sc_toks.append(K.on(
                "gpsimd",
                lambda e, idxt=idxt: e.indirect_dma_start(
                    out=gx,
                    out_offset=bass.IndirectOffsetOnAxis(ap=idxt, axis=0),
                    in_=xgw, in_offset=None,
                    bounds_check=K.cache["sp_bc"], oob_is_err=False),
                waits=([t_x, t_m2] if k == 0 else []) + [t_cp],
                dma=f"sc{k}"))
            trk.set(("scidx", k), sc_toks[-1])
            trk.set(("xgw", k), sc_toks[-1])
        t_sc_last = sc_toks
    done = [K.tok("gpsimd", dma=f"sc{k}") for k in range(4)]
    return done, t_dst


def emit_l0_sparse_experts(K: KB, moe: "MoeEmitter", sc_done,
                           pc_toks=None):
    C = 512
    E = 32
    NR = E * C
    psA = K.psum("psA", [128, 2048], F32)
    psB = K.psum("psB", [128, 2048], F32)
    pT16 = [psB[:, 1024:1536].bitcast(BF16), psB[:, 1536:2048].bitcast(BF16)]
    gx = K.dram["gx"]
    y_gx = K.dscratch("y_gx", [NR, 1024], BF16)
    gxin = [K.view("Ea", [128, 1024], BF16, 0),
            K.view("Ea", [128, 1024], BF16, 2048),
            K.view("Eb", [128, 1024], BF16, 0),
            K.view("Eb", [128, 1024], BF16, 2048)]
    gxmeta = [K.view("xn_chunk", [128, 16], BF16, 1216),
              K.view("xn_chunk", [128, 16], BF16, 1248)]
    gxwm = [K.view("xn_chunk", [128, 8], F32, 1216),
            K.view("xn_chunk", [128, 8], F32, 1248)]
    gxTs = [[K.view("v65", [128, 512], BF16, 24576 + kc * 1024)
             for kc in range(8)],
            [K.view("kTt", [128, 512], BF16, (kc % 4) * 1024)
             if kc < 4 else
             (K.view("rrep", [128, 512], BF16, (kc - 4) * 1024)
              if kc < 6 else
              K.view("r4", [128, 512], BF16, (kc - 6) * 1024))
             for kc in range(8)]]
    heT = [K.view("qT", [128, 512], BF16, j * 1024) for j in range(4)]
    tmp = K.view("sq_scr", [128, 2048], BF16, 0)
    ywst = [K.view("y_sb", [128, 1024], BF16, 0),
            K.view("y_sb", [128, 1024], BF16, 2048)]
    gsel = K.view("xn_chunk", [128, 8], F32, 1152)
    gwall = [K.view("xn_chunk", [128, 4], F32, 1184),
             K.view("xn_chunk", [128, 4], F32, 1200)]
    trk = Track()
    x_tok = K.tiles["x_tok"]
    spdsti = K.view("xn_chunk", [128, 64], I32, 128).rearrange(
        "p (c k) -> p c k", k=4)

    bV = K.tok("vector")
    bA = K.tok("scalar")

    def emit_gather_transpose(e_i):
        si = e_i % 2
        gset = gxTs[si]
        wgxT = trk.wait(("gxT", si)) + trk.wait(("gxTA", si)) + \
            trk.wait(("gxTV", si))
        wsel_t = None
        for rc in range(4):
            gpar = (e_i * 4 + rc) % 4
            mpar = rc % 2
            gi = gxin[gpar]
            t_g = K.on("sync",
                       lambda e, e_i=e_i, rc=rc, gi=gi: e.dma_start(
                           out=gi[:, 0:1024],
                           in_=gx[e_i * C + rc * 128:
                                  e_i * C + (rc + 1) * 128, 0:1024]),
                       waits=(sc_done if e_i == 0 and rc < 2 else []) +
                       trk.wait(("gxin", gpar)), dma=f"gt{gpar}")
            t_m = K.on("sync",
                       lambda e, e_i=e_i, rc=rc, mpar=mpar: e.dma_start(
                           out=gxmeta[mpar],
                           in_=gx[e_i * C + rc * 128:
                                  e_i * C + (rc + 1) * 128, 1024:1040]),
                       waits=(sc_done if e_i == 0 and rc < 2 else []) +
                       trk.wait(("gxm", mpar)), dma=f"gxm{mpar}")
            K.on("vector",
                 lambda e, e_i=e_i, mpar=mpar: e.tensor_single_scalar(
                     gsel[:, 0:4], gxwm[mpar][:, 4:8], float(e_i),
                     ALU.is_equal),
                 waits=[t_m] + ([bV] if e_i == 0 and rc == 0 else []))
            K.on("vector",
                 lambda e, mpar=mpar: e.tensor_tensor(
                     gsel[:, 4:8], gsel[:, 0:4], gxwm[mpar][:, 0:4],
                     ALU.mult))
            wsel_t = K.on("vector",
                          lambda e, e_i=e_i, rc=rc: e.reduce_sum(
                              gwall[e_i % 2][:].rearrange(
                                  "p (a b) -> p a b", a=4)[:, rc:rc + 1, :],
                              gsel[:, 4:8].rearrange("p (a e) -> p a e", a=1),
                              axis=AX.X))
            trk.set(("gxm", mpar), wsel_t)
            for kc in range(8):
                s4 = (rc * 8 + kc) % 2
                pT = pT16[s4]
                w = [t_g] + trk.wait(("ptr", s4))
                if rc == 0 and kc == 0:
                    w = w + wgxT + ([bA] if e_i == 0 else [])
                t_tr = K.on("tensor",
                            lambda e, kc=kc, pT=pT, gi=gi: e.transpose(
                                pT[0:128, 0:128],
                                gi[:, kc * 128:(kc + 1) * 128],
                                K.tiles["eye_bf16"][:]),
                            waits=w)
                eng = "scalar" if kc % 2 == 0 else "vector"
                if eng == "scalar":
                    t_ev = K.on(eng,
                                lambda e, rc=rc, kc=kc, pT=pT, gset=gset:
                                e.copy(
                                    gset[kc][:, rc * 128:(rc + 1) * 128],
                                    pT[0:128, 0:128]),
                                waits=[t_tr])
                else:
                    t_ev = K.on(eng,
                                lambda e, rc=rc, kc=kc, pT=pT, gset=gset:
                                e.tensor_copy(
                                    gset[kc][:, rc * 128:(rc + 1) * 128],
                                    pT[0:128, 0:128]),
                                waits=[t_tr])
                trk.set(("ptr", s4), t_ev)
            trk.set(("gxin", gpar), K.tok("tensor"))
        return si, [K.tok("scalar"), K.tok("vector")], wsel_t

    w1c, w3c, w2c = K.dram["w1c"], K.dram["w3c"], K.dram["w2c"]

    def issue_w13_cached(e_i):
        if e_i >= E:
            moe.issue_w13(e_i)
            return
        par = e_i % 2
        guard = ([moe.bT, moe.bV] if e_i < 2
                 else moe.trk.wait(("w13", par))) + [pc_toks[e_i]]
        a1 = w1c[e_i].rearrange("(kc p) o -> p kc o", p=128)
        a3 = w3c[e_i].rearrange("(kc p) o -> p kc o", p=128)
        t1 = K.on("sync", lambda e, s=a1, par=par: e.dma_start(
            out=moe.w1vs[par], in_=s), waits=guard, dma=f"c1{par}")
        t3 = K.on("sync", lambda e, s=a3, par=par: e.dma_start(
            out=moe.w3vs[par], in_=s), dma=f"c3{par}")
        moe.w13_toks[e_i] = (t1, t3)

    issue_w13_cached(0)
    pending = {0: emit_gather_transpose(0)}
    for e_i in range(E):
        par = e_i % 2
        w1v = moe.w1vs[par]
        w3v = moe.w3vs[par]
        if e_i + 1 < E + 2:
            issue_w13_cached(e_i + 1)
        t_w1, t_w3 = moe.w13_toks[e_i]
        if e_i < E:
            a2 = w2c[e_i].rearrange("(j p) o -> p j o", p=128)
            t_w2 = K.on("sync",
                        lambda e, s=a2: e.dma_start(out=moe.w2v, in_=s),
                        waits=moe.trk.wait("w2") + [pc_toks[e_i]] +
                        ([moe.bT] if e_i == 0 else []),
                        dma="w2c")
        else:
            _, _, a2 = moe.expert_weight_aps(e_i)
            t_w2 = K.on("gpsimd",
                        lambda e, s=a2: e.dma_start(out=moe.w2v, in_=s),
                        waits=moe.trk.wait("w2") +
                        ([moe.bT] if e_i == 0 else []),
                        dma="w2")
        si, gxT_done, t_wsel = pending.pop(e_i)
        gset = gxTs[si]
        gw1 = gwall[si]
        if e_i + 1 < E:
            pending[e_i + 1] = emit_gather_transpose(e_i + 1)

        for j in range(4):
            p1 = psA[:, (j % 2) * 512:(1 + j % 2) * 512]
            p2 = psA[:, (2 + j % 2) * 512:(3 + j % 2) * 512]

            def mh1(e, j=j, p1=p1, w1v=w1v, gset=gset):
                i = None
                for kc in range(DCH):
                    i = e.matmul(p1, w1v[:, kc, j * 128:(j + 1) * 128],
                                 gset[kc][:, 0:512],
                                 start=(kc == 0), stop=(kc == DCH - 1))
                return i
            w = list(gxT_done) if j == 0 else []
            w = w + ([t_w1] if j == 0 else []) + trk.wait(("m1", j % 2))
            t_m1 = K.on("tensor", mh1, waits=w)
            t_si = K.on("scalar",
                        lambda e, j=j, p1=p1: e.activation(
                            tmp[:, j * 512:(j + 1) * 512], p1, AF.Silu),
                        waits=[t_m1])
            trk.set(("m1", j % 2), t_si)

            def mh3(e, j=j, p2=p2, w3v=w3v, gset=gset):
                i = None
                for kc in range(DCH):
                    i = e.matmul(p2, w3v[:, kc, j * 128:(j + 1) * 128],
                                 gset[kc][:, 0:512],
                                 start=(kc == 0), stop=(kc == DCH - 1))
                return i
            w = ([t_w3] if j == 0 else []) + trk.wait(("m3", j % 2))
            t_m3 = K.on("tensor", mh3, waits=w)
            t_he = K.on("vector",
                        lambda e, j=j, p2=p2: e.tensor_tensor(
                            heT[j][:, 0:512], p2,
                            tmp[:, j * 512:(j + 1) * 512], ALU.mult),
                        waits=[t_m3, t_si])
            trk.set(("m3", j % 2), t_he)
        he_done = K.tok("vector")
        trk.set(("gxT", si), K.tok("tensor"))
        trk.set(("gxTA", si), K.tok("scalar"))
        trk.set(("gxTV", si), K.tok("vector"))

        for rc in range(4):
            ypar = rc % 2
            yst = ywst[ypar]
            evs = []
            for n in range(2):
                yp2 = (rc * 2 + n) % 2
                pye = psB[:, yp2 * 512:(yp2 + 1) * 512]

                def mye(e, rc=rc, n=n, pye=pye):
                    i = None
                    for j in range(4):
                        i = e.matmul(pye,
                                     heT[j][:, rc * 128:(rc + 1) * 128],
                                     moe.w2v[:, j, n * 512:(n + 1) * 512],
                                     start=(j == 0), stop=(j == 3))
                    return i
                w = [he_done, t_w2] if (rc == 0 and n == 0) else []
                w = list(w) + trk.wait(("yps", yp2))
                t_ye = K.on("tensor", mye, waits=w)
                t_ev = K.on("scalar",
                            lambda e, n=n, pye=pye, yst=yst, gw1=gw1, rc=rc:
                            e.activation(
                                yst[:, n * 512:(n + 1) * 512], pye,
                                AF.Identity, scale=gw1[:, rc:rc + 1]),
                            waits=[t_ye, t_wsel] +
                            (trk.wait(("yst", ypar)) if n == 0 else []))
                trk.set(("yps", yp2), t_ev)
                evs.append(t_ev)
            t_st = K.on("sync",
                        lambda e, e_i=e_i, rc=rc, yst=yst: e.dma_start(
                            out=y_gx[e_i * C + rc * 128:
                                     e_i * C + (rc + 1) * 128, :],
                            in_=yst[:, :]),
                        waits=evs, dma=f"ygx{ypar}")
            trk.set(("yst", ypar), t_st)
        moe.trk.set(("w13", par), K.tok("tensor"))
        moe.trk.set("w2", K.tok("tensor"))
        y_done = [K.tok("sync", dma="ygx0"), K.tok("sync", dma="ygx1")]

    gcmb = [K.view("Ea", [128, 1024], BF16, 0),
            K.view("Ea", [128, 1024], BF16, 2048),
            K.view("Eb", [128, 1024], BF16, 0),
            K.view("Eb", [128, 1024], BF16, 2048)]
    idxp = [K.view("rt_r", [128, 8], I32, 0)[:, 4 + k:5 + k]
            for k in range(4)]
    for c in range(TCH):
        for k in range(4):
            gpar = (c * 4 + k) % 4
            gb = gcmb[gpar]
            idxt = idxp[gpar]
            t_cp = K.on("vector",
                        lambda e, c=c, k=k, idxt=idxt: e.tensor_copy(
                            idxt, spdsti[:, c, k:k + 1]),
                        waits=trk.wait(("gcidx", gpar)))
            t_g = K.on("gpsimd",
                       lambda e, gb=gb, idxt=idxt: e.indirect_dma_start(
                           out=gb, out_offset=None,
                           in_=y_gx,
                           in_offset=bass.IndirectOffsetOnAxis(
                               ap=idxt, axis=0),
                           bounds_check=K.cache["sp_bc"], oob_is_err=False),
                       waits=(y_done if c == 0 and k < 2 else []) +
                       trk.wait(("gcmb", gpar)) + [t_cp], dma=f"gc{gpar}")
            trk.set(("gcidx", gpar), t_g)
            t_ad = K.on("vector",
                        lambda e, c=c, gb=gb: e.tensor_tensor(
                            x_tok[:, c * D:(c + 1) * D],
                            x_tok[:, c * D:(c + 1) * D], gb, ALU.add),
                        waits=[t_g])
            trk.set(("gcmb", gpar), t_ad)

    tV = K.tok("vector")
    tT = K.tok("tensor")
    for s in range(4):
        moe.trk.set(("mps", s), tV)
    for ypar in range(2):
        moe.trk.set(("yps", ypar), tV)
    for j in range(4):
        moe.trk.set(("he", j), tT)
    moe.emit_expert(E)
    moe.emit_expert(E + 1)
    return K.tok("vector")



class MoeEmitter:

    def __init__(self, K, n_routed, n_iter, expert_weight_aps, wrt3,
                 t_wrt, router_A, bT, ye_mode="wide", first=True):
        self.K = K
        self.ye_mode = ye_mode
        self.n_routed = n_routed
        self.n_iter = n_iter
        self.expert_weight_aps = expert_weight_aps
        self.wrt3 = wrt3
        self.t_wrt = t_wrt
        self.router_A = router_A
        self.bT = bT
        self.xnT = [K.tiles[f"xnT{k}"] for k in range(DCH)]
        self.x_tok = K.tiles["x_tok"]
        self.psA = K.psum("psA", [128, 2048], F32)
        self.psB = K.psum("psB", [128, 2048], F32)
        self.w1vs = [K.view("v65", [128, 8 * 512], BF16, 0)
                     .rearrange("p (kc o) -> p kc o", o=512),
                     K.view("wbig", [128, 8 * 512], BF16, 0)
                     .rearrange("p (kc o) -> p kc o", o=512)]
        self.w3vs = [K.view("v65", [128, 8 * 512], BF16, 8192)
                     .rearrange("p (kc o) -> p kc o", o=512),
                     K.view("wbig", [128, 8 * 512], BF16, 8192)
                     .rearrange("p (kc o) -> p kc o", o=512)]
        self.w2v = K.view("v65", [128, 4 * 1024], BF16, 16384).rearrange(
            "p (j o) -> p j o", o=1024)
        self.heT = [K.tiles[nm] for nm in ("qT", "kTt", "Ea", "Eb")]
        self.tmp = K.view("sq_scr", [128, 2048], BF16, 0)
        self.trk = Track()
        self.bV = K.tok("vector")
        self.w13_toks = {}
        if first:
            self.issue_w13(0)

    def issue_w13(self, e_i):
        K, trk = self.K, self.trk
        par = e_i % 2
        a1, a3, _ = self.expert_weight_aps(e_i)
        guard = [self.bT, self.bV] if e_i < 2 else trk.wait(("w13", par))
        t1 = K.on("gpsimd",
                  lambda e, s=a1, par=par: e.dma_start(out=self.w1vs[par],
                                                       in_=s),
                  waits=list(guard), dma=f"w1{par}")
        t3 = K.on("gpsimd",
                  lambda e, s=a3, par=par: e.dma_start(out=self.w3vs[par],
                                                       in_=s),
                  dma=f"w3{par}")
        self.w13_toks[e_i] = (t1, t3)

    def emit_expert(self, e_i):
        for _ in self.expert_parts(e_i):
            pass

    def expert_parts(self, e_i):
        K, trk = self.K, self.trk
        psA, psB = self.psA, self.psB
        xnT, heT, tmp, w2v = self.xnT, self.heT, self.tmp, self.w2v
        x_tok = self.x_tok
        par = e_i % 2
        w1v = self.w1vs[par]
        w3v = self.w3vs[par]
        if e_i + 1 < self.n_iter:
            self.issue_w13(e_i + 1)
        t_w1, t_w3 = self.w13_toks[e_i]
        _, _, a2 = self.expert_weight_aps(e_i)
        t_w2 = K.on("gpsimd",
                    lambda e, s=a2: e.dma_start(out=self.w2v, in_=s),
                    waits=trk.wait("w2") + ([self.bT] if e_i == 0 else []),
                    dma="w2")

        for j in range(4):
            for n in range(4):
                s1 = n % 2
                s2 = 2 + n % 2
                p1 = psA[:, s1 * 512:(s1 + 1) * 512]
                p2 = psA[:, s2 * 512:(s2 + 1) * 512]

                def mh1(e, j=j, n=n, p1=p1, w1v=w1v):
                    i = None
                    for kc in range(DCH):
                        i = e.matmul(p1, w1v[:, kc, j * 128:(j + 1) * 128],
                                     xnT[kc][:, n * 512:(n + 1) * 512],
                                     start=(kc == 0), stop=(kc == DCH - 1))
                    return i
                w = [t_w1] if (j == 0 and n == 0) else []
                if e_i == 0 and j == 0 and n == 0 and \
                        self.router_A is not None:
                    w = list(w) + [self.router_A]
                w = list(w) + trk.wait(("mps", s1))
                t_m1 = K.on("tensor", mh1, waits=w)
                t_si = K.on("scalar",
                            lambda e, n=n, p1=p1: e.activation(
                                tmp[:, n * 512:(n + 1) * 512], p1, AF.Silu),
                            waits=[t_m1])
                trk.set(("mps", s1), t_si)

                def mh3(e, j=j, n=n, p2=p2, w3v=w3v):
                    i = None
                    for kc in range(DCH):
                        i = e.matmul(p2, w3v[:, kc, j * 128:(j + 1) * 128],
                                     xnT[kc][:, n * 512:(n + 1) * 512],
                                     start=(kc == 0), stop=(kc == DCH - 1))
                    return i
                w = [t_w3] if (j == 0 and n == 0) else []
                w = list(w) + trk.wait(("mps", s2))
                t_m3 = K.on("tensor", mh3, waits=w)
                t_he = K.on("vector",
                            lambda e, j=j, n=n, p2=p2: e.tensor_tensor(
                                heT[j][:, n * 512:(n + 1) * 512], p2,
                                tmp[:, n * 512:(n + 1) * 512], ALU.mult),
                            waits=[t_m3, t_si] +
                            (trk.wait(("he", j)) if n == 0 else []))
                trk.set(("mps", s2), t_he)
            yield
        he_done = K.tok("vector")

        if self.ye_mode == "wide":
            for c in range(TCH):
                ypar = c % 2
                pye = psB[:, ypar * 1024:(ypar + 1) * 1024]

                def mye(e, c=c, pye=pye):
                    i = None
                    for n in range(2):
                        for j in range(4):
                            i = e.matmul(pye[:, n * 512:(n + 1) * 512],
                                         heT[j][:, c * 128:(c + 1) * 128],
                                         w2v[:, j, n * 512:(n + 1) * 512],
                                         start=(j == 0), stop=(j == 3))
                    return i
                w = [he_done, t_w2] if c == 0 else []
                w = list(w) + trk.wait(("yps", ypar))
                t_ye = K.on("tensor", mye, waits=w)
                if e_i < self.n_routed:
                    def acc(e, c=c, e_i=e_i, pye=pye):
                        return e.scalar_tensor_tensor(
                            x_tok[:, c * D:(c + 1) * D], pye,
                            self.wrt3[:, c, e_i:e_i + 1],
                            x_tok[:, c * D:(c + 1) * D], ALU.mult, ALU.add)
                else:
                    def acc(e, c=c, pye=pye):
                        return e.scalar_tensor_tensor(
                            x_tok[:, c * D:(c + 1) * D], pye, 1.0,
                            x_tok[:, c * D:(c + 1) * D], ALU.mult, ALU.add)
                wacc = [t_ye] + ([self.t_wrt]
                                 if (e_i == 0 and c == 0) else [])
                t_acc = K.on("vector", acc, waits=wacc)
                trk.set(("yps", ypar), t_acc)
                if c % 4 == 3:
                    yield
        else:
            k = 0
            for c in range(TCH):
                for n in range(2):
                    ypar = k % 3
                    k += 1
                    pye = psB[:, ypar * 512:(ypar + 1) * 512]

                    def mye(e, c=c, n=n, pye=pye):
                        i = None
                        for j in range(4):
                            i = e.matmul(pye,
                                         heT[j][:, c * 128:(c + 1) * 128],
                                         w2v[:, j, n * 512:(n + 1) * 512],
                                         start=(j == 0), stop=(j == 3))
                        return i
                    w = [he_done, t_w2] if (c == 0 and n == 0) else []
                    w = list(w) + trk.wait(("yps", ypar))
                    t_ye = K.on("tensor", mye, waits=w)
                    sl = slice(c * D + n * 512, c * D + (n + 1) * 512)
                    if e_i < self.n_routed:
                        def acc(e, sl=sl, c=c, e_i=e_i, pye=pye):
                            return e.scalar_tensor_tensor(
                                x_tok[:, sl], pye,
                                self.wrt3[:, c, e_i:e_i + 1],
                                x_tok[:, sl], ALU.mult, ALU.add)
                    else:
                        def acc(e, sl=sl, pye=pye):
                            return e.scalar_tensor_tensor(
                                x_tok[:, sl], pye, 1.0,
                                x_tok[:, sl], ALU.mult, ALU.add)
                    wacc = [t_ye] + ([self.t_wrt]
                                     if (e_i == 0 and c == 0 and n == 0)
                                     else [])
                    t_acc = K.on("vector", acc, waits=wacc)
                    trk.set(("yps", ypar), t_acc)
                if c % 4 == 3:
                    yield
        trk.set(("w13", par), K.tok("tensor"))
        trk.set("w2", K.tok("tensor"))
        for j in range(4):
            trk.set(("he", j), K.tok("tensor"))


def emit_moe_experts(K: KB, n_routed, n_iter, expert_weight_aps, wrt3,
                     t_wrt, router_A, bT):
    me = MoeEmitter(K, n_routed, n_iter, expert_weight_aps, wrt3,
                    t_wrt, router_A, bT)
    for e_i in range(n_iter):
        me.emit_expert(e_i)
    return K.tok("vector")



def emit_l0_moe(K: KB, xnT_ready):
    E = 32
    t_wrt, router_A = emit_l0_router(K, xnT_ready)
    wrt3 = K.tiles["wrt"][:].rearrange("p (c e) -> p c e", e=E)
    w1d, w3d, w2d = K.dram["w1e"], K.dram["w3e"], K.dram["w2e"]
    ws1, ws3, ws2 = K.dram["ws1"], K.dram["ws3"], K.dram["ws2"]

    def expert_weight_aps(e):
        if e < E:
            return (w1d[e].rearrange("(kc p) o -> p kc o", p=128),
                    w3d[e].rearrange("(kc p) o -> p kc o", p=128),
                    w2d[e].rearrange("(j p) o -> p j o", p=128))
        half = e - E
        sl = slice(half * 512, (half + 1) * 512)
        return (ws1[:, sl].rearrange("(kc p) o -> p kc o", p=128),
                ws3[:, sl].rearrange("(kc p) o -> p kc o", p=128),
                ws2[sl, :].rearrange("(j p) o -> p j o", p=128))

    bT = K.tok("tensor")
    return emit_moe_experts(K, E, E + 2, expert_weight_aps, wrt3,
                            t_wrt, router_A, bT)



def emit_l1_attn(K: KB, xnT_ready):
    xnT = [K.tiles[f"xnT{k}"] for k in range(DCH)]
    x_tok = K.tiles["x_tok"]
    psA = K.psum("psA", [128, 2048], F32)
    psB = K.psum("psB", [128, 2048], F32)

    H1 = [K.view("v65", [128, T], BF16, i * 4096) for i in range(DCH)]
    krT = K.view("qT", [128, 1024], F32, 0)
    vT = K.view("kTt", [128, 1024], F32, 0)
    sth = K.view("Ea", [128, 512], F32, 0)
    cth = K.view("Ea", [128, 512], F32, 2048)
    sfh = K.view("Eb", [128, 512], F32, 0)
    cfh = K.view("Eb", [128, 512], F32, 2048)
    s1 = K.view("xn_chunk", [128, 512], F32, 0)
    i32 = K.view("ctx_rd1", [128, 512], I32, 0)
    mem_re = K.view("sq_scr", [128, 512], F32, 0)
    mem_im = K.view("sq_scr", [128, 512], F32, 2048)
    acc = K.view("y_sb", [128, 512], F32, 0)
    ones512 = K.view("rrep", [128, 512], F32, 0)
    wqp = K.tiles["wqp"]
    wkp = K.tiles["wkp"]
    wbig = K.tiles["wbig"]
    wbigv = wbig[:].rearrange("p (kc o) -> p kc o", o=1024)
    iota512 = K.view("wrt", [128, 512], F32, 0)
    cry = K.sb("cry", [128, 8], F32)
    fcol = K.tiles["t_c_freqs_col"]
    g0c = K.tiles["t_c_g0_col"]
    g1c = K.tiles["t_c_g1_col"]
    rsq = K.tiles["t_c_rsq_rep"]
    f2pi = K.sb("f2pi_col", [128, DCH], F32)
    tcol = K.tiles["muy"]
    tcol2 = K.tiles["ssqy"]
    trk = Track()
    R10 = float(10.0 / TWO_PI)

    bTT = K.tok("tensor")
    t_ctxst = K.tok("sync", dma="ctxst")

    t_iota = K.on("sync",
                  lambda e: e.dma_start(out=iota512,
                                        in_=K.dram["c_iota512"]),
                  waits=[K.tok("vector")], dma="eye")
    t_bqk = K.tok("sync", dma="bqk")
    K.on("vector", lambda e: e.memset(ones512, 1.0),
         waits=[t_ctxst, t_iota])
    t_f2pi = K.on("vector",
                  lambda e: e.tensor_scalar_mul(f2pi[:], fcol[:],
                                                1.0 / TWO_PI),
                  waits=[t_bqk, t_iota])

    wo1 = K.dram["wo1"]
    t_wo = K.on("gpsimd",
                lambda e: e.dma_start(
                    out=wbigv, in_=wo1.rearrange("(kc p) o -> p kc o", p=128)),
                waits=[bTT], dma="wbig")

    wk1 = K.dram["wk1"]
    wv1 = K.dram["wv1"]
    first_h1 = True
    for c2 in range(DCH):
        t_lk = K.on("gpsimd",
                    lambda e, c2=c2: e.dma_start(
                        out=wqp[:].rearrange("p (kc o) -> p kc o", o=128),
                        in_=wk1[:, c2 * 128:(c2 + 1) * 128].rearrange(
                            "(kc q) o -> q kc o", q=128)),
                    waits=trk.wait("wqp"), dma="wk1")
        t_lv = K.on("gpsimd",
                    lambda e, c2=c2: e.dma_start(
                        out=wkp[:].rearrange("p (kc o) -> p kc o", o=128),
                        in_=wv1[:, c2 * 128:(c2 + 1) * 128].rearrange(
                            "(kc q) o -> q kc o", q=128)),
                    waits=trk.wait("wkp"), dma="wv1")
        t_cry = K.on("vector", lambda e: e.memset(cry[:, 0:6], 0.0))
        if c2 == 0:
            prev_cfh = [None]
            prev_pos = [None]
            prev_as = [None]

        for half in range(2):
            ev_k = ev_v = None
            for n2 in range(2):
                nt = half * 2 + n2
                pk = psA[:, n2 * 512:(n2 + 1) * 512]
                pv = psA[:, (2 + n2) * 512:(3 + n2) * 512]

                def mk(e, nt=nt, pk=pk):
                    i = None
                    wv3 = wqp[:].rearrange("p (kc o) -> p kc o", o=128)
                    for kc in range(DCH):
                        i = e.matmul(pk, wv3[:, kc, :],
                                     xnT[kc][:, nt * 512:(nt + 1) * 512],
                                     start=(kc == 0), stop=(kc == DCH - 1))
                    return i
                w = [t_lk] if (half == 0 and n2 == 0) else []
                if half == 0 and n2 == 0:
                    w = list(w) + [xnT_ready]
                w = list(w) + trk.wait(("kps", n2))
                t_mk = K.on("tensor", mk, waits=w)
                ev_k = K.on("scalar",
                            lambda e, n2=n2, pk=pk: e.activation(
                                krT[:, n2 * 512:(n2 + 1) * 512], pk,
                                AF.Identity),
                            waits=[t_mk] +
                            ([K.tok("vector")] if n2 == 0 else []))
                trk.set(("kps", n2), ev_k)

                def mv(e, nt=nt, pv=pv):
                    i = None
                    wv3 = wkp[:].rearrange("p (kc o) -> p kc o", o=128)
                    for kc in range(DCH):
                        i = e.matmul(pv, wv3[:, kc, :],
                                     xnT[kc][:, nt * 512:(nt + 1) * 512],
                                     start=(kc == 0), stop=(kc == DCH - 1))
                    return i
                w = ([t_lv] if (half == 0 and n2 == 0) else []) + \
                    trk.wait(("vps", n2))
                t_mv = K.on("tensor", mv, waits=w)
                ev_v = K.on("scalar",
                            lambda e, n2=n2, pv=pv: e.activation(
                                vT[:, n2 * 512:(n2 + 1) * 512], pv,
                                AF.Identity),
                            waits=[t_mv])
                trk.set(("vps", n2), ev_v)

            for tq in range(2):
                q = half * 2 + tq
                t0 = q * 512
                kq = krT[:, tq * 512:(tq + 1) * 512]
                vq = vT[:, tq * 512:(tq + 1) * 512]
                fc = f2pi[:, c2:c2 + 1]

                K.on("gpsimd",
                     lambda e, fc=fc, t0=t0: e.tensor_scalar_mul(
                         tcol[:], fc, float(t0)),
                     waits=([t_cry] if q == 0 else []) + [t_f2pi] +
                     ([prev_cfh[0]] if prev_cfh[0] is not None else []))
                K.on("gpsimd",
                     lambda e, fc=fc: e.tensor_scalar(
                         s1, iota512, fc, tcol[:, 0:1],
                         ALU.mult, ALU.add))
                K.on("gpsimd", lambda e: e.tensor_copy(i32, s1))
                K.on("gpsimd",
                     lambda e: e.tensor_tensor(s1, s1, i32, ALU.subtract))
                t_sth = K.on("scalar",
                             lambda e: e.activation(sth, s1, AF.Sin,
                                                    scale=TWO_PI),
                             waits=[K.tok("gpsimd")] +
                             ([prev_pos[0]] if prev_pos[0] else []))
                K.on("gpsimd",
                     lambda e: e.tensor_scalar_add(tcol2[:], tcol[:], 0.25),
                     waits=[t_sth])
                K.on("gpsimd",
                     lambda e, fc=fc: e.tensor_scalar(
                         s1, iota512, fc, tcol2[:, 0:1],
                         ALU.mult, ALU.add))
                K.on("gpsimd", lambda e: e.tensor_copy(i32, s1))
                K.on("gpsimd",
                     lambda e: e.tensor_tensor(s1, s1, i32, ALU.subtract))
                t_cth = K.on("scalar",
                             lambda e: e.activation(cth, s1, AF.Sin,
                                                    scale=TWO_PI),
                             waits=[K.tok("gpsimd")])

                K.on("vector",
                     lambda e, vq=vq: e.tensor_tensor(mem_re, vq, cth,
                                                      ALU.mult),
                     waits=[t_cth, ev_v])
                K.on("vector",
                     lambda e, vq=vq: e.tensor_tensor(mem_im, vq, sth,
                                                      ALU.mult))
                K.on("vector",
                     lambda e: e.tensor_tensor_scan(
                         mem_re, ones512, mem_re, cry[:, 0:1],
                         ALU.mult, ALU.add))
                K.on("vector",
                     lambda e: e.tensor_copy(cry[:, 0:1], mem_re[:, 511:512]))
                K.on("vector",
                     lambda e: e.tensor_tensor_scan(
                         mem_im, ones512, mem_im, cry[:, 1:2],
                         ALU.mult, ALU.add))
                K.on("vector",
                     lambda e: e.tensor_copy(cry[:, 1:2], mem_im[:, 511:512]))
                K.on("vector",
                     lambda e: e.tensor_tensor(mem_re, mem_re, cth, ALU.mult))
                K.on("vector",
                     lambda e: e.tensor_tensor(mem_im, mem_im, sth, ALU.mult))
                K.on("vector",
                     lambda e: e.tensor_tensor(mem_re, mem_re, mem_im,
                                               ALU.add))
                K.on("vector",
                     lambda e, c2=c2: e.tensor_scalar_mul(
                         acc, mem_re, g0c[:, c2:c2 + 1]))
                prev_pos[0] = K.tok("vector")

                K.on("gpsimd",
                     lambda e, kq=kq: e.tensor_scalar_mul(s1, kq, R10),
                     waits=[ev_k, t_cth])
                K.on("gpsimd", lambda e: e.tensor_copy(i32, s1))
                K.on("gpsimd",
                     lambda e: e.tensor_tensor(s1, s1, i32, ALU.subtract))
                t_sfh = K.on("scalar",
                             lambda e: e.activation(sfh, s1, AF.Sin,
                                                    scale=TWO_PI),
                             waits=[K.tok("gpsimd")] +
                             ([prev_as[0]] if prev_as[0] else []))
                K.on("gpsimd",
                     lambda e, kq=kq: e.tensor_scalar(
                         s1, kq, R10, 0.25, ALU.mult, ALU.add),
                     waits=[t_sfh])
                K.on("gpsimd", lambda e: e.tensor_copy(i32, s1))
                K.on("gpsimd",
                     lambda e: e.tensor_tensor(s1, s1, i32, ALU.subtract))
                t_cfh = K.on("scalar",
                             lambda e: e.activation(cfh, s1, AF.Sin,
                                                    scale=TWO_PI),
                             waits=[K.tok("gpsimd")])
                prev_cfh[0] = t_cfh

                K.on("vector",
                     lambda e, vq=vq: e.tensor_tensor(
                         mem_re[0:128, 1:512], vq[0:128, 1:512],
                         cfh[0:128, 0:511], ALU.mult),
                     waits=[t_cfh])
                if q == 0:
                    K.on("vector",
                         lambda e: e.memset(mem_re[0:128, 0:1], 0.0))
                else:
                    K.on("vector",
                         lambda e, vq=vq: e.tensor_tensor(
                             mem_re[0:128, 0:1], vq[0:128, 0:1],
                             cry[:, 4:5], ALU.mult))
                K.on("vector",
                     lambda e, vq=vq: e.tensor_tensor(
                         mem_im[0:128, 1:512], vq[0:128, 1:512],
                         sfh[0:128, 0:511], ALU.mult))
                if q == 0:
                    K.on("vector",
                         lambda e: e.memset(mem_im[0:128, 0:1], 0.0))
                else:
                    K.on("vector",
                         lambda e, vq=vq: e.tensor_tensor(
                             mem_im[0:128, 0:1], vq[0:128, 0:1],
                             cry[:, 5:6], ALU.mult))
                K.on("vector",
                     lambda e: e.tensor_copy(cry[:, 4:5], cfh[:, 511:512]))
                K.on("vector",
                     lambda e: e.tensor_copy(cry[:, 5:6], sfh[:, 511:512]))
                K.on("vector",
                     lambda e: e.tensor_tensor_scan(
                         mem_re, ones512, mem_re, cry[:, 2:3],
                         ALU.mult, ALU.add))
                K.on("vector",
                     lambda e: e.tensor_copy(cry[:, 2:3], mem_re[:, 511:512]))
                K.on("vector",
                     lambda e: e.tensor_tensor_scan(
                         mem_im, ones512, mem_im, cry[:, 3:4],
                         ALU.mult, ALU.add))
                K.on("vector",
                     lambda e: e.tensor_copy(cry[:, 3:4], mem_im[:, 511:512]))
                K.on("vector",
                     lambda e: e.tensor_tensor(mem_re, mem_re, cfh, ALU.mult))
                K.on("vector",
                     lambda e: e.tensor_tensor(mem_im, mem_im, sfh, ALU.mult))
                K.on("vector",
                     lambda e: e.tensor_tensor(mem_re, mem_re, mem_im,
                                               ALU.add))
                K.on("vector",
                     lambda e, c2=c2: e.scalar_tensor_tensor(
                         acc, mem_re, g1c[:, c2:c2 + 1], acc,
                         ALU.mult, ALU.add))
                prev_as[0] = K.tok("vector")
                wq_ = [bTT] if first_h1 else []
                first_h1 = False
                K.on("vector",
                     lambda e, c2=c2, t0=t0: e.tensor_tensor(
                         H1[c2][:, t0:t0 + 512], acc,
                         rsq[:, t0:t0 + 512], ALU.mult),
                     waits=wq_)
        trk.set("wqp", K.tok("vector"))
        trk.set("wkp", K.tok("vector"))
    h1_done = K.tok("vector")

    for c in range(TCH):
        par = c % 2
        pao = psB[:, par * 1024:(par + 1) * 1024]

        def mm(e, c=c, pao=pao):
            i = None
            for n in range(2):
                for kc in range(DCH):
                    i = e.matmul(pao[:, n * 512:(n + 1) * 512],
                                 H1[kc][:, c * 128:(c + 1) * 128],
                                 wbigv[:, kc, n * 512:(n + 1) * 512],
                                 start=(kc == 0), stop=(kc == DCH - 1))
            return i
        w = ([h1_done, t_wo] if c == 0 else []) + trk.wait(("wops", par))
        t_mm = K.on("tensor", mm, waits=w)
        t_acc = K.on("vector",
                     lambda e, c=c, pao=pao: e.scalar_tensor_tensor(
                         x_tok[:, c * D:(c + 1) * D], pao, 1.0,
                         x_tok[:, c * D:(c + 1) * D], ALU.mult, ALU.add),
                     waits=[t_mm])
        trk.set(("wops", par), t_acc)
    return K.tok("vector")


def emit_l1_router(K: KB, xnT_ready):
    E = 16
    G = 4
    xnT = [K.tiles[f"xnT{k}"] for k in range(DCH)]
    psA = K.psum("psA", [128, 2048], F32)
    s_sig = K.view("ctx_rd0", [128, 256], F32, 0)
    sc = K.view("ctx_rd0", [128, 256], F32, 1024)
    msc = K.view("r4", [128, 256], F32, 0)
    gsc = K.view("r4", [128, 256], F32, 1024)
    wrt = K.sb("wrt", [128, 512], F32)
    ge_bf = K.sb("ge_bf", [128, 512], BF16)
    rt_r = K.sb("rt_r", [128, TCH], F32)
    rt_r3 = rt_r[:].rearrange("p (c o) -> p c o", o=1)
    G1 = K.view("wbig", [128, 4096], BF16, 0)
    G2 = K.view("wbig", [128, 4096], BF16, 8192)
    rnk = K.view("ctx_rd1", [128, 256], F32, 0)
    gs64 = K.view("ctx_rd1", [128, 64], F32, 1024)
    gm64 = K.view("ctx_rd1", [128, 64], F32, 1280)
    tril = K.tiles["t_c_tril16"]
    ebr = K.tiles["t_c_ebias_rep"]
    trk = Track()
    bA = K.tok("scalar")
    bV = K.tok("vector")
    bT = K.tok("tensor")

    rwt = K.dram["rwt"]
    wg_sb = K.sb("wrt_wg", [128, DCH * 32], BF16)
    wgv = wg_sb[:].rearrange("p (kc o) -> p kc o", o=32)
    t_wg = K.on("gpsimd",
                lambda e: e.dma_start(
                    out=wgv[:, :, 0:16],
                    in_=rwt.rearrange("(kc p) o -> p kc o", p=128)),
                waits=[bT], dma="wg")
    for c in range(TCH):
        slot = c % 4
        pv = psA[:, slot * 512:slot * 512 + 16]

        def mm(e, c=c, pv=pv):
            i = None
            for kc in range(DCH):
                i = e.matmul(pv, xnT[kc][:, c * 128:(c + 1) * 128],
                             wgv[:, kc, 0:16],
                             start=(kc == 0), stop=(kc == DCH - 1))
            return i
        w = [t_wg, xnT_ready, bA, bV, bT] if c == 0 else []
        w = list(w) + trk.wait(("rps", slot))
        t_mm = K.on("tensor", mm, waits=w)
        t_sg = K.on("scalar",
                    lambda e, c=c, pv=pv: e.activation(
                        s_sig[:, c * E:(c + 1) * E], pv, AF.Sigmoid),
                    waits=[t_mm] + ([bV] if c == 0 else []))
        trk.set(("rps", slot), t_sg)
    t_sig = K.tok("scalar")
    t_sc = K.on("vector",
                lambda e: e.tensor_tensor(
                    sc.rearrange("p (c e) -> p c e", e=E),
                    s_sig.rearrange("p (c e) -> p c e", e=E),
                    ebr[:, 0:E].rearrange("p (o e) -> p o e", o=1)
                    .broadcast_to([128, TCH, E]), ALU.add),
                waits=[t_sig, bV, bT])
    sc4 = sc.rearrange("p (c g i) -> p (c g) i", g=G, i=4)
    gsc4 = gsc.rearrange("p (c g i) -> p (c g) i", g=G, i=4)
    gs3 = gs64.rearrange("p (x o) -> p x o", o=1)
    K.on("vector", lambda e: e.tensor_copy(gsc, sc))
    K.on("vector", lambda e: e.reduce_max(gs3, gsc4, axis=AX.X))
    K.on("vector",
         lambda e: e.tensor_tensor(
             ge_bf[:, 0:256].rearrange("p (x i) -> p x i", i=4), gsc4,
             gs3.broadcast_to([128, 64, 4]), ALU.is_ge))
    K.on("vector",
         lambda e: e.scalar_tensor_tensor(
             gsc, ge_bf[:, 0:256], -1e30, gsc, ALU.mult, ALU.add))
    K.on("vector", lambda e: e.reduce_max(
        gm64.rearrange("p (x o) -> p x o", o=1), gsc4, axis=AX.X))
    K.on("vector", lambda e: e.tensor_tensor(gs64, gs64, gm64, ALU.add))
    gsg = gs64.rearrange("p (c g) -> p c g", g=G)
    K.on("vector", lambda e: e.tensor_copy(gm64, gs64))
    gmg = gm64.rearrange("p (c g) -> p c g", g=G)
    K.on("vector", lambda e: e.reduce_max(rt_r3, gmg, axis=AX.X))
    K.on("vector",
         lambda e: e.tensor_tensor(
             ge_bf[:, 0:64].rearrange("p (c g) -> p c g", g=G), gmg,
             rt_r3.broadcast_to([128, TCH, G]), ALU.is_ge))
    K.on("vector",
         lambda e: e.scalar_tensor_tensor(
             gm64, ge_bf[:, 0:64], -1e30, gm64, ALU.mult, ALU.add))
    K.on("vector", lambda e: e.reduce_max(rt_r3, gmg, axis=AX.X))
    K.on("vector",
         lambda e: e.tensor_tensor(
             gm64.rearrange("p (c g) -> p c g", g=G), gsg,
             rt_r3.broadcast_to([128, TCH, G]), ALU.is_ge))
    K.on("vector",
         lambda e: e.tensor_tensor(
             msc.rearrange("p (c g i) -> p c g i", g=G, i=4),
             sc.rearrange("p (c g i) -> p c g i", g=G, i=4),
             gm64.rearrange("p (c g) -> p c g", g=G).unsqueeze(3)
             .broadcast_to([128, TCH, G, 4]),
             ALU.mult))
    mA = msc.rearrange("p (c e) -> p c e", e=E).unsqueeze(2).broadcast_to(
        [128, TCH, E, E])
    mB = msc.rearrange("p (c e) -> p c e", e=E).unsqueeze(3).broadcast_to(
        [128, TCH, E, E])
    g1v = G1.rearrange("p (c e f) -> p c e f", e=E, f=E)
    g2v = G2.rearrange("p (c e f) -> p c e f", e=E, f=E)
    K.on("vector", lambda e: e.tensor_tensor(g1v, mA, mB, ALU.is_gt))
    K.on("vector", lambda e: e.tensor_tensor(g2v, mA, mB, ALU.is_equal))
    K.on("vector",
         lambda e: e.tensor_tensor(
             g2v, g2v,
             tril[:, 0:256].rearrange("p (o e f) -> p o e f", e=E, f=E)
             .broadcast_to([128, TCH, E, E]), ALU.mult))
    K.on("vector", lambda e: e.tensor_tensor(g1v, g1v, g2v, ALU.add))
    K.on("vector",
         lambda e: e.reduce_sum(
             rnk.rearrange("p (c e) -> p c e", e=E).unsqueeze(3),
             g1v, axis=AX.X))
    K.on("vector",
         lambda e: e.tensor_single_scalar(rnk, rnk, 3.5, ALU.is_le))
    K.on("vector",
         lambda e: e.tensor_tensor(msc, s_sig, rnk, ALU.mult))
    K.on("vector",
         lambda e: e.reduce_sum(
             rt_r3, msc.rearrange("p (c e) -> p c e", e=E), axis=AX.X))
    K.on("vector",
         lambda e: e.tensor_scalar_add(rt_r[:], rt_r[:], 1e-20))
    K.on("vector", lambda e: e.reciprocal(rt_r[:], rt_r[:]))
    t_w = None
    for c in range(TCH):
        t_w = K.on("vector",
                   lambda e, c=c: e.tensor_scalar(
                       wrt[:, c * E:(c + 1) * E], msc[:, c * E:(c + 1) * E],
                       rt_r[:, c:c + 1], 2.5, ALU.mult, ALU.mult))
    router_A = K.tok("scalar")
    return t_w, router_A


def emit_l1_moe(K: KB, xnT_ready):
    E = 16
    t_wrt, router_A = emit_l1_router(K, xnT_ready)
    wrt3 = K.tiles["wrt"][:].rearrange("p (c e) -> p c e", e=E)
    gwe, uwe, dwe = K.dram["gwe"], K.dram["uwe"], K.dram["dwe"]
    sg, su, sd = K.dram["sg"], K.dram["su"], K.dram["sd"]

    def expert_weight_aps(e):
        if e < E:
            return (gwe[e].rearrange("(kc p) o -> p kc o", p=128),
                    uwe[e].rearrange("(kc p) o -> p kc o", p=128),
                    dwe[e].rearrange("(j p) o -> p j o", p=128))
        half = e - E
        sl = slice(half * 512, (half + 1) * 512)
        return (sg[:, sl].rearrange("(kc p) o -> p kc o", p=128),
                su[:, sl].rearrange("(kc p) o -> p kc o", p=128),
                sd[sl, :].rearrange("(j p) o -> p j o", p=128))

    bT = K.tok("tensor")
    return emit_moe_experts(K, E, E + 2, expert_weight_aps, wrt3,
                            t_wrt, router_A, bT)


class L1AttnEmitter:

    def __init__(self, K: KB, xnT_ready, t_router_V):
        self.K = K
        self.xnT = [K.tiles[f"xnT{k}"] for k in range(DCH)]
        self.x_tok = K.tiles["x_tok"]
        self.psB = K.psum("psB", [128, 2048], F32)
        self.pk = self.psB[:, 1536:2048]
        self.pv = self.psB[:, 1536:2048]
        self.krT = K.view("ctx_rd0", [128, 512], F32, 0)
        self.vT = K.view("r4", [128, 512], F32, 0)
        self.sth = K.view("xn0_scr", [128, 512], F32, 0)
        self.cth = K.view("xn0_scr", [128, 512], F32, 2048)
        self.sfh = K.view("y_sb", [128, 512], F32, 0)
        self.cfh = K.view("y_sb", [128, 512], F32, 2048)
        self.s1 = K.view("rrep", [128, 512], F32, 0)
        self.i32 = K.view("ctx_rd1", [128, 512], I32, 0)
        self.iota = K.view("xn_chunk", [128, 512], F32, 0)
        self.mem_re = K.view("v65", [128, 512], F32, 24576)
        self.mem_im = K.view("v65", [128, 512], F32, 26624)
        self.acc = K.view("v65", [128, 512], F32, 28672)
        self.h1stg = K.view("ge_bf", [128, 512], BF16, 0)
        self.wqp = K.tiles["wqp"]
        self.wkp = K.tiles["wkp"]
        self.cry = K.sb("cry", [128, 8], F32)
        self.ones1 = K.sb("ones1", [128, 1], F32)
        self.fcol = K.tiles["t_c_freqs_col"]
        self.g0c = K.tiles["t_c_g0_col"]
        self.g1c = K.tiles["t_c_g1_col"]
        self.rsq = K.tiles["t_c_rsq_rep"]
        self.f2pi = K.sb("f2pi_col", [128, DCH], F32)
        self.tcol = K.tiles["muy"]
        self.tcol2 = K.tiles["ssqy"]
        self.h1_scr = K.dscratch("h1_scr", [DCH, 128, T], BF16)
        self.trk = Track()
        self.prev = {}
        bV = K.tok("vector")
        bT = K.tok("tensor")
        t_iota = K.on("sync",
                      lambda e: e.dma_start(out=self.iota,
                                            in_=K.dram["c_iota512"]),
                      waits=[bV, bT], dma="eye")
        t_bqk = K.tok("sync", dma="bqk")
        K.on("vector", lambda e: e.memset(self.ones1[:], 1.0),
             waits=[t_router_V])
        self.t_f2pi = K.on("vector",
                           lambda e: e.tensor_scalar_mul(
                               self.f2pi[:], self.fcol[:], 1.0 / TWO_PI),
                           waits=[t_bqk, t_iota])
        self.t_router_V = t_router_V

    def emit_chunk_start(self, c2):
        K, trk = self.K, self.trk
        wk1 = K.dram["wk1"]
        wv1 = K.dram["wv1"]

        t_lk = K.on("gpsimd",
                    lambda e, c2=c2: e.dma_start(
                        out=self.wqp[:].rearrange("p (kc o) -> p kc o",
                                                  o=128),
                        in_=wk1[:, c2 * 128:(c2 + 1) * 128].rearrange(
                            "(kc q) o -> q kc o", q=128)),
                    waits=trk.wait("wqp"), dma="wk1")
        t_lv = K.on("gpsimd",
                    lambda e, c2=c2: e.dma_start(
                        out=self.wkp[:].rearrange("p (kc o) -> p kc o",
                                                  o=128),
                        in_=wv1[:, c2 * 128:(c2 + 1) * 128].rearrange(
                            "(kc q) o -> q kc o", q=128)),
                    waits=trk.wait("wkp"), dma="wv1")
        t_cry = K.on("vector", lambda e: e.memset(self.cry[:, 0:6], 0.0))
        self._chunk_state = (t_lk, t_lv, t_cry)

    def emit_tile(self, c2, q):
        K, trk = self.K, self.trk
        t_lk, t_lv, t_cry = self._chunk_state
        krT, vT = self.krT, self.vT
        sth, cth, sfh, cfh = self.sth, self.cth, self.sfh, self.cfh
        s1, i32, iota = self.s1, self.i32, self.iota
        mem_re, mem_im, acc = self.mem_re, self.mem_im, self.acc
        cry = self.cry
        R10 = float(10.0 / TWO_PI)
        xnT = self.xnT
        if True:
            t0 = q * 512
            nt = q

            def mk(e, nt=nt):
                i = None
                wv3 = self.wqp[:].rearrange("p (kc o) -> p kc o", o=128)
                for kc in range(DCH):
                    i = e.matmul(self.pk, wv3[:, kc, :],
                                 xnT[kc][:, nt * 512:(nt + 1) * 512],
                                 start=(kc == 0), stop=(kc == DCH - 1))
                return i
            w = [t_lk] + trk.wait("pk")
            t_mk = K.on("tensor", mk, waits=w)
            ev_k = K.on("scalar",
                        lambda e: e.activation(krT, self.pk, AF.Identity),
                        waits=[t_mk] + trk.wait("krT") +
                        ([self.t_router_V] if c2 == 0 and q == 0 else []))
            trk.set("pk", ev_k)

            def mv(e, nt=nt):
                i = None
                wv3 = self.wkp[:].rearrange("p (kc o) -> p kc o", o=128)
                for kc in range(DCH):
                    i = e.matmul(self.pv, wv3[:, kc, :],
                                 xnT[kc][:, nt * 512:(nt + 1) * 512],
                                 start=(kc == 0), stop=(kc == DCH - 1))
                return i
            w = [t_lv] + trk.wait("pk")
            t_mv = K.on("tensor", mv, waits=w)
            ev_v = K.on("scalar",
                        lambda e: e.activation(vT, self.pv, AF.Identity),
                        waits=[t_mv] + trk.wait("vT") +
                        ([self.t_router_V] if c2 == 0 and q == 0 else []))
            trk.set("pk", ev_v)

            fc = self.f2pi[:, c2:c2 + 1]
            tcol, tcol2 = self.tcol, self.tcol2
            K.on("vector",
                 lambda e, fc=fc, t0=t0: e.tensor_scalar_mul(
                     tcol[:], fc, float(t0)),
                 waits=([t_cry] if q == 0 else []) + [self.t_f2pi])
            t_ts = K.on("vector",
                        lambda e, fc=fc: e.tensor_scalar(
                            s1, iota, fc, tcol[:, 0:1], ALU.mult, ALU.add),
                        waits=self.prev.get("cfh_done", []))
            K.on("gpsimd", lambda e: e.tensor_copy(i32, s1), waits=[t_ts])
            K.on("gpsimd",
                 lambda e: e.tensor_tensor(s1, s1, i32, ALU.subtract))
            t_sth = K.on("scalar",
                         lambda e: e.activation(sth, s1, AF.Sin,
                                                scale=TWO_PI),
                         waits=[K.tok("gpsimd")] +
                         self.prev.get("pos_done", []))
            K.on("vector",
                 lambda e: e.tensor_scalar_add(tcol2[:], tcol[:], 0.25))
            t_ts = K.on("vector",
                        lambda e, fc=fc: e.tensor_scalar(
                            s1, iota, fc, tcol2[:, 0:1], ALU.mult, ALU.add),
                        waits=[t_sth])
            K.on("gpsimd", lambda e: e.tensor_copy(i32, s1), waits=[t_ts])
            K.on("gpsimd",
                 lambda e: e.tensor_tensor(s1, s1, i32, ALU.subtract))
            t_cth = K.on("scalar",
                         lambda e: e.activation(cth, s1, AF.Sin,
                                                scale=TWO_PI),
                         waits=[K.tok("gpsimd")] +
                         self.prev.get("pos_done", []))

            K.on("vector",
                 lambda e: e.tensor_tensor(mem_re, vT, cth, ALU.mult),
                 waits=[t_cth, ev_v])
            K.on("vector",
                 lambda e: e.tensor_tensor(mem_im, vT, sth, ALU.mult),
                 waits=[t_sth])
            K.on("vector",
                 lambda e: e.tensor_tensor_scan(
                     mem_re, self.ones1[:].broadcast_to([128, 512]), mem_re,
                     cry[:, 0:1], ALU.mult, ALU.add))
            K.on("vector",
                 lambda e: e.tensor_copy(cry[:, 0:1], mem_re[:, 511:512]))
            K.on("vector",
                 lambda e: e.tensor_tensor_scan(
                     mem_im, self.ones1[:].broadcast_to([128, 512]), mem_im,
                     cry[:, 1:2], ALU.mult, ALU.add))
            K.on("vector",
                 lambda e: e.tensor_copy(cry[:, 1:2], mem_im[:, 511:512]))
            K.on("vector",
                 lambda e: e.tensor_tensor(mem_re, mem_re, cth, ALU.mult))
            K.on("vector",
                 lambda e: e.tensor_tensor(mem_im, mem_im, sth, ALU.mult))
            K.on("vector",
                 lambda e: e.tensor_tensor(mem_re, mem_re, mem_im, ALU.add))
            t_pc = K.on("vector",
                        lambda e, c2=c2: e.tensor_scalar_mul(
                            acc, mem_re, self.g0c[:, c2:c2 + 1]))
            self.prev["pos_done"] = [t_pc]

            t_ts = K.on("vector",
                        lambda e: e.tensor_scalar_mul(s1, krT, R10),
                        waits=[ev_k, t_cth])
            K.on("gpsimd", lambda e: e.tensor_copy(i32, s1), waits=[t_ts])
            K.on("gpsimd",
                 lambda e: e.tensor_tensor(s1, s1, i32, ALU.subtract))
            t_sfh = K.on("scalar",
                         lambda e: e.activation(sfh, s1, AF.Sin,
                                                scale=TWO_PI),
                         waits=[K.tok("gpsimd")] +
                         self.prev.get("as_done", []))
            t_ts = K.on("vector",
                        lambda e: e.tensor_scalar(
                            s1, krT, R10, 0.25, ALU.mult, ALU.add),
                        waits=[t_sfh])
            K.on("gpsimd", lambda e: e.tensor_copy(i32, s1), waits=[t_ts])
            K.on("gpsimd",
                 lambda e: e.tensor_tensor(s1, s1, i32, ALU.subtract))
            t_cfh = K.on("scalar",
                         lambda e: e.activation(cfh, s1, AF.Sin,
                                                scale=TWO_PI),
                         waits=[K.tok("gpsimd")] +
                         self.prev.get("as_done", []))
            self.prev["cfh_done"] = [t_cfh]

            K.on("vector",
                 lambda e: e.tensor_tensor(
                     mem_re[0:128, 1:512], vT[0:128, 1:512],
                     cfh[0:128, 0:511], ALU.mult),
                 waits=[t_cfh])
            if q == 0:
                K.on("vector", lambda e: e.memset(mem_re[0:128, 0:1], 0.0))
            else:
                K.on("vector",
                     lambda e: e.tensor_tensor(
                         mem_re[0:128, 0:1], vT[0:128, 0:1],
                         cry[:, 4:5], ALU.mult))
            K.on("vector",
                 lambda e: e.tensor_tensor(
                     mem_im[0:128, 1:512], vT[0:128, 1:512],
                     sfh[0:128, 0:511], ALU.mult))
            if q == 0:
                K.on("vector", lambda e: e.memset(mem_im[0:128, 0:1], 0.0))
            else:
                K.on("vector",
                     lambda e: e.tensor_tensor(
                         mem_im[0:128, 0:1], vT[0:128, 0:1],
                         cry[:, 5:6], ALU.mult))
            t_lastv = K.on("vector",
                           lambda e: e.tensor_copy(cry[:, 4:5],
                                                   cfh[:, 511:512]))
            trk.set("vT", t_lastv)
            K.on("vector",
                 lambda e: e.tensor_copy(cry[:, 5:6], sfh[:, 511:512]))
            K.on("vector",
                 lambda e: e.tensor_tensor_scan(
                     mem_re, self.ones1[:].broadcast_to([128, 512]), mem_re,
                     cry[:, 2:3], ALU.mult, ALU.add))
            K.on("vector",
                 lambda e: e.tensor_copy(cry[:, 2:3], mem_re[:, 511:512]))
            K.on("vector",
                 lambda e: e.tensor_tensor_scan(
                     mem_im, self.ones1[:].broadcast_to([128, 512]), mem_im,
                     cry[:, 3:4], ALU.mult, ALU.add))
            K.on("vector",
                 lambda e: e.tensor_copy(cry[:, 3:4], mem_im[:, 511:512]))
            K.on("vector",
                 lambda e: e.tensor_tensor(mem_re, mem_re, cfh, ALU.mult))
            K.on("vector",
                 lambda e: e.tensor_tensor(mem_im, mem_im, sfh, ALU.mult))
            K.on("vector",
                 lambda e: e.tensor_tensor(mem_re, mem_re, mem_im, ALU.add))
            t_ac = K.on("vector",
                        lambda e, c2=c2: e.scalar_tensor_tensor(
                            acc, mem_re, self.g1c[:, c2:c2 + 1], acc,
                            ALU.mult, ALU.add))
            self.prev["as_done"] = [t_ac]
            trk.set("krT", t_ac)

            t_h1 = K.on("vector",
                        lambda e, t0=t0: e.tensor_tensor(
                            self.h1stg, acc, self.rsq[:, t0:t0 + 512],
                            ALU.mult),
                        waits=trk.wait("h1stg"))
            t_st = K.on("sync",
                        lambda e, c2=c2, t0=t0: e.dma_start(
                            out=self.h1_scr[c2][:, t0:t0 + 512],
                            in_=self.h1stg),
                        waits=[t_h1], dma="h1st")
            trk.set("h1stg", t_st)
        trk.set("wqp", K.tok("tensor"))
        trk.set("wkp", K.tok("tensor"))

    def emit_wo(self, moe_V_done):
        K, trk = self.K, self.trk
        psB = self.psB
        x_tok = self.x_tok
        bT2 = K.tok("tensor")
        wo1 = K.dram["wo1"]
        wbig = K.tiles["wbig"]
        wbigv = wbig[:].rearrange("p (kc o) -> p kc o", o=1024)
        t_wo1 = K.on("gpsimd",
                     lambda e: e.dma_start(
                         out=wbigv,
                         in_=wo1.rearrange("(kc p) o -> p kc o", p=128)),
                     waits=[bT2], dma="wbig")
        t_h1all = K.tok("sync", dma="h1st")
        bufs = [self.wqp, self.wkp]
        for c in range(TCH):
            par = c % 2
            buf = bufs[par]
            src = self.h1_scr.rearrange("kc p t -> p kc t")[
                :, :, c * 128:(c + 1) * 128]
            dst = buf[:].rearrange("p (kc t) -> p kc t", t=128)
            t_ld = K.on("sync",
                        lambda e, s=src, d=dst: e.dma_start(out=d, in_=s),
                        waits=[t_h1all, bT2] + trk.wait(("wobuf", par)),
                        dma=f"word{par}")
            pao = psB[:, par * 1024:(par + 1) * 1024]

            def mm(e, buf=buf, pao=pao):
                b3 = buf[:].rearrange("p (kc t) -> p kc t", t=128)
                i = None
                for n in range(2):
                    for kc in range(DCH):
                        i = e.matmul(pao[:, n * 512:(n + 1) * 512],
                                     b3[:, kc, :],
                                     wbigv[:, kc, n * 512:(n + 1) * 512],
                                     start=(kc == 0), stop=(kc == DCH - 1))
                return i
            w = [t_ld, t_wo1] + trk.wait(("wops", par))
            if c < 2:
                w += [moe_V_done] + trk.wait("pk") + trk.wait("pv")
            t_mm = K.on("tensor", mm, waits=w)
            trk.set(("wobuf", par), t_mm)
            t_acc = K.on("vector",
                         lambda e, c=c, pao=pao: e.scalar_tensor_tensor(
                             x_tok[:, c * D:(c + 1) * D], pao, 1.0,
                             x_tok[:, c * D:(c + 1) * D],
                             ALU.mult, ALU.add),
                         waits=[t_mm])
            trk.set(("wops", par), t_acc)


def emit_l1_fused(K: KB, xnT_ready):
    E = 16
    t_wrt, router_A = emit_l1_router(K, xnT_ready)
    t_router_V = K.tok("vector")
    wrt3 = K.tiles["wrt"][:].rearrange("p (c e) -> p c e", e=E)
    gwe, uwe, dwe = K.dram["gwe"], K.dram["uwe"], K.dram["dwe"]
    sg, su, sd = K.dram["sg"], K.dram["su"], K.dram["sd"]

    def expert_weight_aps(e):
        if e < E:
            return (gwe[e].rearrange("(kc p) o -> p kc o", p=128),
                    uwe[e].rearrange("(kc p) o -> p kc o", p=128),
                    dwe[e].rearrange("(j p) o -> p j o", p=128))
        half = e - E
        sl = slice(half * 512, (half + 1) * 512)
        return (sg[:, sl].rearrange("(kc p) o -> p kc o", p=128),
                su[:, sl].rearrange("(kc p) o -> p kc o", p=128),
                sd[sl, :].rearrange("(j p) o -> p j o", p=128))

    bT = K.tok("tensor")
    moe = MoeEmitter(K, E, E + 2, expert_weight_aps, wrt3,
                     t_wrt, router_A, bT, ye_mode="narrow")
    la = L1AttnEmitter(K, xnT_ready, t_router_V)

    def moe_parts():
        for e_i in range(E + 2):
            for _ in moe.expert_parts(e_i):
                yield

    gen = moe_parts()
    for c2 in range(DCH):
        la.emit_chunk_start(c2)
        for q in range(4):
            la.emit_tile(c2, q)
            for _ in range(4):
                next(gen, None)
    for _ in gen:
        pass
    la.emit_wo(K.tok("vector"))
    return K.tok("vector")



def emit_store_out(K: KB, waits):
    out = K.dout("out", [T, D], F32)
    x_tok = K.tiles["x_tok"]
    src = x_tok[:].rearrange("p (c d) -> p c d", d=D)
    dst = out.rearrange("(c p) d -> p c d", p=128)
    return K.on("sync", lambda e, s=src, d=dst: e.dma_start(out=d, in_=s),
                waits=waits, dma="out")


def emit_store_debug(K: KB, waits, pairs):
    toks = []
    for nm, ap in pairs:
        toks.append(K.on(
            "sync",
            lambda e, d=K.dram[nm], s=ap: e.dma_start(out=d, in_=s),
            waits=waits, dma="dbg"))
    return toks



def build_program():
    K = KB()
    K.din("x", [T, D], F32)
    K.din("c_eye", [128, 128], BF16)
    K.din("wq", [D, D], F32)
    K.din("wk", [D, D], F32)
    K.din("wv", [D, D], F32)
    K.din("wo", [D, D], F32)
    K.din("c_bq_col", [128, 8], F32)
    K.din("c_bk_col", [128, 8], F32)
    K.din("c_bv_rep", [128, D], F32)
    K.din("c_bo_rep", [128, D], F32)
    K.din("c_gamma_rep", [128, D], F32)
    K.din("c_beta_rep", [128, D], F32)

    for nm in ("c_bq_col", "c_bk_col"):
        tile = K.sb("t_" + nm, list(K.dram[nm].shape), F32)
        K.on("sync",
             lambda e, d=tile, s=K.dram[nm]: e.dma_start(out=d[:], in_=s),
             dma="bqk")
    t_gpc = None
    for nm in ("c_bv_rep", "c_bo_rep", "c_gamma_rep", "c_beta_rep"):
        tile = K.sb("t_" + nm, list(K.dram[nm].shape), BF16)
        t_gpc = K.on("gpsimd",
                     lambda e, d=tile, s=K.dram[nm]: e.dma_start(out=d[:],
                                                                 in_=s),
                     dma="rep")

    K.din("wg", [D, 32], F32)
    K.din("w1e", [32, D, 512], F32)
    K.din("w3e", [32, D, 512], F32)
    K.din("w2e", [32, 512, D], F32)
    K.din("ws1", [D, D], F32)
    K.din("ws3", [D, D], F32)
    K.din("ws2", [D, D], F32)

    K.din("wk1", [D, D], F32)
    K.din("wv1", [D, D], F32)
    K.din("wo1", [D, D], F32)
    K.din("rwt", [D, 16], F32)
    K.din("gwe", [16, D, 512], F32)
    K.din("uwe", [16, D, 512], F32)
    K.din("dwe", [16, 512, D], F32)
    K.din("sg", [D, D], F32)
    K.din("su", [D, D], F32)
    K.din("sd", [D, D], F32)
    K.din("c_freqs_col", [128, 8], F32)
    K.din("c_g0_col", [128, 8], F32)
    K.din("c_g1_col", [128, 8], F32)
    K.din("c_rsq_rep", [128, T], BF16)
    K.din("c_ebias_rep", [128, 16], F32)
    K.din("c_tril16", [128, 256], BF16)
    K.din("c_iota512", [128, 512], F32)
    for nm in ("c_freqs_col", "c_g0_col", "c_g1_col", "c_rsq_rep",
               "c_ebias_rep", "c_tril16"):
        dt_ = BF16 if nm in ("c_tril16", "c_rsq_rep") else F32
        tile = K.sb("t_" + nm, list(K.dram[nm].shape), dt_)
        K.on("sync",
             lambda e, d=tile, s=K.dram[nm]: e.dma_start(out=d[:], in_=s),
             dma="bqk")

    K.din("c_eye32", [128, 128], F32)
    K.dscratch("xn_scr", [T, D], BF16)

    t_x = emit_load_x(K)
    t_eye, t_ms = emit_consts(K)
    ensure_shared_tiles(K)
    done0 = emit_layernorm(K, "0", ready=t_x, t_eye=t_eye, store_xn=True)

    t_wrt, router_A = emit_l0_router(K, done0)
    sc_done, t_dst = emit_l0_sparse_prep(K, t_wrt)
    prep_V = K.tok("vector")

    K.dscratch("w1c", [32, D, 512], BF16)
    K.dscratch("w3c", [32, D, 512], BF16)
    K.dscratch("w2c", [32, 512, D], BF16)
    K.cache = getattr(K, "cache", {})
    K.cache["pc_toks"] = pc_toks = {}

    def precast_gen():
        for e in range(32):
            for nm_src, nm_dst in (("w1e", "w1c"), ("w3e", "w3c"),
                                   ("w2e", "w2c")):
                t = K.on("gpsimd",
                         lambda ee, s=K.dram[nm_src], d=K.dram[nm_dst],
                         e=e: ee.dma_start(out=d[e], in_=s[e]),
                         dma=f"pc{e // 8}")
            if e % 8 == 7:
                for e2 in range(e - 7, e + 1):
                    pc_toks[e2] = t
            yield True

    pcg = precast_gen()
    a0_done = emit_l0_attn(K, done0, t_gpc, sq_guard=sc_done,
                           psA_guard=[router_A],
                           psB_guard=[prep_V],
                           dma_filler=pcg)
    for _ in pcg:
        pass

    E0 = 32
    wrt3 = K.tiles["wrt"][:].rearrange("p (c e) -> p c e", e=E0)
    w1d, w3d, w2d = K.dram["w1e"], K.dram["w3e"], K.dram["w2e"]
    ws1, ws3, ws2 = K.dram["ws1"], K.dram["ws3"], K.dram["ws2"]

    def l0_aps(e):
        if e < E0:
            return (w1d[e].rearrange("(kc p) o -> p kc o", p=128),
                    w3d[e].rearrange("(kc p) o -> p kc o", p=128),
                    w2d[e].rearrange("(j p) o -> p j o", p=128))
        half = e - E0
        sl = slice(half * 512, (half + 1) * 512)
        return (ws1[:, sl].rearrange("(kc p) o -> p kc o", p=128),
                ws3[:, sl].rearrange("(kc p) o -> p kc o", p=128),
                ws2[sl, :].rearrange("(j p) o -> p j o", p=128))

    moe0 = MoeEmitter(K, E0, E0 + 2, l0_aps, wrt3, t_wrt, router_A,
                      K.tok("tensor"), ye_mode="wide", first=False)
    f0_done = emit_l0_sparse_experts(K, moe0, sc_done,
                                     K.cache.get("pc_toks"))
    done1 = emit_layernorm(K, "1", ready=f0_done, stat_tag="0")
    emit_l1_attn(K, done1)
    f1_done = emit_l1_moe(K, done1)

    out_toks = [emit_store_out(K, [f1_done])]
    K.on("sync", lambda e: None, waits=out_toks)
    return K.finish()


def host_inputs(inputs):
    f32 = np.float32
    com = {
        "c_eye": np.eye(128, dtype=ml_dtypes.bfloat16),
        "c_eye32": np.eye(128, dtype=np.float32),
        "wq": np.ascontiguousarray(inputs["L0_attn_Wq"].reshape(D, D), f32),
        "wk": np.ascontiguousarray(inputs["L0_attn_Wk"].reshape(D, D), f32),
        "wv": np.ascontiguousarray(inputs["L0_attn_Wv"].reshape(D, D), f32),
        "wo": np.ascontiguousarray(
            np.asarray(inputs["L0_attn_Wo"], f32).reshape(D, D)),
        "c_bq_col": np.ascontiguousarray(
            np.asarray(inputs["L0_attn_bq"], f32).reshape(-1).reshape(8, 128).T),
        "c_bk_col": np.ascontiguousarray(
            np.asarray(inputs["L0_attn_bk"], f32).reshape(-1).reshape(8, 128).T),
        "c_bv_rep": np.ascontiguousarray(np.tile(
            np.asarray(inputs["L0_attn_bv"], f32).reshape(1, -1), (128, 1))),
        "c_bo_rep": np.ascontiguousarray(np.tile(
            np.asarray(inputs["L0_attn_bo"], f32).reshape(1, -1), (128, 1))),
        "c_gamma_rep": np.ascontiguousarray(np.tile(
            np.asarray(inputs["L0_attn_gamma"], f32).reshape(1, -1),
            (128, 1))),
        "c_beta_rep": np.ascontiguousarray(np.tile(
            np.asarray(inputs["L0_attn_beta"], f32).reshape(1, -1),
            (128, 1))),
        "wg": np.ascontiguousarray(inputs["L0_ffn_Wg"], f32),
        "w1e": np.ascontiguousarray(inputs["L0_ffn_W1"], f32),
        "w3e": np.ascontiguousarray(inputs["L0_ffn_W3"], f32),
        "w2e": np.ascontiguousarray(inputs["L0_ffn_W2"], f32),
        "ws1": np.ascontiguousarray(inputs["L0_ffn_Ws1"], f32),
        "ws3": np.ascontiguousarray(inputs["L0_ffn_Ws3"], f32),
        "ws2": np.ascontiguousarray(inputs["L0_ffn_Ws2"], f32),
        "wk1": np.ascontiguousarray(inputs["L1_attn_Wk"], f32),
        "wv1": np.ascontiguousarray(inputs["L1_attn_Wv"], f32),
        "wo1": np.ascontiguousarray(inputs["L1_attn_Wo"], f32),
        "rwt": np.ascontiguousarray(
            np.asarray(inputs["L1_ffn_router_weight"], f32).T),
        "gwe": np.ascontiguousarray(
            np.asarray(inputs["L1_ffn_gate_w"], f32).transpose(0, 2, 1)),
        "uwe": np.ascontiguousarray(
            np.asarray(inputs["L1_ffn_up_w"], f32).transpose(0, 2, 1)),
        "dwe": np.ascontiguousarray(
            np.asarray(inputs["L1_ffn_down_w"], f32).transpose(0, 2, 1)),
        "sg": np.ascontiguousarray(
            np.asarray(inputs["L1_ffn_sh_gate_w"], f32).T),
        "su": np.ascontiguousarray(
            np.asarray(inputs["L1_ffn_sh_up_w"], f32).T),
        "sd": np.ascontiguousarray(
            np.asarray(inputs["L1_ffn_sh_down_w"], f32).T),
        "c_freqs_col": np.ascontiguousarray(
            np.asarray(inputs["L1_attn_freqs"], f32).reshape(-1)
            .reshape(8, 128).T),
        "c_g0_col": np.ascontiguousarray(
            np.repeat(np.asarray(inputs["L1_attn_gate"], f32)[:, 0], 64)
            .reshape(8, 128).T),
        "c_g1_col": np.ascontiguousarray(
            np.repeat(np.asarray(inputs["L1_attn_gate"], f32)[:, 1], 64)
            .reshape(8, 128).T),
        "c_rsq_rep": np.ascontiguousarray(np.tile(
            (1.0 / np.sqrt(np.arange(1, T + 1, dtype=f32))).reshape(1, -1),
            (128, 1)).astype(ml_dtypes.bfloat16)),
        "c_ebias_rep": np.ascontiguousarray(np.tile(
            np.asarray(inputs["L1_ffn_e_bias"], f32).reshape(1, -1),
            (128, 1))),
        "c_tril16": np.ascontiguousarray(np.tile(
            np.tril(np.ones((16, 16)), -1).reshape(1, -1),
            (128, 1)).astype(ml_dtypes.bfloat16)),
        "c_iota512": np.ascontiguousarray(np.tile(
            np.arange(512, dtype=f32).reshape(1, -1), (128, 1))),
    }
    return com


def kernel(**inputs):
    x = np.asarray(inputs["L0_attn_x"], dtype=np.float32)
    com = host_inputs(inputs)
    nc = build_program()
    in_maps = []
    for c in range(NCORES):
        m = dict(com)
        m["x"] = np.ascontiguousarray(x[c], np.float32)
        in_maps.append(m)
    res = run_bass_kernel_spmd(nc, in_maps, core_ids=list(range(NCORES)))
    out = np.zeros((B, T, D), dtype=np.float32)
    R = res.results
    for c in range(NCORES):
        out[c] = np.asarray(R[c]["out"])
    return out


# version 19 (final)
# speedup vs baseline: 1.2180x; 1.2180x over previous
import contextlib

import numpy as np
import ml_dtypes

import concourse.bass as bass
import concourse.mybir as mybir
from concourse.bass_utils import run_bass_kernel_spmd

F32 = mybir.dt.float32
BF16 = mybir.dt.bfloat16
I32 = mybir.dt.int32
AF = mybir.ActivationFunctionType
ALU = mybir.AluOpType
AX = mybir.AxisListType

B, T, D = 8, 2048, 1024
NCORES = 8
TCH = T // 128
DCH = D // 128
PI = float(np.pi)
TWO_PI = float(2 * np.pi)

ENGINES = ("tensor", "vector", "scalar", "gpsimd", "sync")
SERIAL = ("vector", "scalar", "gpsimd")

DEBUG_OUTS = True


class Sem:
    __slots__ = ("h", "n", "name")

    def __init__(self, h, name):
        self.h = h
        self.n = 0
        self.name = name


class KB:

    def __init__(self):
        self.nc = bass.Bass("TRN2", target_bir_lowering=False)
        self.ctx = contextlib.ExitStack()
        self.queues = {e: [] for e in ENGINES}
        self.tiles = {}
        self.sems = {}
        self.dram = {}

    def din(self, name, shape, dtype=F32):
        ap = self.nc.dram_tensor(name, list(shape), dtype,
                                 kind="ExternalInput").ap()
        self.dram[name] = ap
        return ap

    def dout(self, name, shape, dtype=F32):
        ap = self.nc.dram_tensor(name, list(shape), dtype,
                                 kind="ExternalOutput").ap()
        self.dram[name] = ap
        return ap

    def dscratch(self, name, shape, dtype=F32):
        ap = self.nc.dram_tensor(name, list(shape), dtype,
                                 kind="Internal").ap()
        self.dram[name] = ap
        return ap

    def sb(self, name, shape, dtype=F32):
        if name not in self.tiles:
            h = self.ctx.enter_context(
                self.nc.sbuf_tensor(name, list(shape), dtype))
            self.tiles[name] = h
        return self.tiles[name]

    def psum(self, name, shape, dtype=F32):
        if name not in self.tiles:
            h = self.ctx.enter_context(
                self.nc.psum_tensor(name, list(shape), dtype))
            self.tiles[name] = h
        return self.tiles[name]

    def sem(self, name):
        if name not in self.sems:
            h = self.ctx.enter_context(self.nc.semaphore(name))
            self.sems[name] = Sem(h, name)
        return self.sems[name]

    def on(self, eng, fn, waits=(), dma=False):
        wlist = []
        for w in waits:
            s, v = w
            if v > 0:
                wlist.append((s.h, v, s.name))
        if dma:
            s = self.sem(f"dma_{eng}_{dma}")
            k = 16
        else:
            s = self.sem("ser_" + eng)
            k = 1
            if eng in SERIAL and s.n > 0:
                wlist.append((s.h, s.n, s.name))
        s.n += k
        token = (s, s.n)

        def body(e, _w=tuple(wlist), _fn=fn, _sh=s.h, _k=k):
            for (h, v, _nm) in _w:
                e.wait_ge(h, v)
            inst = _fn(e)
            if inst is None:
                inst = e.nop()
            inst.then_inc(_sh, _k)

        self.queues[eng].append(body)
        return token

    def tok(self, eng, dma=None):
        s = self.sem(f"dma_{eng}_{dma}" if dma else "ser_" + eng)
        return (s, s.n)

    def view(self, src, shape, dtype, byte_off=0):
        h = self.tiles[src]
        ap = h[:].bitcast(dtype)
        esz = 2 if dtype == BF16 else 4
        off = byte_off // esz
        return ap[0:shape[0], off:off + shape[1]]

    def finish(self):
        block = self.ctx.enter_context(self.nc.Block())
        for eng in ENGINES:
            q = self.queues[eng]
            if not q:
                continue

            def run_all(e, _q=tuple(q)):
                for fn in _q:
                    fn(e)

            getattr(block, eng)(run_all)
        self.ctx.close()
        return self.nc


class Track:

    def __init__(self):
        self.d = {}

    def wait(self, key):
        return [self.d[key]] if key in self.d else []

    def set(self, key, token):
        self.d[key] = token



def emit_load_x(K: KB):
    x = K.dram["x"]
    x_tok = K.sb("x_tok", [128, TCH * D], F32)
    src = x.rearrange("(c p) d -> p c d", p=128)
    dst = x_tok[:].rearrange("p (c d) -> p c d", d=D)
    return K.on("sync", lambda e, s=src, d=dst: e.dma_start(out=d, in_=s),
                dma="x")


def emit_consts(K: KB):
    eye = K.sb("eye_bf16", [128, 128], BF16)
    t_eye = K.on("sync",
                 lambda e, d=eye: e.dma_start(out=d[:], in_=K.dram["c_eye"]),
                 dma="eye")
    K.sb("c_eps5", [128, 1], F32)
    K.sb("c_eps3", [128, 1], F32)
    K.on("vector", lambda e: e.memset(K.tiles["c_eps5"][:], 1e-5))
    t_ms = K.on("vector", lambda e: e.memset(K.tiles["c_eps3"][:], 1e-3))
    return t_eye, t_ms


def emit_layernorm(K: KB, tag, ready, t_eye=None, eps_name="c_eps5",
                   src="x_tok", stat_tag=None, store_xn=False):
    x_tok = K.tiles[src]
    stat_tag = tag if stat_tag is None else stat_tag
    mu = K.sb(f"mu{stat_tag}", [128, TCH], F32)
    nmr = K.sb(f"nmr{stat_tag}", [128, TCH], F32)
    rstd = K.sb(f"rstd{stat_tag}", [128, TCH], F32)
    ssq = K.sb(f"ssq{stat_tag}", [128, TCH], F32)
    sq_scr = K.sb("sq_scr", [128, D], F32)
    xn_chunk = K.sb("xn_chunk", [128, D], BF16)
    xnT = [K.sb(f"xnT{k}", [128, T], BF16) for k in range(DCH)]
    psA = K.psum("psA", [128, 2048], F32)
    pT2 = [psA[:, 0:512].bitcast(BF16), psA[:, 512:1024].bitcast(BF16)]
    eps = K.tiles[eps_name]
    eye = K.tiles["eye_bf16"]

    x3 = x_tok[:].rearrange("p (c d) -> p c d", d=D)
    K.on("vector",
         lambda e: e.reduce_sum(mu[:].rearrange("p (c o) -> p c o", o=1),
                                x3, axis=AX.X), waits=[ready])
    K.on("vector", lambda e: e.tensor_scalar_mul(mu[:], mu[:], 1.0 / D))
    for c in range(TCH):
        K.on("scalar",
             lambda e, c=c: e.activation(
                 sq_scr[:], x_tok[:, c * D:(c + 1) * D], AF.Square,
                 bias=mu[:, c:c + 1], scale=-1.0,
                 accum_out=ssq[:, c:c + 1]),
             waits=[K.tok("vector")] if c == 0 else [])
    K.on("scalar",
         lambda e: e.activation(rstd[:], ssq[:], AF.Ln,
                                bias=eps[:, 0:1], scale=1.0 / D))
    K.on("scalar",
         lambda e: e.activation(rstd[:], rstd[:], AF.Exp, scale=-0.5))
    K.on("vector",
         lambda e: e.tensor_tensor(nmr[:], mu[:], rstd[:], ALU.mult),
         waits=[K.tok("scalar")])
    t_stats = K.on("vector",
                   lambda e: e.tensor_scalar_mul(nmr[:], nmr[:], -1.0))

    tr_toks = {}
    ev_toks = {}
    norm_toks = {}
    st_toks = {}
    for c in range(TCH):
        pT = pT2[c % 2]
        w = [t_stats]
        if c >= 1:
            w.append(tr_toks[c - 1])
        if store_xn and c >= 1:
            w.append(st_toks[c - 1])
        norm_toks[c] = K.on(
            "scalar",
            lambda e, c=c: e.activation(
                xn_chunk[:], x_tok[:, c * D:(c + 1) * D], AF.Identity,
                bias=nmr[:, c:c + 1], scale=rstd[:, c:c + 1]),
            waits=w)
        if store_xn:
            st_toks[c] = K.on(
                "sync",
                lambda e, c=c: e.dma_start(
                    out=K.dram["xn_scr"][c * 128:(c + 1) * 128, :],
                    in_=xn_chunk[:]),
                waits=[norm_toks[c]], dma="xnst")
        for k in range(DCH):
            w = []
            if k == 0:
                w = [norm_toks[c]]
                if c == 0 and t_eye is not None:
                    w.append(t_eye)
                if c >= 2:
                    w.append(ev_toks[c - 2])
            tr_toks[c] = K.on(
                "tensor",
                lambda e, c=c, k=k, pT=pT: e.transpose(
                    pT[:, k * 128:(k + 1) * 128],
                    xn_chunk[:, k * 128:(k + 1) * 128], eye[:]),
                waits=w)
        for k in range(DCH):
            ev_toks[c] = K.on(
                "vector",
                lambda e, c=c, k=k, pT=pT: e.tensor_copy(
                    xnT[k][:, c * 128:(c + 1) * 128],
                    pT[:, k * 128:(k + 1) * 128]),
                waits=[tr_toks[c]] if k == 0 else [])
    return ev_toks[TCH - 1]


def emit_l0_attn(K: KB, xnT_ready, t_gpc, sq_guard=(), psA_guard=(),
                 psB_guard=()):
    H = 16
    xnT = [K.tiles[f"xnT{k}"] for k in range(DCH)]
    x_tok = K.tiles["x_tok"]
    psA = K.psum("psA", [128, 2048], F32)
    psB = K.psum("psB", [128, 2048], F32)

    v65 = K.sb("v65", [128, TCH * H * 65], BF16)
    v65v = v65[:].rearrange("p (c h o) -> p c h o", h=H, o=65)
    qT = K.sb("qT", [128, T], BF16)
    kTt = K.sb("kTt", [128, T], BF16)
    Ea = K.sb("Ea", [128, T], BF16)
    Eb = K.sb("Eb", [128, T], BF16)
    r4 = K.sb("r4", [128, 512], F32)
    ones64 = K.sb("ones64", [1, 64], F32)
    rrep = K.sb("rrep", [128, 512], F32)
    ctx_stg = K.sb("ctx_stg", [64, T], BF16)
    ctx_rd = [K.sb("ctx_rd0", [128, 1024], BF16),
              K.sb("ctx_rd1", [128, 1024], BF16)]
    wbig = K.sb("wbig", [128, DCH * 1024], BF16)
    wbigv = wbig[:].rearrange("p (kc o) -> p kc o", o=1024)
    wqp = K.sb("wqp", [128, 1024], BF16)
    wkp = K.sb("wkp", [128, 1024], BF16)
    xn0_scr = K.sb("xn0_scr", [128, D], F32)
    y_sb = K.sb("y_sb", [128, D], F32)
    yn_sb = y_sb
    a0_sb = xn0_scr
    sq_scr = K.sb("sq_scr", [128, D], F32)
    muy = K.sb("muy", [128, 1], F32)
    ssqy = K.sb("ssqy", [128, 1], F32)
    rstdy = K.sb("rstdy", [128, 1], F32)
    nmry = K.sb("nmry", [128, 1], F32)
    c_n16 = K.sb("c_n16", [128, 1], F32)
    ctx_scr = K.dscratch("ctx_scr", [H, 64, T], BF16)

    trk = Track()
    K.on("vector", lambda e: e.memset(ones64[:], 1.0))
    K.on("vector", lambda e: e.memset(c_n16[:], -16.0))
    t_ones = K.on("vector", lambda e: e.memset(v65v[:, :, :, 64:65], 1.0))

    wv = K.dram["wv"]
    t_wv = K.on("gpsimd",
                lambda e: e.dma_start(
                    out=wbigv, in_=wv.rearrange("(kc p) o -> p kc o", p=128)),
                dma="wbig")

    bv_rep = K.tiles["t_c_bv_rep"]
    bv3 = bv_rep[:].rearrange("p (h o) -> p h o", o=64)
    for c in range(TCH):
        for n in range(2):
            slot = (c * 2 + n) % 4
            pv = psA[:, slot * 512:(slot + 1) * 512]

            def mm(e, c=c, n=n, pv=pv):
                i = None
                for kc in range(DCH):
                    i = e.matmul(pv, xnT[kc][:, c * 128:(c + 1) * 128],
                                 wbigv[:, kc, n * 512:(n + 1) * 512],
                                 start=(kc == 0), stop=(kc == DCH - 1))
                return i
            w = list([t_wv, xnT_ready, t_ones] + list(psA_guard)) \
                if (c == 0 and n == 0) else []
            w = list(w) + trk.wait(("psA", slot))
            t_mm = K.on("tensor", mm, waits=w)

            def ev(e, c=c, n=n, pv=pv):
                return e.scalar_tensor_tensor(
                    v65v[:, c, n * 8:(n + 1) * 8, 0:64],
                    pv.rearrange("p (h o) -> p h o", o=64),
                    1.0, bv3[:, n * 8:(n + 1) * 8, :],
                    ALU.mult, ALU.add)
            wev = [t_mm]
            if c == 0 and n == 0:
                wev.append(t_gpc)
            t_ev = K.on("vector", ev, waits=wev)
            trk.set(("psA", slot), t_ev)
    v_prod = K.tok("tensor")

    wo = K.dram["wo"]
    t_wo = K.on("gpsimd",
                lambda e: e.dma_start(
                    out=wbigv, in_=wo.rearrange("(kc p) o -> p kc o", p=128)),
                waits=[v_prod], dma="wbig")

    bqc = K.tiles["t_c_bq_col"]
    bkc = K.tiles["t_c_bk_col"]

    for p in range(8):
        wq = K.dram["wq"]
        wk = K.dram["wk"]
        wqv = wq[:, p * 128:(p + 1) * 128].rearrange(
            "(kc q) o -> q kc o", q=128)
        wkv = wk[:, p * 128:(p + 1) * 128].rearrange(
            "(kc q) o -> q kc o", q=128)
        t_lq = K.on("gpsimd",
                    lambda e, s=wqv: e.dma_start(
                        out=wqp[:].rearrange("p (kc o) -> p kc o", o=128),
                        in_=s),
                    waits=trk.wait("wqp"), dma="wq")
        t_lk = K.on("gpsimd",
                    lambda e, s=wkv: e.dma_start(
                        out=wkp[:].rearrange("p (kc o) -> p kc o", o=128),
                        in_=s),
                    waits=trk.wait("wkp"), dma="wk")

        for (wtile, otile, bcol, t_l) in ((wqp, qT, bqc, t_lq),
                                          (wkp, kTt, bkc, t_lk)):
            for n in range(4):
                slot = n
                pv = psA[:, slot * 512:(slot + 1) * 512]

                def mm(e, wtile=wtile, n=n, pv=pv):
                    wv3 = wtile[:].rearrange("p (kc o) -> p kc o", o=128)
                    i = None
                    for kc in range(DCH):
                        i = e.matmul(pv, wv3[:, kc, :],
                                     xnT[kc][:, n * 512:(n + 1) * 512],
                                     start=(kc == 0), stop=(kc == DCH - 1))
                    return i
                w = [t_l] + trk.wait(("psA", slot)) + trk.wait("psA_free")
                t_mm = K.on("tensor", mm, waits=w)

                wev2 = [t_mm]
                if p == 0 and n == 0:
                    wev2.append(K.tok("sync", dma="bqk"))
                t_ev = K.on(
                    "scalar",
                    lambda e, otile=otile, bcol=bcol, n=n, pv=pv, p=p:
                    e.activation(otile[:, n * 512:(n + 1) * 512], pv,
                                 AF.Identity, bias=bcol[:, p:p + 1]),
                    waits=wev2)
                trk.set(("psA", slot), t_ev)
        trk.set("wqp", K.tok("scalar"))
        trk.set("wkp", K.tok("scalar"))
        qk_done = K.tok("scalar")

        for hh in range(2):
            h = 2 * p + hh
            sT = K.sem("ser_tensor")
            sA = K.sem("ser_scalar")
            t0 = sT.n
            a0 = sA.n
            for c in range(TCH):
                Ebuf = Ea if c % 2 == 0 else Eb
                for half in range(2):
                    def smm(e, c=c, hh=hh, half=half):
                        lhs = kTt[hh * 64:(hh + 1) * 64,
                                  c * 128:(c + 1) * 128]
                        i = None
                        for n in (2 * half, 2 * half + 1):
                            i = e.matmul(psA[:, n * 512:(n + 1) * 512], lhs,
                                         qT[hh * 64:(hh + 1) * 64,
                                            n * 512:(n + 1) * 512],
                                         start=True, stop=True)
                        return i
                    if c == 0:
                        w = ([qk_done] + trk.wait("psA_free")) \
                            if half == 0 else []
                    else:
                        w = [(sA, a0 + 2 * (c - 1) + half + 1)]
                    K.on("tensor", smm, waits=w)
                for half in range(2):
                    wex = [(sT, t0 + 4 * c + half + 1)]
                    K.on("scalar",
                         lambda e, half=half, Ebuf=Ebuf: e.activation(
                             Ebuf[:, half * 1024:(half + 1) * 1024],
                             psA[:, half * 1024:(half + 1) * 1024],
                             AF.Exp, bias=c_n16[:, 0:1], scale=0.125),
                         waits=wex)
                for half in range(2):
                    def cmm(e, c=c, h=h, half=half, Ebuf=Ebuf):
                        i = None
                        for n in (2 * half, 2 * half + 1):
                            i = e.matmul(psB[0:65, n * 512:(n + 1) * 512],
                                         v65v[:, c, h, :],
                                         Ebuf[:, n * 512:(n + 1) * 512],
                                         start=(c == 0), stop=(c == TCH - 1))
                        return i
                    w = [(sA, a0 + 2 * c + half + 1)]
                    if c == 0 and half == 0:
                        w += trk.wait("psB_free")
                        if p == 0 and hh == 0:
                            w += list(psB_guard)
                    K.on("tensor", cmm, waits=w)
            ctx_done = K.tok("tensor")

            bc_prev = None
            for n in range(4):
                w = [ctx_done] if n == 0 else [bc_prev]
                t_rc = K.on(
                    "vector",
                    lambda e, n=n: e.reciprocal(
                        r4[0:1, 0:512],
                        psB[64:65, n * 512:(n + 1) * 512]),
                    waits=w)
                w = [t_rc]
                if n == 0:
                    w.append((sA, a0 + 2 * TCH))
                bc_prev = K.on(
                    "tensor",
                    lambda e, n=n: e.matmul(
                        psA[0:64, n * 512:(n + 1) * 512], ones64[0:1, :],
                        r4[0:1, 0:512],
                        start=True, stop=True),
                    waits=w)
                t_cp = K.on(
                    "scalar",
                    lambda e, n=n: e.copy(rrep[0:64, 0:512],
                                          psA[0:64, n * 512:(n + 1) * 512]),
                    waits=[bc_prev] + trk.wait("rrep"))
                t_ml = K.on(
                    "vector",
                    lambda e, n=n: e.tensor_tensor(
                        ctx_stg[:, n * 512:(n + 1) * 512],
                        psB[0:64, n * 512:(n + 1) * 512],
                        rrep[0:64, 0:512], ALU.mult),
                    waits=[t_cp] + (trk.wait("ctx_stg") if n == 0 else []))
                trk.set("rrep", t_ml)
            trk.set("psB_free", K.tok("vector"))
            t_st = K.on("sync",
                        lambda e, h=h: e.dma_start(out=ctx_scr[h],
                                                   in_=ctx_stg[:]),
                        waits=[K.tok("vector")], dma="ctxst")
            trk.set("ctx_stg", t_st)
            trk.set("psA_free", K.tok("scalar"))

    g_rep = K.tiles["t_c_gamma_rep"]
    be_rep = K.tiles["t_c_beta_rep"]
    bo_rep = K.tiles["t_c_bo_rep"]
    rstd0 = K.tiles["rstd0"]
    nmr0 = K.tiles["nmr0"]
    c_eps3 = K.tiles["c_eps3"]
    for c in range(TCH):
        buf = ctx_rd[c % 2]
        src = ctx_scr.rearrange("(kc h2) dv t -> (h2 dv) kc t", h2=2)[
            :, :, c * 128:(c + 1) * 128]
        dst = buf[:].rearrange("p (kc t) -> p kc t", t=128)
        t_ld = K.on("sync",
                    lambda e, s=src, d=dst: e.dma_start(out=d, in_=s),
                    waits=[K.tok("sync", dma="ctxst")] +
                    trk.wait(("ctx_rd", c % 2)), dma=f"ctxrd{c % 2}")

        pao = psB[:, (c % 2) * 1024:((c % 2) + 1) * 1024]

        def mm(e, buf=buf, pao=pao):
            b3 = buf[:].rearrange("p (kc t) -> p kc t", t=128)
            i = None
            for n in range(2):
                for kc in range(DCH):
                    i = e.matmul(pao[:, n * 512:(n + 1) * 512], b3[:, kc, :],
                                 wbigv[:, kc, n * 512:(n + 1) * 512],
                                 start=(kc == 0), stop=(kc == DCH - 1))
            return i
        w = [t_ld, t_wo] + trk.wait(("psB_ao", c % 2))
        if c < 2:
            w += trk.wait("psB_free")
        t_mm = K.on("tensor", mm, waits=w)
        trk.set(("ctx_rd", c % 2), t_mm)

        t_xn0 = K.on("scalar",
                     lambda e, c=c: e.activation(
                         xn0_scr[:], x_tok[:, c * D:(c + 1) * D], AF.Identity,
                         bias=nmr0[:, c:c + 1], scale=rstd0[:, c:c + 1]),
                     waits=trk.wait("finchunk"))
        wy = [t_mm, t_xn0]
        if c == 0:
            wy.append(t_gpc)
        t_y = K.on("vector",
                   lambda e, pao=pao: e.scalar_tensor_tensor(
                       y_sb[:], pao, 1.0, xn0_scr[:], ALU.mult, ALU.add),
                   waits=wy)
        trk.set(("psB_ao", c % 2), t_y)
        K.on("vector",
             lambda e: e.tensor_tensor(y_sb[:], y_sb[:], bo_rep[:], ALU.add))
        K.on("vector",
             lambda e: e.reduce_sum(muy[:].rearrange("p (c o) -> p c o", o=1),
                                    y_sb[:].rearrange("p (c d) -> p c d", c=1),
                                    axis=AX.X))
        t_mu = K.on("vector",
                    lambda e: e.tensor_scalar_mul(muy[:], muy[:], 1.0 / D))
        wsq = [t_mu]
        if c == 0:
            wsq += list(sq_guard)
        K.on("scalar",
             lambda e: e.activation(sq_scr[:], y_sb[:], AF.Square,
                                    bias=muy[:, 0:1], scale=-1.0,
                                    accum_out=ssqy[:, 0:1]),
             waits=wsq)
        K.on("scalar",
             lambda e: e.activation(rstdy[:], ssqy[:], AF.Ln,
                                    bias=c_eps3[:, 0:1], scale=1.0 / D))
        t_rs = K.on("scalar",
                    lambda e: e.activation(rstdy[:], rstdy[:], AF.Exp,
                                           scale=-0.5))
        K.on("vector",
             lambda e: e.tensor_tensor(nmry[:], muy[:], rstdy[:], ALU.mult),
             waits=[t_rs])
        t_nm = K.on("vector",
                    lambda e: e.tensor_scalar_mul(nmry[:], nmry[:], -1.0))
        t_yn = K.on("scalar",
                    lambda e: e.activation(yn_sb[:], y_sb[:], AF.Identity,
                                           bias=nmry[:, 0:1],
                                           scale=rstdy[:, 0:1]),
                    waits=[t_nm])
        K.on("vector",
             lambda e: e.tensor_tensor(a0_sb[:], yn_sb[:], g_rep[:],
                                       ALU.mult),
             waits=[t_yn])
        K.on("vector",
             lambda e: e.tensor_tensor(a0_sb[:], a0_sb[:], be_rep[:],
                                       ALU.add))
        t_fin = K.on("vector",
                     lambda e, c=c: e.tensor_tensor(
                         x_tok[:, c * D:(c + 1) * D],
                         x_tok[:, c * D:(c + 1) * D],
                         a0_sb[:], ALU.add))
        trk.set("finchunk", t_fin)
    return K.tok("vector")


def emit_l0_router(K: KB, xnT_ready):
    E = 32
    xnT = [K.tiles[f"xnT{k}"] for k in range(DCH)]
    psA = K.psum("psA", [128, 2048], F32)
    s_all = K.view("ctx_rd0", [128, 512], F32, 0)
    s3 = s_all.rearrange("p (c e) -> p c e", e=E)
    rt_scr = K.view("r4", [128, 512], F32, 0)
    rt3 = rt_scr.rearrange("p (c e) -> p c e", e=E)
    wrt = K.sb("wrt", [128, 512], F32)
    ge_bf = K.sb("ge_bf", [128, 512], BF16)
    ge3 = ge_bf[:].rearrange("p (c e) -> p c e", e=E)
    rt_r = K.sb("rt_r", [128, TCH], F32)
    rt_r3 = rt_r[:].rearrange("p (c o) -> p c o", o=1)
    rt_m = K.tiles["muy"]
    rt_sum = K.tiles["ssqy"]
    trk = Track()
    bA = K.tok("scalar")
    bV = K.tok("vector")

    wg = K.dram["wg"]
    wg_sb = K.sb("wrt_wg", [128, DCH * 32], BF16)
    wgv = wg_sb[:].rearrange("p (kc o) -> p kc o", o=32)
    t_wg = K.on("gpsimd",
                lambda e: e.dma_start(
                    out=wgv, in_=wg.rearrange("(kc p) o -> p kc o", p=128)),
                dma="wg")
    for c in range(TCH):
        slot = c % 4
        pv = psA[:, slot * 512:slot * 512 + 32]

        def mm(e, c=c, pv=pv):
            i = None
            for kc in range(DCH):
                i = e.matmul(pv, xnT[kc][:, c * 128:(c + 1) * 128],
                             wgv[:, kc, :],
                             start=(kc == 0), stop=(kc == DCH - 1))
            return i
        w = [t_wg, xnT_ready, bA, bV] if c == 0 else []
        w = list(w) + trk.wait(("rps", slot))
        t_mm = K.on("tensor", mm, waits=w)

        K.on("vector",
             lambda e, pv=pv: e.reduce_max(
                 rt_m[:].rearrange("p (c o) -> p c o", o=1),
                 pv.rearrange("p (c e) -> p c e", c=1), axis=AX.X),
             waits=[t_mm])
        K.on("vector",
             lambda e: e.tensor_scalar_mul(rt_m[:], rt_m[:], -1.0))
        t_exp = K.on("scalar",
                     lambda e, c=c, pv=pv: e.activation(
                         s_all[:, c * E:(c + 1) * E], pv, AF.Exp,
                         bias=rt_m[:, 0:1], accum_out=rt_sum[:, 0:1]),
                     waits=[K.tok("vector")])
        trk.set(("rps", slot), t_exp)
        K.on("vector", lambda e: e.reciprocal(rt_sum[:], rt_sum[:]),
             waits=[t_exp])
        K.on("vector",
             lambda e, c=c: e.tensor_scalar_mul(
                 s_all[:, c * E:(c + 1) * E], s_all[:, c * E:(c + 1) * E],
                 rt_sum[:, 0:1]))
    K.on("vector", lambda e: e.tensor_copy(rt_scr[:], s_all[:]))
    for rnd in range(3):
        K.on("vector",
             lambda e: e.reduce_max(rt_r3, rt3, axis=AX.X))
        K.on("vector",
             lambda e: e.tensor_tensor(
                 ge3, rt3, rt_r3.broadcast_to([128, TCH, E]), ALU.is_ge))
        K.on("vector",
             lambda e: e.scalar_tensor_tensor(
                 rt_scr[:], ge_bf[:], -1e30, rt_scr[:], ALU.mult, ALU.add))
    K.on("vector", lambda e: e.reduce_max(rt_r3, rt3, axis=AX.X))
    K.on("vector",
         lambda e: e.tensor_tensor(
             ge3, s3, rt_r3.broadcast_to([128, TCH, E]), ALU.is_ge))
    t_wrt = K.on("vector",
                 lambda e: e.tensor_tensor(wrt[:], s_all[:], ge_bf[:],
                                           ALU.mult))
    router_A = K.tok("scalar")
    return t_wrt, router_A


def ensure_shared_tiles(K: KB):
    K.sb("v65", [128, TCH * 16 * 65], BF16)
    K.sb("qT", [128, T], BF16)
    K.sb("kTt", [128, T], BF16)
    K.sb("Ea", [128, T], BF16)
    K.sb("Eb", [128, T], BF16)
    K.sb("r4", [128, 512], F32)
    K.sb("ones64", [1, 64], F32)
    K.sb("rrep", [128, 512], F32)
    K.sb("ctx_stg", [64, T], BF16)
    K.sb("ctx_rd0", [128, 1024], BF16)
    K.sb("ctx_rd1", [128, 1024], BF16)
    K.sb("wbig", [128, DCH * 1024], BF16)
    K.sb("wqp", [128, 1024], BF16)
    K.sb("wkp", [128, 1024], BF16)
    K.sb("xn0_scr", [128, D], F32)
    K.sb("y_sb", [128, D], F32)
    K.sb("muy", [128, 1], F32)
    K.sb("ssqy", [128, 1], F32)
    K.sb("rstdy", [128, 1], F32)
    K.sb("nmry", [128, 1], F32)
    K.sb("c_n16", [128, 1], F32)


def emit_l0_sparse_prep(K: KB, t_wrt):
    C = 512
    E = 32
    NR = E * C
    mask = K.tiles["ge_bf"]
    mask3 = mask[:].rearrange("p (c e) -> p c e", e=E)
    wrt = K.tiles["wrt"]
    wrt3 = wrt[:].rearrange("p (c e) -> p c e", e=E)
    psB = K.psum("psB", [128, 2048], F32)
    pTb16 = [psB[:, 0:512].bitcast(BF16), psB[:, 512:1024].bitcast(BF16)]
    pTf32 = [psB[:, 1024:1536], psB[:, 1536:2048]]
    eyeb = K.tiles["eye_bf16"]
    eye32 = K.view("wrt_wg", [128, 128], F32, 0)
    maskT = K.view("y_sb", [32, 2048], BF16, 0)
    pos_exp = K.view("v65", [32, 2048], F32, 0)
    pos_tok = K.view("xn0_scr", [128, 512], F32, 0)
    pos3 = pos_tok.rearrange("p (c e) -> p c e", e=E)
    cum_all = K.view("r4", [128, 512], F32, 0)
    cum3 = cum_all.rearrange("p (c e) -> p c e", e=E)
    sel = K.view("ctx_rd1", [128, 512], BF16, 0)
    sel3 = sel.rearrange("p (c e) -> p c e", e=E)
    tmul = K.view("sq_scr", [128, 512], F32, 0)
    tm3 = tmul.rearrange("p (c e) -> p c e", e=E)
    eio = K.sb("xn_chunk", [128, D], BF16)[:].bitcast(F32)[0:128, 0:32]
    spdst = K.view("xn_chunk", [128, 64], I32, 128)
    spdf = K.view("xn_chunk", [128, 64], F32, 384)
    spdf3 = spdf[:].rearrange("p (c k) -> p c k", k=4)
    spw = K.view("xn_chunk", [128, 64], F32, 640)
    spw3 = spw.rearrange("p (c k) -> p c k", k=4)
    spe = K.view("xn_chunk", [128, 64], F32, 896)
    spe3 = spe.rearrange("p (c k) -> p c k", k=4)
    ones1 = K.sb("ones1", [128, 1], F32)
    gx = K.dscratch("gx", [NR, 1040], BF16)
    xn_dram = K.dram["xn_scr"]
    xgw = K.view("sq_scr", [128, 1040], BF16, 0)
    xgw_w = K.view("sq_scr", [128, 8], F32, 2048)
    trk = Track()

    def mkbc(e):
        r = e.alloc_register("sp_bc")
        K.cache = getattr(K, "cache", {})
        K.cache["sp_bc"] = r
        return e.reg_mov(r, NR - 1)
    K.on("gpsimd", mkbc)
    t_xnst = K.tok("sync", dma="xnst")
    t_lnT = K.tok("tensor")
    t_eio = K.on("sync",
                 lambda e: e.dma_start(out=eio,
                                       in_=K.dram["c_iota512"][:, 0:32]),
                 waits=[t_wrt, t_xnst, t_lnT], dma="eio")
    t_e32 = K.on("sync",
                 lambda e: e.dma_start(out=eye32, in_=K.dram["c_eye32"]),
                 waits=[K.tok("tensor")], dma="e32")
    K.on("vector", lambda e: e.memset(ones1[:], 1.0),
         waits=[t_wrt, t_xnst, t_eio])

    ev_toks = []
    for c in range(TCH):
        pT = pTb16[c % 2]
        w = []
        if c < 2:
            w = [t_wrt]
        else:
            w = [ev_toks[c - 2]]
        t_tr = K.on("tensor",
                    lambda e, c=c, pT=pT: e.transpose(
                        pT[0:32, 0:128], mask[:, c * 32:(c + 1) * 32],
                        eyeb[:]),
                    waits=w)
        ev_toks.append(K.on("vector",
                            lambda e, c=c, pT=pT: e.tensor_copy(
                                maskT[:, c * 128:(c + 1) * 128],
                                pT[0:32, 0:128]),
                            waits=[t_tr]))
    t_pos = K.on("vector",
                 lambda e: e.tensor_tensor_scan(
                     pos_exp, ones1[0:32, 0:1].broadcast_to([32, 2048]),
                     maskT, 0.0, ALU.mult, ALU.add))
    ev2 = []
    for c in range(TCH):
        pT = pTf32[c % 2]
        w = [t_pos, t_e32] if c < 2 else [ev2[c - 2]]
        t_tr = K.on("tensor",
                    lambda e, c=c, pT=pT: e.transpose(
                        pT[0:128, 0:32],
                        pos_exp[:, c * 128:(c + 1) * 128],
                        eye32[0:32, 0:32]),
                    waits=w)
        ev2.append(K.on("vector",
                        lambda e, c=c, pT=pT: e.tensor_copy(
                            pos3[:, c, :], pT[0:128, 0:32]),
                        waits=[t_tr]))
    for c in range(TCH):
        K.on("vector",
             lambda e, c=c: e.tensor_tensor_scan(
                 cum3[:, c, :], ones1[:, 0:1].broadcast_to([128, 32]),
                 mask3[:, c, :], 0.0, ALU.mult, ALU.add),
             waits=[ev2[TCH - 1]] if c == 0 else [])
    for k in range(4):
        K.on("vector",
             lambda e, k=k: e.tensor_single_scalar(sel, cum_all,
                                                   float(k + 1),
                                                   ALU.is_equal))
        K.on("vector",
             lambda e: e.tensor_tensor(sel, sel, mask[:], ALU.mult))
        K.on("vector",
             lambda e: e.tensor_tensor(
                 tm3, sel3,
                 eio[:].unsqueeze(1).broadcast_to([128, TCH, 32]),
                 ALU.mult))
        K.on("vector",
             lambda e, k=k: e.reduce_sum(spe3[:, :, k:k + 1], tm3,
                                         axis=AX.X))
        K.on("vector",
             lambda e: e.tensor_tensor(tm3, sel3, pos3, ALU.mult))
        K.on("vector",
             lambda e, k=k: e.reduce_sum(spdf3[:, :, k:k + 1], tm3,
                                         axis=AX.X))
        K.on("vector",
             lambda e: e.tensor_tensor(tm3, sel3, wrt3, ALU.mult))
        K.on("vector",
             lambda e, k=k: e.reduce_sum(spw3[:, :, k:k + 1], tm3,
                                         axis=AX.X))
    K.on("vector",
         lambda e: e.scalar_tensor_tensor(spdf[:], spe[:], float(C),
                                          spdf[:], ALU.mult, ALU.add))
    K.on("vector",
         lambda e: e.tensor_scalar_add(spdf[:], spdf[:], -1.0))
    K.on("vector",
         lambda e: e.scalar_tensor_tensor(tmul[:, 64:128], spe[:],
                                          float(-C), spdf[:],
                                          ALU.mult, ALU.add))
    K.on("vector",
         lambda e: e.tensor_single_scalar(tmul[:, 0:64], tmul[:, 64:128],
                                          float(C - 1), ALU.is_gt))
    K.on("vector",
         lambda e: e.scalar_tensor_tensor(spdf[:], tmul[:, 0:64], 1e9,
                                          spdf[:], ALU.mult, ALU.add))
    t_dst = K.on("vector", lambda e: e.tensor_copy(spdst[:], spdf[:]))

    spdsti = spdst.rearrange("p (c k) -> p c k", k=4)
    idxp = [K.view("rt_r", [128, 8], I32, 0)[:, k:k + 1] for k in range(4)]
    t_sc_last = None
    for c in range(TCH):
        wx = sum((trk.wait(("xgw", k)) for k in range(4)), [])
        if c == 0:
            wx = wx + [t_dst]
        t_x = K.on("sync",
                   lambda e, c=c: e.dma_start(
                       out=xgw[:, 0:1024],
                       in_=xn_dram[c * 128:(c + 1) * 128, :]),
                   waits=wx, dma="xg")
        t_m1 = K.on("vector",
                    lambda e, c=c: e.tensor_copy(xgw_w[:, 0:4],
                                                 spw3[:, c, :]),
                    waits=[t_dst] + wx)
        t_m2 = K.on("vector",
                    lambda e, c=c: e.tensor_copy(xgw_w[:, 4:8],
                                                 spe3[:, c, :]))
        sc_toks = []
        for k in range(4):
            idxt = idxp[k]
            t_cp = K.on("vector",
                        lambda e, c=c, k=k, idxt=idxt: e.tensor_copy(
                            idxt, spdsti[:, c, k:k + 1]),
                        waits=trk.wait(("scidx", k)))
            sc_toks.append(K.on(
                "gpsimd",
                lambda e, idxt=idxt: e.indirect_dma_start(
                    out=gx,
                    out_offset=bass.IndirectOffsetOnAxis(ap=idxt, axis=0),
                    in_=xgw, in_offset=None,
                    bounds_check=K.cache["sp_bc"], oob_is_err=False),
                waits=([t_x, t_m2] if k == 0 else []) + [t_cp],
                dma=f"sc{k}"))
            trk.set(("scidx", k), sc_toks[-1])
            trk.set(("xgw", k), sc_toks[-1])
        t_sc_last = sc_toks
    done = [K.tok("gpsimd", dma=f"sc{k}") for k in range(4)]
    return done, t_dst


def emit_l0_sparse_experts(K: KB, moe: "MoeEmitter", sc_done,
                           pc_toks=None):
    C = 512
    E = 32
    NR = E * C
    psA = K.psum("psA", [128, 2048], F32)
    psB = K.psum("psB", [128, 2048], F32)
    pT16 = [psB[:, 1024:1536].bitcast(BF16), psB[:, 1536:2048].bitcast(BF16)]
    gx = K.dram["gx"]
    y_gx = K.dscratch("y_gx", [NR, 1024], BF16)
    gxin = [K.view("Ea", [128, 1024], BF16, 0),
            K.view("Ea", [128, 1024], BF16, 2048),
            K.view("Eb", [128, 1024], BF16, 0),
            K.view("Eb", [128, 1024], BF16, 2048)]
    gxmeta = [K.view("xn_chunk", [128, 16], BF16, 1216),
              K.view("xn_chunk", [128, 16], BF16, 1248)]
    gxwm = [K.view("xn_chunk", [128, 8], F32, 1216),
            K.view("xn_chunk", [128, 8], F32, 1248)]
    gxTs = [[K.view("v65", [128, 512], BF16, 24576 + kc * 1024)
             for kc in range(8)],
            [K.view("kTt", [128, 512], BF16, (kc % 4) * 1024)
             if kc < 4 else
             (K.view("rrep", [128, 512], BF16, (kc - 4) * 1024)
              if kc < 6 else
              K.view("r4", [128, 512], BF16, (kc - 6) * 1024))
             for kc in range(8)]]
    heT = [K.view("qT", [128, 512], BF16, j * 1024) for j in range(4)]
    tmp = K.view("sq_scr", [128, 2048], BF16, 0)
    ywst = [K.view("y_sb", [128, 1024], BF16, 0),
            K.view("y_sb", [128, 1024], BF16, 2048)]
    gsel = K.view("xn_chunk", [128, 8], F32, 1152)
    gwall = [K.view("xn_chunk", [128, 4], F32, 1184),
             K.view("xn_chunk", [128, 4], F32, 1200)]
    trk = Track()
    x_tok = K.tiles["x_tok"]
    spdsti = K.view("xn_chunk", [128, 64], I32, 128).rearrange(
        "p (c k) -> p c k", k=4)

    bV = K.tok("vector")
    bA = K.tok("scalar")

    def emit_gather_transpose(e_i):
        si = e_i % 2
        gset = gxTs[si]
        wgxT = trk.wait(("gxT", si)) + trk.wait(("gxTA", si)) + \
            trk.wait(("gxTV", si))
        wsel_t = None
        for rc in range(4):
            gpar = (e_i * 4 + rc) % 4
            mpar = rc % 2
            gi = gxin[gpar]
            t_g = K.on("sync",
                       lambda e, e_i=e_i, rc=rc, gi=gi: e.dma_start(
                           out=gi[:, 0:1024],
                           in_=gx[e_i * C + rc * 128:
                                  e_i * C + (rc + 1) * 128, 0:1024]),
                       waits=(sc_done if e_i == 0 and rc < 2 else []) +
                       trk.wait(("gxin", gpar)), dma=f"gt{gpar}")
            t_m = K.on("sync",
                       lambda e, e_i=e_i, rc=rc, mpar=mpar: e.dma_start(
                           out=gxmeta[mpar],
                           in_=gx[e_i * C + rc * 128:
                                  e_i * C + (rc + 1) * 128, 1024:1040]),
                       waits=(sc_done if e_i == 0 and rc < 2 else []) +
                       trk.wait(("gxm", mpar)), dma=f"gxm{mpar}")
            K.on("vector",
                 lambda e, e_i=e_i, mpar=mpar: e.tensor_single_scalar(
                     gsel[:, 0:4], gxwm[mpar][:, 4:8], float(e_i),
                     ALU.is_equal),
                 waits=[t_m] + ([bV] if e_i == 0 and rc == 0 else []))
            K.on("vector",
                 lambda e, mpar=mpar: e.tensor_tensor(
                     gsel[:, 4:8], gsel[:, 0:4], gxwm[mpar][:, 0:4],
                     ALU.mult))
            wsel_t = K.on("vector",
                          lambda e, e_i=e_i, rc=rc: e.reduce_sum(
                              gwall[e_i % 2][:].rearrange(
                                  "p (a b) -> p a b", a=4)[:, rc:rc + 1, :],
                              gsel[:, 4:8].rearrange("p (a e) -> p a e", a=1),
                              axis=AX.X))
            trk.set(("gxm", mpar), wsel_t)
            for kc in range(8):
                s4 = (rc * 8 + kc) % 2
                pT = pT16[s4]
                w = [t_g] + trk.wait(("ptr", s4))
                if rc == 0 and kc == 0:
                    w = w + wgxT + ([bA] if e_i == 0 else [])
                t_tr = K.on("tensor",
                            lambda e, kc=kc, pT=pT, gi=gi: e.transpose(
                                pT[0:128, 0:128],
                                gi[:, kc * 128:(kc + 1) * 128],
                                K.tiles["eye_bf16"][:]),
                            waits=w)
                eng = "scalar" if kc % 2 == 0 else "vector"
                if eng == "scalar":
                    t_ev = K.on(eng,
                                lambda e, rc=rc, kc=kc, pT=pT, gset=gset:
                                e.copy(
                                    gset[kc][:, rc * 128:(rc + 1) * 128],
                                    pT[0:128, 0:128]),
                                waits=[t_tr])
                else:
                    t_ev = K.on(eng,
                                lambda e, rc=rc, kc=kc, pT=pT, gset=gset:
                                e.tensor_copy(
                                    gset[kc][:, rc * 128:(rc + 1) * 128],
                                    pT[0:128, 0:128]),
                                waits=[t_tr])
                trk.set(("ptr", s4), t_ev)
            trk.set(("gxin", gpar), K.tok("tensor"))
        return si, [K.tok("scalar"), K.tok("vector")], wsel_t

    moe.issue_w13(0)
    pending = {0: emit_gather_transpose(0)}
    for e_i in range(E):
        par = e_i % 2
        w1v = moe.w1vs[par]
        w3v = moe.w3vs[par]
        if e_i + 1 < E + 2:
            moe.issue_w13(e_i + 1)
        t_w1, t_w3 = moe.w13_toks[e_i]
        _, _, a2 = moe.expert_weight_aps(e_i)
        t_w2 = K.on("gpsimd",
                    lambda e, s=a2: e.dma_start(out=moe.w2v, in_=s),
                    waits=moe.trk.wait("w2") + ([moe.bT] if e_i == 0 else []),
                    dma="w2")
        si, gxT_done, t_wsel = pending.pop(e_i)
        gset = gxTs[si]
        gw1 = gwall[si]
        if e_i + 1 < E:
            pending[e_i + 1] = emit_gather_transpose(e_i + 1)

        for j in range(4):
            p1 = psA[:, (j % 2) * 512:(1 + j % 2) * 512]
            p2 = psA[:, (2 + j % 2) * 512:(3 + j % 2) * 512]

            def mh1(e, j=j, p1=p1, w1v=w1v, gset=gset):
                i = None
                for kc in range(DCH):
                    i = e.matmul(p1, w1v[:, kc, j * 128:(j + 1) * 128],
                                 gset[kc][:, 0:512],
                                 start=(kc == 0), stop=(kc == DCH - 1))
                return i
            w = list(gxT_done) if j == 0 else []
            w = w + ([t_w1] if j == 0 else []) + trk.wait(("m1", j % 2))
            t_m1 = K.on("tensor", mh1, waits=w)
            t_si = K.on("scalar",
                        lambda e, j=j, p1=p1: e.activation(
                            tmp[:, j * 512:(j + 1) * 512], p1, AF.Silu),
                        waits=[t_m1])
            trk.set(("m1", j % 2), t_si)

            def mh3(e, j=j, p2=p2, w3v=w3v, gset=gset):
                i = None
                for kc in range(DCH):
                    i = e.matmul(p2, w3v[:, kc, j * 128:(j + 1) * 128],
                                 gset[kc][:, 0:512],
                                 start=(kc == 0), stop=(kc == DCH - 1))
                return i
            w = ([t_w3] if j == 0 else []) + trk.wait(("m3", j % 2))
            t_m3 = K.on("tensor", mh3, waits=w)
            t_he = K.on("vector",
                        lambda e, j=j, p2=p2: e.tensor_tensor(
                            heT[j][:, 0:512], p2,
                            tmp[:, j * 512:(j + 1) * 512], ALU.mult),
                        waits=[t_m3, t_si])
            trk.set(("m3", j % 2), t_he)
        he_done = K.tok("vector")
        trk.set(("gxT", si), K.tok("tensor"))
        trk.set(("gxTA", si), K.tok("scalar"))
        trk.set(("gxTV", si), K.tok("vector"))

        for rc in range(4):
            ypar = rc % 2
            yst = ywst[ypar]
            evs = []
            for n in range(2):
                yp2 = (rc * 2 + n) % 2
                pye = psB[:, yp2 * 512:(yp2 + 1) * 512]

                def mye(e, rc=rc, n=n, pye=pye):
                    i = None
                    for j in range(4):
                        i = e.matmul(pye,
                                     heT[j][:, rc * 128:(rc + 1) * 128],
                                     moe.w2v[:, j, n * 512:(n + 1) * 512],
                                     start=(j == 0), stop=(j == 3))
                    return i
                w = [he_done, t_w2] if (rc == 0 and n == 0) else []
                w = list(w) + trk.wait(("yps", yp2))
                t_ye = K.on("tensor", mye, waits=w)
                t_ev = K.on("scalar",
                            lambda e, n=n, pye=pye, yst=yst, gw1=gw1, rc=rc:
                            e.activation(
                                yst[:, n * 512:(n + 1) * 512], pye,
                                AF.Identity, scale=gw1[:, rc:rc + 1]),
                            waits=[t_ye, t_wsel] +
                            (trk.wait(("yst", ypar)) if n == 0 else []))
                trk.set(("yps", yp2), t_ev)
                evs.append(t_ev)
            t_st = K.on("sync",
                        lambda e, e_i=e_i, rc=rc, yst=yst: e.dma_start(
                            out=y_gx[e_i * C + rc * 128:
                                     e_i * C + (rc + 1) * 128, :],
                            in_=yst[:, :]),
                        waits=evs, dma=f"ygx{ypar}")
            trk.set(("yst", ypar), t_st)
        moe.trk.set(("w13", par), K.tok("tensor"))
        moe.trk.set("w2", K.tok("tensor"))
        y_done = [K.tok("sync", dma="ygx0"), K.tok("sync", dma="ygx1")]

    gcmb = [K.view("Ea", [128, 1024], BF16, 0),
            K.view("Ea", [128, 1024], BF16, 2048),
            K.view("Eb", [128, 1024], BF16, 0),
            K.view("Eb", [128, 1024], BF16, 2048)]
    idxp = [K.view("rt_r", [128, 8], I32, 0)[:, 4 + k:5 + k]
            for k in range(4)]
    for c in range(TCH):
        for k in range(4):
            gpar = (c * 4 + k) % 4
            gb = gcmb[gpar]
            idxt = idxp[gpar]
            t_cp = K.on("vector",
                        lambda e, c=c, k=k, idxt=idxt: e.tensor_copy(
                            idxt, spdsti[:, c, k:k + 1]),
                        waits=trk.wait(("gcidx", gpar)))
            t_g = K.on("gpsimd",
                       lambda e, gb=gb, idxt=idxt: e.indirect_dma_start(
                           out=gb, out_offset=None,
                           in_=y_gx,
                           in_offset=bass.IndirectOffsetOnAxis(
                               ap=idxt, axis=0),
                           bounds_check=K.cache["sp_bc"], oob_is_err=False),
                       waits=(y_done if c == 0 and k < 2 else []) +
                       trk.wait(("gcmb", gpar)) + [t_cp], dma=f"gc{gpar}")
            trk.set(("gcidx", gpar), t_g)
            t_ad = K.on("vector",
                        lambda e, c=c, gb=gb: e.tensor_tensor(
                            x_tok[:, c * D:(c + 1) * D],
                            x_tok[:, c * D:(c + 1) * D], gb, ALU.add),
                        waits=[t_g])
            trk.set(("gcmb", gpar), t_ad)

    tV = K.tok("vector")
    tT = K.tok("tensor")
    for s in range(4):
        moe.trk.set(("mps", s), tV)
    for ypar in range(2):
        moe.trk.set(("yps", ypar), tV)
    for j in range(4):
        moe.trk.set(("he", j), tT)
    moe.emit_expert(E)
    moe.emit_expert(E + 1)
    return K.tok("vector")



class MoeEmitter:

    def __init__(self, K, n_routed, n_iter, expert_weight_aps, wrt3,
                 t_wrt, router_A, bT, ye_mode="wide", first=True):
        self.K = K
        self.ye_mode = ye_mode
        self.n_routed = n_routed
        self.n_iter = n_iter
        self.expert_weight_aps = expert_weight_aps
        self.wrt3 = wrt3
        self.t_wrt = t_wrt
        self.router_A = router_A
        self.bT = bT
        self.xnT = [K.tiles[f"xnT{k}"] for k in range(DCH)]
        self.x_tok = K.tiles["x_tok"]
        self.psA = K.psum("psA", [128, 2048], F32)
        self.psB = K.psum("psB", [128, 2048], F32)
        self.w1vs = [K.view("v65", [128, 8 * 512], BF16, 0)
                     .rearrange("p (kc o) -> p kc o", o=512),
                     K.view("wbig", [128, 8 * 512], BF16, 0)
                     .rearrange("p (kc o) -> p kc o", o=512)]
        self.w3vs = [K.view("v65", [128, 8 * 512], BF16, 8192)
                     .rearrange("p (kc o) -> p kc o", o=512),
                     K.view("wbig", [128, 8 * 512], BF16, 8192)
                     .rearrange("p (kc o) -> p kc o", o=512)]
        self.w2v = K.view("v65", [128, 4 * 1024], BF16, 16384).rearrange(
            "p (j o) -> p j o", o=1024)
        self.heT = [K.tiles[nm] for nm in ("qT", "kTt", "Ea", "Eb")]
        self.tmp = K.view("sq_scr", [128, 2048], BF16, 0)
        self.trk = Track()
        self.bV = K.tok("vector")
        self.w13_toks = {}
        if first:
            self.issue_w13(0)

    def issue_w13(self, e_i):
        K, trk = self.K, self.trk
        par = e_i % 2
        a1, a3, _ = self.expert_weight_aps(e_i)
        guard = [self.bT, self.bV] if e_i < 2 else trk.wait(("w13", par))
        t1 = K.on("gpsimd",
                  lambda e, s=a1, par=par: e.dma_start(out=self.w1vs[par],
                                                       in_=s),
                  waits=list(guard), dma=f"w1{par}")
        t3 = K.on("gpsimd",
                  lambda e, s=a3, par=par: e.dma_start(out=self.w3vs[par],
                                                       in_=s),
                  dma=f"w3{par}")
        self.w13_toks[e_i] = (t1, t3)

    def emit_expert(self, e_i):
        for _ in self.expert_parts(e_i):
            pass

    def expert_parts(self, e_i):
        K, trk = self.K, self.trk
        psA, psB = self.psA, self.psB
        xnT, heT, tmp, w2v = self.xnT, self.heT, self.tmp, self.w2v
        x_tok = self.x_tok
        par = e_i % 2
        w1v = self.w1vs[par]
        w3v = self.w3vs[par]
        if e_i + 1 < self.n_iter:
            self.issue_w13(e_i + 1)
        t_w1, t_w3 = self.w13_toks[e_i]
        _, _, a2 = self.expert_weight_aps(e_i)
        t_w2 = K.on("gpsimd",
                    lambda e, s=a2: e.dma_start(out=self.w2v, in_=s),
                    waits=trk.wait("w2") + ([self.bT] if e_i == 0 else []),
                    dma="w2")

        for j in range(4):
            for n in range(4):
                s1 = n % 2
                s2 = 2 + n % 2
                p1 = psA[:, s1 * 512:(s1 + 1) * 512]
                p2 = psA[:, s2 * 512:(s2 + 1) * 512]

                def mh1(e, j=j, n=n, p1=p1, w1v=w1v):
                    i = None
                    for kc in range(DCH):
                        i = e.matmul(p1, w1v[:, kc, j * 128:(j + 1) * 128],
                                     xnT[kc][:, n * 512:(n + 1) * 512],
                                     start=(kc == 0), stop=(kc == DCH - 1))
                    return i
                w = [t_w1] if (j == 0 and n == 0) else []
                if e_i == 0 and j == 0 and n == 0 and \
                        self.router_A is not None:
                    w = list(w) + [self.router_A]
                w = list(w) + trk.wait(("mps", s1))
                t_m1 = K.on("tensor", mh1, waits=w)
                t_si = K.on("scalar",
                            lambda e, n=n, p1=p1: e.activation(
                                tmp[:, n * 512:(n + 1) * 512], p1, AF.Silu),
                            waits=[t_m1])
                trk.set(("mps", s1), t_si)

                def mh3(e, j=j, n=n, p2=p2, w3v=w3v):
                    i = None
                    for kc in range(DCH):
                        i = e.matmul(p2, w3v[:, kc, j * 128:(j + 1) * 128],
                                     xnT[kc][:, n * 512:(n + 1) * 512],
                                     start=(kc == 0), stop=(kc == DCH - 1))
                    return i
                w = [t_w3] if (j == 0 and n == 0) else []
                w = list(w) + trk.wait(("mps", s2))
                t_m3 = K.on("tensor", mh3, waits=w)
                t_he = K.on("vector",
                            lambda e, j=j, n=n, p2=p2: e.tensor_tensor(
                                heT[j][:, n * 512:(n + 1) * 512], p2,
                                tmp[:, n * 512:(n + 1) * 512], ALU.mult),
                            waits=[t_m3, t_si] +
                            (trk.wait(("he", j)) if n == 0 else []))
                trk.set(("mps", s2), t_he)
            yield
        he_done = K.tok("vector")

        if self.ye_mode == "wide":
            for c in range(TCH):
                ypar = c % 2
                pye = psB[:, ypar * 1024:(ypar + 1) * 1024]

                def mye(e, c=c, pye=pye):
                    i = None
                    for n in range(2):
                        for j in range(4):
                            i = e.matmul(pye[:, n * 512:(n + 1) * 512],
                                         heT[j][:, c * 128:(c + 1) * 128],
                                         w2v[:, j, n * 512:(n + 1) * 512],
                                         start=(j == 0), stop=(j == 3))
                    return i
                w = [he_done, t_w2] if c == 0 else []
                w = list(w) + trk.wait(("yps", ypar))
                t_ye = K.on("tensor", mye, waits=w)
                if e_i < self.n_routed:
                    def acc(e, c=c, e_i=e_i, pye=pye):
                        return e.scalar_tensor_tensor(
                            x_tok[:, c * D:(c + 1) * D], pye,
                            self.wrt3[:, c, e_i:e_i + 1],
                            x_tok[:, c * D:(c + 1) * D], ALU.mult, ALU.add)
                else:
                    def acc(e, c=c, pye=pye):
                        return e.scalar_tensor_tensor(
                            x_tok[:, c * D:(c + 1) * D], pye, 1.0,
                            x_tok[:, c * D:(c + 1) * D], ALU.mult, ALU.add)
                wacc = [t_ye] + ([self.t_wrt]
                                 if (e_i == 0 and c == 0) else [])
                t_acc = K.on("vector", acc, waits=wacc)
                trk.set(("yps", ypar), t_acc)
                if c % 4 == 3:
                    yield
        else:
            k = 0
            for c in range(TCH):
                for n in range(2):
                    ypar = k % 3
                    k += 1
                    pye = psB[:, ypar * 512:(ypar + 1) * 512]

                    def mye(e, c=c, n=n, pye=pye):
                        i = None
                        for j in range(4):
                            i = e.matmul(pye,
                                         heT[j][:, c * 128:(c + 1) * 128],
                                         w2v[:, j, n * 512:(n + 1) * 512],
                                         start=(j == 0), stop=(j == 3))
                        return i
                    w = [he_done, t_w2] if (c == 0 and n == 0) else []
                    w = list(w) + trk.wait(("yps", ypar))
                    t_ye = K.on("tensor", mye, waits=w)
                    sl = slice(c * D + n * 512, c * D + (n + 1) * 512)
                    if e_i < self.n_routed:
                        def acc(e, sl=sl, c=c, e_i=e_i, pye=pye):
                            return e.scalar_tensor_tensor(
                                x_tok[:, sl], pye,
                                self.wrt3[:, c, e_i:e_i + 1],
                                x_tok[:, sl], ALU.mult, ALU.add)
                    else:
                        def acc(e, sl=sl, pye=pye):
                            return e.scalar_tensor_tensor(
                                x_tok[:, sl], pye, 1.0,
                                x_tok[:, sl], ALU.mult, ALU.add)
                    wacc = [t_ye] + ([self.t_wrt]
                                     if (e_i == 0 and c == 0 and n == 0)
                                     else [])
                    t_acc = K.on("vector", acc, waits=wacc)
                    trk.set(("yps", ypar), t_acc)
                if c % 4 == 3:
                    yield
        trk.set(("w13", par), K.tok("tensor"))
        trk.set("w2", K.tok("tensor"))
        for j in range(4):
            trk.set(("he", j), K.tok("tensor"))


def emit_moe_experts(K: KB, n_routed, n_iter, expert_weight_aps, wrt3,
                     t_wrt, router_A, bT):
    me = MoeEmitter(K, n_routed, n_iter, expert_weight_aps, wrt3,
                    t_wrt, router_A, bT)
    for e_i in range(n_iter):
        me.emit_expert(e_i)
    return K.tok("vector")



def emit_l0_moe(K: KB, xnT_ready):
    E = 32
    t_wrt, router_A = emit_l0_router(K, xnT_ready)
    wrt3 = K.tiles["wrt"][:].rearrange("p (c e) -> p c e", e=E)
    w1d, w3d, w2d = K.dram["w1e"], K.dram["w3e"], K.dram["w2e"]
    ws1, ws3, ws2 = K.dram["ws1"], K.dram["ws3"], K.dram["ws2"]

    def expert_weight_aps(e):
        if e < E:
            return (w1d[e].rearrange("(kc p) o -> p kc o", p=128),
                    w3d[e].rearrange("(kc p) o -> p kc o", p=128),
                    w2d[e].rearrange("(j p) o -> p j o", p=128))
        half = e - E
        sl = slice(half * 512, (half + 1) * 512)
        return (ws1[:, sl].rearrange("(kc p) o -> p kc o", p=128),
                ws3[:, sl].rearrange("(kc p) o -> p kc o", p=128),
                ws2[sl, :].rearrange("(j p) o -> p j o", p=128))

    bT = K.tok("tensor")
    return emit_moe_experts(K, E, E + 2, expert_weight_aps, wrt3,
                            t_wrt, router_A, bT)



def emit_l1_attn(K: KB, xnT_ready):
    xnT = [K.tiles[f"xnT{k}"] for k in range(DCH)]
    x_tok = K.tiles["x_tok"]
    psA = K.psum("psA", [128, 2048], F32)
    psB = K.psum("psB", [128, 2048], F32)

    H1 = [K.view("v65", [128, T], BF16, i * 4096) for i in range(DCH)]
    krT = K.view("qT", [128, 1024], F32, 0)
    vT = K.view("kTt", [128, 1024], F32, 0)
    sth = K.view("Ea", [128, 512], F32, 0)
    cth = K.view("Ea", [128, 512], F32, 2048)
    sfh = K.view("Eb", [128, 512], F32, 0)
    cfh = K.view("Eb", [128, 512], F32, 2048)
    s1 = K.view("xn_chunk", [128, 512], F32, 0)
    i32 = K.view("ctx_rd1", [128, 512], I32, 0)
    mem_re = K.view("sq_scr", [128, 512], F32, 0)
    mem_im = K.view("sq_scr", [128, 512], F32, 2048)
    acc = K.view("y_sb", [128, 512], F32, 0)
    ones512 = K.view("rrep", [128, 512], F32, 0)
    wqp = K.tiles["wqp"]
    wkp = K.tiles["wkp"]
    wbig = K.tiles["wbig"]
    wbigv = wbig[:].rearrange("p (kc o) -> p kc o", o=1024)
    iota512 = K.view("wrt", [128, 512], F32, 0)
    cry = K.sb("cry", [128, 8], F32)
    fcol = K.tiles["t_c_freqs_col"]
    g0c = K.tiles["t_c_g0_col"]
    g1c = K.tiles["t_c_g1_col"]
    rsq = K.tiles["t_c_rsq_rep"]
    f2pi = K.sb("f2pi_col", [128, DCH], F32)
    tcol = K.tiles["muy"]
    tcol2 = K.tiles["ssqy"]
    trk = Track()
    R10 = float(10.0 / TWO_PI)

    bTT = K.tok("tensor")
    t_ctxst = K.tok("sync", dma="ctxst")

    t_iota = K.on("sync",
                  lambda e: e.dma_start(out=iota512,
                                        in_=K.dram["c_iota512"]),
                  waits=[K.tok("vector")], dma="eye")
    t_bqk = K.tok("sync", dma="bqk")
    K.on("vector", lambda e: e.memset(ones512, 1.0),
         waits=[t_ctxst, t_iota])
    t_f2pi = K.on("vector",
                  lambda e: e.tensor_scalar_mul(f2pi[:], fcol[:],
                                                1.0 / TWO_PI),
                  waits=[t_bqk, t_iota])

    wo1 = K.dram["wo1"]
    t_wo = K.on("gpsimd",
                lambda e: e.dma_start(
                    out=wbigv, in_=wo1.rearrange("(kc p) o -> p kc o", p=128)),
                waits=[bTT], dma="wbig")

    wk1 = K.dram["wk1"]
    wv1 = K.dram["wv1"]
    first_h1 = True
    for c2 in range(DCH):
        t_lk = K.on("gpsimd",
                    lambda e, c2=c2: e.dma_start(
                        out=wqp[:].rearrange("p (kc o) -> p kc o", o=128),
                        in_=wk1[:, c2 * 128:(c2 + 1) * 128].rearrange(
                            "(kc q) o -> q kc o", q=128)),
                    waits=trk.wait("wqp"), dma="wk1")
        t_lv = K.on("gpsimd",
                    lambda e, c2=c2: e.dma_start(
                        out=wkp[:].rearrange("p (kc o) -> p kc o", o=128),
                        in_=wv1[:, c2 * 128:(c2 + 1) * 128].rearrange(
                            "(kc q) o -> q kc o", q=128)),
                    waits=trk.wait("wkp"), dma="wv1")
        t_cry = K.on("vector", lambda e: e.memset(cry[:, 0:6], 0.0))
        if c2 == 0:
            prev_cfh = [None]
            prev_pos = [None]
            prev_as = [None]

        for half in range(2):
            ev_k = ev_v = None
            for n2 in range(2):
                nt = half * 2 + n2
                pk = psA[:, n2 * 512:(n2 + 1) * 512]
                pv = psA[:, (2 + n2) * 512:(3 + n2) * 512]

                def mk(e, nt=nt, pk=pk):
                    i = None
                    wv3 = wqp[:].rearrange("p (kc o) -> p kc o", o=128)
                    for kc in range(DCH):
                        i = e.matmul(pk, wv3[:, kc, :],
                                     xnT[kc][:, nt * 512:(nt + 1) * 512],
                                     start=(kc == 0), stop=(kc == DCH - 1))
                    return i
                w = [t_lk] if (half == 0 and n2 == 0) else []
                if half == 0 and n2 == 0:
                    w = list(w) + [xnT_ready]
                w = list(w) + trk.wait(("kps", n2))
                t_mk = K.on("tensor", mk, waits=w)
                ev_k = K.on("scalar",
                            lambda e, n2=n2, pk=pk: e.activation(
                                krT[:, n2 * 512:(n2 + 1) * 512], pk,
                                AF.Identity),
                            waits=[t_mk] +
                            ([K.tok("vector")] if n2 == 0 else []))
                trk.set(("kps", n2), ev_k)

                def mv(e, nt=nt, pv=pv):
                    i = None
                    wv3 = wkp[:].rearrange("p (kc o) -> p kc o", o=128)
                    for kc in range(DCH):
                        i = e.matmul(pv, wv3[:, kc, :],
                                     xnT[kc][:, nt * 512:(nt + 1) * 512],
                                     start=(kc == 0), stop=(kc == DCH - 1))
                    return i
                w = ([t_lv] if (half == 0 and n2 == 0) else []) + \
                    trk.wait(("vps", n2))
                t_mv = K.on("tensor", mv, waits=w)
                ev_v = K.on("scalar",
                            lambda e, n2=n2, pv=pv: e.activation(
                                vT[:, n2 * 512:(n2 + 1) * 512], pv,
                                AF.Identity),
                            waits=[t_mv])
                trk.set(("vps", n2), ev_v)

            for tq in range(2):
                q = half * 2 + tq
                t0 = q * 512
                kq = krT[:, tq * 512:(tq + 1) * 512]
                vq = vT[:, tq * 512:(tq + 1) * 512]
                fc = f2pi[:, c2:c2 + 1]

                K.on("gpsimd",
                     lambda e, fc=fc, t0=t0: e.tensor_scalar_mul(
                         tcol[:], fc, float(t0)),
                     waits=([t_cry] if q == 0 else []) + [t_f2pi] +
                     ([prev_cfh[0]] if prev_cfh[0] is not None else []))
                K.on("gpsimd",
                     lambda e, fc=fc: e.tensor_scalar(
                         s1, iota512, fc, tcol[:, 0:1],
                         ALU.mult, ALU.add))
                K.on("gpsimd", lambda e: e.tensor_copy(i32, s1))
                K.on("gpsimd",
                     lambda e: e.tensor_tensor(s1, s1, i32, ALU.subtract))
                t_sth = K.on("scalar",
                             lambda e: e.activation(sth, s1, AF.Sin,
                                                    scale=TWO_PI),
                             waits=[K.tok("gpsimd")] +
                             ([prev_pos[0]] if prev_pos[0] else []))
                K.on("gpsimd",
                     lambda e: e.tensor_scalar_add(tcol2[:], tcol[:], 0.25),
                     waits=[t_sth])
                K.on("gpsimd",
                     lambda e, fc=fc: e.tensor_scalar(
                         s1, iota512, fc, tcol2[:, 0:1],
                         ALU.mult, ALU.add))
                K.on("gpsimd", lambda e: e.tensor_copy(i32, s1))
                K.on("gpsimd",
                     lambda e: e.tensor_tensor(s1, s1, i32, ALU.subtract))
                t_cth = K.on("scalar",
                             lambda e: e.activation(cth, s1, AF.Sin,
                                                    scale=TWO_PI),
                             waits=[K.tok("gpsimd")])

                K.on("vector",
                     lambda e, vq=vq: e.tensor_tensor(mem_re, vq, cth,
                                                      ALU.mult),
                     waits=[t_cth, ev_v])
                K.on("vector",
                     lambda e, vq=vq: e.tensor_tensor(mem_im, vq, sth,
                                                      ALU.mult))
                K.on("vector",
                     lambda e: e.tensor_tensor_scan(
                         mem_re, ones512, mem_re, cry[:, 0:1],
                         ALU.mult, ALU.add))
                K.on("vector",
                     lambda e: e.tensor_copy(cry[:, 0:1], mem_re[:, 511:512]))
                K.on("vector",
                     lambda e: e.tensor_tensor_scan(
                         mem_im, ones512, mem_im, cry[:, 1:2],
                         ALU.mult, ALU.add))
                K.on("vector",
                     lambda e: e.tensor_copy(cry[:, 1:2], mem_im[:, 511:512]))
                K.on("vector",
                     lambda e: e.tensor_tensor(mem_re, mem_re, cth, ALU.mult))
                K.on("vector",
                     lambda e: e.tensor_tensor(mem_im, mem_im, sth, ALU.mult))
                K.on("vector",
                     lambda e: e.tensor_tensor(mem_re, mem_re, mem_im,
                                               ALU.add))
                K.on("vector",
                     lambda e, c2=c2: e.tensor_scalar_mul(
                         acc, mem_re, g0c[:, c2:c2 + 1]))
                prev_pos[0] = K.tok("vector")

                K.on("gpsimd",
                     lambda e, kq=kq: e.tensor_scalar_mul(s1, kq, R10),
                     waits=[ev_k, t_cth])
                K.on("gpsimd", lambda e: e.tensor_copy(i32, s1))
                K.on("gpsimd",
                     lambda e: e.tensor_tensor(s1, s1, i32, ALU.subtract))
                t_sfh = K.on("scalar",
                             lambda e: e.activation(sfh, s1, AF.Sin,
                                                    scale=TWO_PI),
                             waits=[K.tok("gpsimd")] +
                             ([prev_as[0]] if prev_as[0] else []))
                K.on("gpsimd",
                     lambda e, kq=kq: e.tensor_scalar(
                         s1, kq, R10, 0.25, ALU.mult, ALU.add),
                     waits=[t_sfh])
                K.on("gpsimd", lambda e: e.tensor_copy(i32, s1))
                K.on("gpsimd",
                     lambda e: e.tensor_tensor(s1, s1, i32, ALU.subtract))
                t_cfh = K.on("scalar",
                             lambda e: e.activation(cfh, s1, AF.Sin,
                                                    scale=TWO_PI),
                             waits=[K.tok("gpsimd")])
                prev_cfh[0] = t_cfh

                K.on("vector",
                     lambda e, vq=vq: e.tensor_tensor(
                         mem_re[0:128, 1:512], vq[0:128, 1:512],
                         cfh[0:128, 0:511], ALU.mult),
                     waits=[t_cfh])
                if q == 0:
                    K.on("vector",
                         lambda e: e.memset(mem_re[0:128, 0:1], 0.0))
                else:
                    K.on("vector",
                         lambda e, vq=vq: e.tensor_tensor(
                             mem_re[0:128, 0:1], vq[0:128, 0:1],
                             cry[:, 4:5], ALU.mult))
                K.on("vector",
                     lambda e, vq=vq: e.tensor_tensor(
                         mem_im[0:128, 1:512], vq[0:128, 1:512],
                         sfh[0:128, 0:511], ALU.mult))
                if q == 0:
                    K.on("vector",
                         lambda e: e.memset(mem_im[0:128, 0:1], 0.0))
                else:
                    K.on("vector",
                         lambda e, vq=vq: e.tensor_tensor(
                             mem_im[0:128, 0:1], vq[0:128, 0:1],
                             cry[:, 5:6], ALU.mult))
                K.on("vector",
                     lambda e: e.tensor_copy(cry[:, 4:5], cfh[:, 511:512]))
                K.on("vector",
                     lambda e: e.tensor_copy(cry[:, 5:6], sfh[:, 511:512]))
                K.on("vector",
                     lambda e: e.tensor_tensor_scan(
                         mem_re, ones512, mem_re, cry[:, 2:3],
                         ALU.mult, ALU.add))
                K.on("vector",
                     lambda e: e.tensor_copy(cry[:, 2:3], mem_re[:, 511:512]))
                K.on("vector",
                     lambda e: e.tensor_tensor_scan(
                         mem_im, ones512, mem_im, cry[:, 3:4],
                         ALU.mult, ALU.add))
                K.on("vector",
                     lambda e: e.tensor_copy(cry[:, 3:4], mem_im[:, 511:512]))
                K.on("vector",
                     lambda e: e.tensor_tensor(mem_re, mem_re, cfh, ALU.mult))
                K.on("vector",
                     lambda e: e.tensor_tensor(mem_im, mem_im, sfh, ALU.mult))
                K.on("vector",
                     lambda e: e.tensor_tensor(mem_re, mem_re, mem_im,
                                               ALU.add))
                K.on("vector",
                     lambda e, c2=c2: e.scalar_tensor_tensor(
                         acc, mem_re, g1c[:, c2:c2 + 1], acc,
                         ALU.mult, ALU.add))
                prev_as[0] = K.tok("vector")
                wq_ = [bTT] if first_h1 else []
                first_h1 = False
                K.on("vector",
                     lambda e, c2=c2, t0=t0: e.tensor_tensor(
                         H1[c2][:, t0:t0 + 512], acc,
                         rsq[:, t0:t0 + 512], ALU.mult),
                     waits=wq_)
        trk.set("wqp", K.tok("vector"))
        trk.set("wkp", K.tok("vector"))
    h1_done = K.tok("vector")

    for c in range(TCH):
        par = c % 2
        pao = psB[:, par * 1024:(par + 1) * 1024]

        def mm(e, c=c, pao=pao):
            i = None
            for n in range(2):
                for kc in range(DCH):
                    i = e.matmul(pao[:, n * 512:(n + 1) * 512],
                                 H1[kc][:, c * 128:(c + 1) * 128],
                                 wbigv[:, kc, n * 512:(n + 1) * 512],
                                 start=(kc == 0), stop=(kc == DCH - 1))
            return i
        w = ([h1_done, t_wo] if c == 0 else []) + trk.wait(("wops", par))
        t_mm = K.on("tensor", mm, waits=w)
        t_acc = K.on("vector",
                     lambda e, c=c, pao=pao: e.scalar_tensor_tensor(
                         x_tok[:, c * D:(c + 1) * D], pao, 1.0,
                         x_tok[:, c * D:(c + 1) * D], ALU.mult, ALU.add),
                     waits=[t_mm])
        trk.set(("wops", par), t_acc)
    return K.tok("vector")


def emit_l1_router(K: KB, xnT_ready):
    E = 16
    G = 4
    xnT = [K.tiles[f"xnT{k}"] for k in range(DCH)]
    psA = K.psum("psA", [128, 2048], F32)
    s_sig = K.view("ctx_rd0", [128, 256], F32, 0)
    sc = K.view("ctx_rd0", [128, 256], F32, 1024)
    msc = K.view("r4", [128, 256], F32, 0)
    gsc = K.view("r4", [128, 256], F32, 1024)
    wrt = K.sb("wrt", [128, 512], F32)
    ge_bf = K.sb("ge_bf", [128, 512], BF16)
    rt_r = K.sb("rt_r", [128, TCH], F32)
    rt_r3 = rt_r[:].rearrange("p (c o) -> p c o", o=1)
    G1 = K.view("wbig", [128, 4096], BF16, 0)
    G2 = K.view("wbig", [128, 4096], BF16, 8192)
    rnk = K.view("ctx_rd1", [128, 256], F32, 0)
    gs64 = K.view("ctx_rd1", [128, 64], F32, 1024)
    gm64 = K.view("ctx_rd1", [128, 64], F32, 1280)
    tril = K.tiles["t_c_tril16"]
    ebr = K.tiles["t_c_ebias_rep"]
    trk = Track()
    bA = K.tok("scalar")
    bV = K.tok("vector")
    bT = K.tok("tensor")

    rwt = K.dram["rwt"]
    wg_sb = K.sb("wrt_wg", [128, DCH * 32], BF16)
    wgv = wg_sb[:].rearrange("p (kc o) -> p kc o", o=32)
    t_wg = K.on("gpsimd",
                lambda e: e.dma_start(
                    out=wgv[:, :, 0:16],
                    in_=rwt.rearrange("(kc p) o -> p kc o", p=128)),
                waits=[bT], dma="wg")
    for c in range(TCH):
        slot = c % 4
        pv = psA[:, slot * 512:slot * 512 + 16]

        def mm(e, c=c, pv=pv):
            i = None
            for kc in range(DCH):
                i = e.matmul(pv, xnT[kc][:, c * 128:(c + 1) * 128],
                             wgv[:, kc, 0:16],
                             start=(kc == 0), stop=(kc == DCH - 1))
            return i
        w = [t_wg, xnT_ready, bA, bV, bT] if c == 0 else []
        w = list(w) + trk.wait(("rps", slot))
        t_mm = K.on("tensor", mm, waits=w)
        t_sg = K.on("scalar",
                    lambda e, c=c, pv=pv: e.activation(
                        s_sig[:, c * E:(c + 1) * E], pv, AF.Sigmoid),
                    waits=[t_mm] + ([bV] if c == 0 else []))
        trk.set(("rps", slot), t_sg)
    t_sig = K.tok("scalar")
    t_sc = K.on("vector",
                lambda e: e.tensor_tensor(
                    sc.rearrange("p (c e) -> p c e", e=E),
                    s_sig.rearrange("p (c e) -> p c e", e=E),
                    ebr[:, 0:E].rearrange("p (o e) -> p o e", o=1)
                    .broadcast_to([128, TCH, E]), ALU.add),
                waits=[t_sig, bV, bT])
    sc4 = sc.rearrange("p (c g i) -> p (c g) i", g=G, i=4)
    gsc4 = gsc.rearrange("p (c g i) -> p (c g) i", g=G, i=4)
    gs3 = gs64.rearrange("p (x o) -> p x o", o=1)
    K.on("vector", lambda e: e.tensor_copy(gsc, sc))
    K.on("vector", lambda e: e.reduce_max(gs3, gsc4, axis=AX.X))
    K.on("vector",
         lambda e: e.tensor_tensor(
             ge_bf[:, 0:256].rearrange("p (x i) -> p x i", i=4), gsc4,
             gs3.broadcast_to([128, 64, 4]), ALU.is_ge))
    K.on("vector",
         lambda e: e.scalar_tensor_tensor(
             gsc, ge_bf[:, 0:256], -1e30, gsc, ALU.mult, ALU.add))
    K.on("vector", lambda e: e.reduce_max(
        gm64.rearrange("p (x o) -> p x o", o=1), gsc4, axis=AX.X))
    K.on("vector", lambda e: e.tensor_tensor(gs64, gs64, gm64, ALU.add))
    gsg = gs64.rearrange("p (c g) -> p c g", g=G)
    K.on("vector", lambda e: e.tensor_copy(gm64, gs64))
    gmg = gm64.rearrange("p (c g) -> p c g", g=G)
    K.on("vector", lambda e: e.reduce_max(rt_r3, gmg, axis=AX.X))
    K.on("vector",
         lambda e: e.tensor_tensor(
             ge_bf[:, 0:64].rearrange("p (c g) -> p c g", g=G), gmg,
             rt_r3.broadcast_to([128, TCH, G]), ALU.is_ge))
    K.on("vector",
         lambda e: e.scalar_tensor_tensor(
             gm64, ge_bf[:, 0:64], -1e30, gm64, ALU.mult, ALU.add))
    K.on("vector", lambda e: e.reduce_max(rt_r3, gmg, axis=AX.X))
    K.on("vector",
         lambda e: e.tensor_tensor(
             gm64.rearrange("p (c g) -> p c g", g=G), gsg,
             rt_r3.broadcast_to([128, TCH, G]), ALU.is_ge))
    K.on("vector",
         lambda e: e.tensor_tensor(
             msc.rearrange("p (c g i) -> p c g i", g=G, i=4),
             sc.rearrange("p (c g i) -> p c g i", g=G, i=4),
             gm64.rearrange("p (c g) -> p c g", g=G).unsqueeze(3)
             .broadcast_to([128, TCH, G, 4]),
             ALU.mult))
    mA = msc.rearrange("p (c e) -> p c e", e=E).unsqueeze(2).broadcast_to(
        [128, TCH, E, E])
    mB = msc.rearrange("p (c e) -> p c e", e=E).unsqueeze(3).broadcast_to(
        [128, TCH, E, E])
    g1v = G1.rearrange("p (c e f) -> p c e f", e=E, f=E)
    g2v = G2.rearrange("p (c e f) -> p c e f", e=E, f=E)
    K.on("vector", lambda e: e.tensor_tensor(g1v, mA, mB, ALU.is_gt))
    K.on("vector", lambda e: e.tensor_tensor(g2v, mA, mB, ALU.is_equal))
    K.on("vector",
         lambda e: e.tensor_tensor(
             g2v, g2v,
             tril[:, 0:256].rearrange("p (o e f) -> p o e f", e=E, f=E)
             .broadcast_to([128, TCH, E, E]), ALU.mult))
    K.on("vector", lambda e: e.tensor_tensor(g1v, g1v, g2v, ALU.add))
    K.on("vector",
         lambda e: e.reduce_sum(
             rnk.rearrange("p (c e) -> p c e", e=E).unsqueeze(3),
             g1v, axis=AX.X))
    K.on("vector",
         lambda e: e.tensor_single_scalar(rnk, rnk, 3.5, ALU.is_le))
    K.on("vector",
         lambda e: e.tensor_tensor(msc, s_sig, rnk, ALU.mult))
    K.on("vector",
         lambda e: e.reduce_sum(
             rt_r3, msc.rearrange("p (c e) -> p c e", e=E), axis=AX.X))
    K.on("vector",
         lambda e: e.tensor_scalar_add(rt_r[:], rt_r[:], 1e-20))
    K.on("vector", lambda e: e.reciprocal(rt_r[:], rt_r[:]))
    t_w = None
    for c in range(TCH):
        t_w = K.on("vector",
                   lambda e, c=c: e.tensor_scalar(
                       wrt[:, c * E:(c + 1) * E], msc[:, c * E:(c + 1) * E],
                       rt_r[:, c:c + 1], 2.5, ALU.mult, ALU.mult))
    router_A = K.tok("scalar")
    return t_w, router_A


def emit_l1_moe(K: KB, xnT_ready):
    E = 16
    t_wrt, router_A = emit_l1_router(K, xnT_ready)
    wrt3 = K.tiles["wrt"][:].rearrange("p (c e) -> p c e", e=E)
    gwe, uwe, dwe = K.dram["gwe"], K.dram["uwe"], K.dram["dwe"]
    sg, su, sd = K.dram["sg"], K.dram["su"], K.dram["sd"]

    def expert_weight_aps(e):
        if e < E:
            return (gwe[e].rearrange("(kc p) o -> p kc o", p=128),
                    uwe[e].rearrange("(kc p) o -> p kc o", p=128),
                    dwe[e].rearrange("(j p) o -> p j o", p=128))
        half = e - E
        sl = slice(half * 512, (half + 1) * 512)
        return (sg[:, sl].rearrange("(kc p) o -> p kc o", p=128),
                su[:, sl].rearrange("(kc p) o -> p kc o", p=128),
                sd[sl, :].rearrange("(j p) o -> p j o", p=128))

    bT = K.tok("tensor")
    return emit_moe_experts(K, E, E + 2, expert_weight_aps, wrt3,
                            t_wrt, router_A, bT)


class L1AttnEmitter:

    def __init__(self, K: KB, xnT_ready, t_router_V):
        self.K = K
        self.xnT = [K.tiles[f"xnT{k}"] for k in range(DCH)]
        self.x_tok = K.tiles["x_tok"]
        self.psB = K.psum("psB", [128, 2048], F32)
        self.pk = self.psB[:, 1536:2048]
        self.pv = self.psB[:, 1536:2048]
        self.krT = K.view("ctx_rd0", [128, 512], F32, 0)
        self.vT = K.view("r4", [128, 512], F32, 0)
        self.sth = K.view("xn0_scr", [128, 512], F32, 0)
        self.cth = K.view("xn0_scr", [128, 512], F32, 2048)
        self.sfh = K.view("y_sb", [128, 512], F32, 0)
        self.cfh = K.view("y_sb", [128, 512], F32, 2048)
        self.s1 = K.view("rrep", [128, 512], F32, 0)
        self.i32 = K.view("ctx_rd1", [128, 512], I32, 0)
        self.iota = K.view("xn_chunk", [128, 512], F32, 0)
        self.mem_re = K.view("v65", [128, 512], F32, 24576)
        self.mem_im = K.view("v65", [128, 512], F32, 26624)
        self.acc = K.view("v65", [128, 512], F32, 28672)
        self.h1stg = K.view("ge_bf", [128, 512], BF16, 0)
        self.wqp = K.tiles["wqp"]
        self.wkp = K.tiles["wkp"]
        self.cry = K.sb("cry", [128, 8], F32)
        self.ones1 = K.sb("ones1", [128, 1], F32)
        self.fcol = K.tiles["t_c_freqs_col"]
        self.g0c = K.tiles["t_c_g0_col"]
        self.g1c = K.tiles["t_c_g1_col"]
        self.rsq = K.tiles["t_c_rsq_rep"]
        self.f2pi = K.sb("f2pi_col", [128, DCH], F32)
        self.tcol = K.tiles["muy"]
        self.tcol2 = K.tiles["ssqy"]
        self.h1_scr = K.dscratch("h1_scr", [DCH, 128, T], BF16)
        self.trk = Track()
        self.prev = {}
        bV = K.tok("vector")
        bT = K.tok("tensor")
        t_iota = K.on("sync",
                      lambda e: e.dma_start(out=self.iota,
                                            in_=K.dram["c_iota512"]),
                      waits=[bV, bT], dma="eye")
        t_bqk = K.tok("sync", dma="bqk")
        K.on("vector", lambda e: e.memset(self.ones1[:], 1.0),
             waits=[t_router_V])
        self.t_f2pi = K.on("vector",
                           lambda e: e.tensor_scalar_mul(
                               self.f2pi[:], self.fcol[:], 1.0 / TWO_PI),
                           waits=[t_bqk, t_iota])
        self.t_router_V = t_router_V

    def emit_chunk_start(self, c2):
        K, trk = self.K, self.trk
        wk1 = K.dram["wk1"]
        wv1 = K.dram["wv1"]

        t_lk = K.on("gpsimd",
                    lambda e, c2=c2: e.dma_start(
                        out=self.wqp[:].rearrange("p (kc o) -> p kc o",
                                                  o=128),
                        in_=wk1[:, c2 * 128:(c2 + 1) * 128].rearrange(
                            "(kc q) o -> q kc o", q=128)),
                    waits=trk.wait("wqp"), dma="wk1")
        t_lv = K.on("gpsimd",
                    lambda e, c2=c2: e.dma_start(
                        out=self.wkp[:].rearrange("p (kc o) -> p kc o",
                                                  o=128),
                        in_=wv1[:, c2 * 128:(c2 + 1) * 128].rearrange(
                            "(kc q) o -> q kc o", q=128)),
                    waits=trk.wait("wkp"), dma="wv1")
        t_cry = K.on("vector", lambda e: e.memset(self.cry[:, 0:6], 0.0))
        self._chunk_state = (t_lk, t_lv, t_cry)

    def emit_tile(self, c2, q):
        K, trk = self.K, self.trk
        t_lk, t_lv, t_cry = self._chunk_state
        krT, vT = self.krT, self.vT
        sth, cth, sfh, cfh = self.sth, self.cth, self.sfh, self.cfh
        s1, i32, iota = self.s1, self.i32, self.iota
        mem_re, mem_im, acc = self.mem_re, self.mem_im, self.acc
        cry = self.cry
        R10 = float(10.0 / TWO_PI)
        xnT = self.xnT
        if True:
            t0 = q * 512
            nt = q

            def mk(e, nt=nt):
                i = None
                wv3 = self.wqp[:].rearrange("p (kc o) -> p kc o", o=128)
                for kc in range(DCH):
                    i = e.matmul(self.pk, wv3[:, kc, :],
                                 xnT[kc][:, nt * 512:(nt + 1) * 512],
                                 start=(kc == 0), stop=(kc == DCH - 1))
                return i
            w = [t_lk] + trk.wait("pk")
            t_mk = K.on("tensor", mk, waits=w)
            ev_k = K.on("scalar",
                        lambda e: e.activation(krT, self.pk, AF.Identity),
                        waits=[t_mk] + trk.wait("krT") +
                        ([self.t_router_V] if c2 == 0 and q == 0 else []))
            trk.set("pk", ev_k)

            def mv(e, nt=nt):
                i = None
                wv3 = self.wkp[:].rearrange("p (kc o) -> p kc o", o=128)
                for kc in range(DCH):
                    i = e.matmul(self.pv, wv3[:, kc, :],
                                 xnT[kc][:, nt * 512:(nt + 1) * 512],
                                 start=(kc == 0), stop=(kc == DCH - 1))
                return i
            w = [t_lv] + trk.wait("pk")
            t_mv = K.on("tensor", mv, waits=w)
            ev_v = K.on("scalar",
                        lambda e: e.activation(vT, self.pv, AF.Identity),
                        waits=[t_mv] + trk.wait("vT") +
                        ([self.t_router_V] if c2 == 0 and q == 0 else []))
            trk.set("pk", ev_v)

            fc = self.f2pi[:, c2:c2 + 1]
            tcol, tcol2 = self.tcol, self.tcol2
            K.on("vector",
                 lambda e, fc=fc, t0=t0: e.tensor_scalar_mul(
                     tcol[:], fc, float(t0)),
                 waits=([t_cry] if q == 0 else []) + [self.t_f2pi])
            t_ts = K.on("vector",
                        lambda e, fc=fc: e.tensor_scalar(
                            s1, iota, fc, tcol[:, 0:1], ALU.mult, ALU.add),
                        waits=self.prev.get("cfh_done", []))
            K.on("gpsimd", lambda e: e.tensor_copy(i32, s1), waits=[t_ts])
            K.on("gpsimd",
                 lambda e: e.tensor_tensor(s1, s1, i32, ALU.subtract))
            t_sth = K.on("scalar",
                         lambda e: e.activation(sth, s1, AF.Sin,
                                                scale=TWO_PI),
                         waits=[K.tok("gpsimd")] +
                         self.prev.get("pos_done", []))
            K.on("vector",
                 lambda e: e.tensor_scalar_add(tcol2[:], tcol[:], 0.25))
            t_ts = K.on("vector",
                        lambda e, fc=fc: e.tensor_scalar(
                            s1, iota, fc, tcol2[:, 0:1], ALU.mult, ALU.add),
                        waits=[t_sth])
            K.on("gpsimd", lambda e: e.tensor_copy(i32, s1), waits=[t_ts])
            K.on("gpsimd",
                 lambda e: e.tensor_tensor(s1, s1, i32, ALU.subtract))
            t_cth = K.on("scalar",
                         lambda e: e.activation(cth, s1, AF.Sin,
                                                scale=TWO_PI),
                         waits=[K.tok("gpsimd")] +
                         self.prev.get("pos_done", []))

            K.on("vector",
                 lambda e: e.tensor_tensor(mem_re, vT, cth, ALU.mult),
                 waits=[t_cth, ev_v])
            K.on("vector",
                 lambda e: e.tensor_tensor(mem_im, vT, sth, ALU.mult),
                 waits=[t_sth])
            K.on("vector",
                 lambda e: e.tensor_tensor_scan(
                     mem_re, self.ones1[:].broadcast_to([128, 512]), mem_re,
                     cry[:, 0:1], ALU.mult, ALU.add))
            K.on("vector",
                 lambda e: e.tensor_copy(cry[:, 0:1], mem_re[:, 511:512]))
            K.on("vector",
                 lambda e: e.tensor_tensor_scan(
                     mem_im, self.ones1[:].broadcast_to([128, 512]), mem_im,
                     cry[:, 1:2], ALU.mult, ALU.add))
            K.on("vector",
                 lambda e: e.tensor_copy(cry[:, 1:2], mem_im[:, 511:512]))
            K.on("vector",
                 lambda e: e.tensor_tensor(mem_re, mem_re, cth, ALU.mult))
            K.on("vector",
                 lambda e: e.tensor_tensor(mem_im, mem_im, sth, ALU.mult))
            K.on("vector",
                 lambda e: e.tensor_tensor(mem_re, mem_re, mem_im, ALU.add))
            t_pc = K.on("vector",
                        lambda e, c2=c2: e.tensor_scalar_mul(
                            acc, mem_re, self.g0c[:, c2:c2 + 1]))
            self.prev["pos_done"] = [t_pc]

            t_ts = K.on("vector",
                        lambda e: e.tensor_scalar_mul(s1, krT, R10),
                        waits=[ev_k, t_cth])
            K.on("gpsimd", lambda e: e.tensor_copy(i32, s1), waits=[t_ts])
            K.on("gpsimd",
                 lambda e: e.tensor_tensor(s1, s1, i32, ALU.subtract))
            t_sfh = K.on("scalar",
                         lambda e: e.activation(sfh, s1, AF.Sin,
                                                scale=TWO_PI),
                         waits=[K.tok("gpsimd")] +
                         self.prev.get("as_done", []))
            t_ts = K.on("vector",
                        lambda e: e.tensor_scalar(
                            s1, krT, R10, 0.25, ALU.mult, ALU.add),
                        waits=[t_sfh])
            K.on("gpsimd", lambda e: e.tensor_copy(i32, s1), waits=[t_ts])
            K.on("gpsimd",
                 lambda e: e.tensor_tensor(s1, s1, i32, ALU.subtract))
            t_cfh = K.on("scalar",
                         lambda e: e.activation(cfh, s1, AF.Sin,
                                                scale=TWO_PI),
                         waits=[K.tok("gpsimd")] +
                         self.prev.get("as_done", []))
            self.prev["cfh_done"] = [t_cfh]

            K.on("vector",
                 lambda e: e.tensor_tensor(
                     mem_re[0:128, 1:512], vT[0:128, 1:512],
                     cfh[0:128, 0:511], ALU.mult),
                 waits=[t_cfh])
            if q == 0:
                K.on("vector", lambda e: e.memset(mem_re[0:128, 0:1], 0.0))
            else:
                K.on("vector",
                     lambda e: e.tensor_tensor(
                         mem_re[0:128, 0:1], vT[0:128, 0:1],
                         cry[:, 4:5], ALU.mult))
            K.on("vector",
                 lambda e: e.tensor_tensor(
                     mem_im[0:128, 1:512], vT[0:128, 1:512],
                     sfh[0:128, 0:511], ALU.mult))
            if q == 0:
                K.on("vector", lambda e: e.memset(mem_im[0:128, 0:1], 0.0))
            else:
                K.on("vector",
                     lambda e: e.tensor_tensor(
                         mem_im[0:128, 0:1], vT[0:128, 0:1],
                         cry[:, 5:6], ALU.mult))
            t_lastv = K.on("vector",
                           lambda e: e.tensor_copy(cry[:, 4:5],
                                                   cfh[:, 511:512]))
            trk.set("vT", t_lastv)
            K.on("vector",
                 lambda e: e.tensor_copy(cry[:, 5:6], sfh[:, 511:512]))
            K.on("vector",
                 lambda e: e.tensor_tensor_scan(
                     mem_re, self.ones1[:].broadcast_to([128, 512]), mem_re,
                     cry[:, 2:3], ALU.mult, ALU.add))
            K.on("vector",
                 lambda e: e.tensor_copy(cry[:, 2:3], mem_re[:, 511:512]))
            K.on("vector",
                 lambda e: e.tensor_tensor_scan(
                     mem_im, self.ones1[:].broadcast_to([128, 512]), mem_im,
                     cry[:, 3:4], ALU.mult, ALU.add))
            K.on("vector",
                 lambda e: e.tensor_copy(cry[:, 3:4], mem_im[:, 511:512]))
            K.on("vector",
                 lambda e: e.tensor_tensor(mem_re, mem_re, cfh, ALU.mult))
            K.on("vector",
                 lambda e: e.tensor_tensor(mem_im, mem_im, sfh, ALU.mult))
            K.on("vector",
                 lambda e: e.tensor_tensor(mem_re, mem_re, mem_im, ALU.add))
            t_ac = K.on("vector",
                        lambda e, c2=c2: e.scalar_tensor_tensor(
                            acc, mem_re, self.g1c[:, c2:c2 + 1], acc,
                            ALU.mult, ALU.add))
            self.prev["as_done"] = [t_ac]
            trk.set("krT", t_ac)

            t_h1 = K.on("vector",
                        lambda e, t0=t0: e.tensor_tensor(
                            self.h1stg, acc, self.rsq[:, t0:t0 + 512],
                            ALU.mult),
                        waits=trk.wait("h1stg"))
            t_st = K.on("sync",
                        lambda e, c2=c2, t0=t0: e.dma_start(
                            out=self.h1_scr[c2][:, t0:t0 + 512],
                            in_=self.h1stg),
                        waits=[t_h1], dma="h1st")
            trk.set("h1stg", t_st)
        trk.set("wqp", K.tok("tensor"))
        trk.set("wkp", K.tok("tensor"))

    def emit_wo(self, moe_V_done):
        K, trk = self.K, self.trk
        psB = self.psB
        x_tok = self.x_tok
        bT2 = K.tok("tensor")
        wo1 = K.dram["wo1"]
        wbig = K.tiles["wbig"]
        wbigv = wbig[:].rearrange("p (kc o) -> p kc o", o=1024)
        t_wo1 = K.on("gpsimd",
                     lambda e: e.dma_start(
                         out=wbigv,
                         in_=wo1.rearrange("(kc p) o -> p kc o", p=128)),
                     waits=[bT2], dma="wbig")
        t_h1all = K.tok("sync", dma="h1st")
        bufs = [self.wqp, self.wkp]
        for c in range(TCH):
            par = c % 2
            buf = bufs[par]
            src = self.h1_scr.rearrange("kc p t -> p kc t")[
                :, :, c * 128:(c + 1) * 128]
            dst = buf[:].rearrange("p (kc t) -> p kc t", t=128)
            t_ld = K.on("sync",
                        lambda e, s=src, d=dst: e.dma_start(out=d, in_=s),
                        waits=[t_h1all, bT2] + trk.wait(("wobuf", par)),
                        dma=f"word{par}")
            pao = psB[:, par * 1024:(par + 1) * 1024]

            def mm(e, buf=buf, pao=pao):
                b3 = buf[:].rearrange("p (kc t) -> p kc t", t=128)
                i = None
                for n in range(2):
                    for kc in range(DCH):
                        i = e.matmul(pao[:, n * 512:(n + 1) * 512],
                                     b3[:, kc, :],
                                     wbigv[:, kc, n * 512:(n + 1) * 512],
                                     start=(kc == 0), stop=(kc == DCH - 1))
                return i
            w = [t_ld, t_wo1] + trk.wait(("wops", par))
            if c < 2:
                w += [moe_V_done] + trk.wait("pk") + trk.wait("pv")
            t_mm = K.on("tensor", mm, waits=w)
            trk.set(("wobuf", par), t_mm)
            t_acc = K.on("vector",
                         lambda e, c=c, pao=pao: e.scalar_tensor_tensor(
                             x_tok[:, c * D:(c + 1) * D], pao, 1.0,
                             x_tok[:, c * D:(c + 1) * D],
                             ALU.mult, ALU.add),
                         waits=[t_mm])
            trk.set(("wops", par), t_acc)


def emit_l1_fused(K: KB, xnT_ready):
    E = 16
    t_wrt, router_A = emit_l1_router(K, xnT_ready)
    t_router_V = K.tok("vector")
    wrt3 = K.tiles["wrt"][:].rearrange("p (c e) -> p c e", e=E)
    gwe, uwe, dwe = K.dram["gwe"], K.dram["uwe"], K.dram["dwe"]
    sg, su, sd = K.dram["sg"], K.dram["su"], K.dram["sd"]

    def expert_weight_aps(e):
        if e < E:
            return (gwe[e].rearrange("(kc p) o -> p kc o", p=128),
                    uwe[e].rearrange("(kc p) o -> p kc o", p=128),
                    dwe[e].rearrange("(j p) o -> p j o", p=128))
        half = e - E
        sl = slice(half * 512, (half + 1) * 512)
        return (sg[:, sl].rearrange("(kc p) o -> p kc o", p=128),
                su[:, sl].rearrange("(kc p) o -> p kc o", p=128),
                sd[sl, :].rearrange("(j p) o -> p j o", p=128))

    bT = K.tok("tensor")
    moe = MoeEmitter(K, E, E + 2, expert_weight_aps, wrt3,
                     t_wrt, router_A, bT, ye_mode="narrow")
    la = L1AttnEmitter(K, xnT_ready, t_router_V)

    def moe_parts():
        for e_i in range(E + 2):
            for _ in moe.expert_parts(e_i):
                yield

    gen = moe_parts()
    for c2 in range(DCH):
        la.emit_chunk_start(c2)
        for q in range(4):
            la.emit_tile(c2, q)
            for _ in range(4):
                next(gen, None)
    for _ in gen:
        pass
    la.emit_wo(K.tok("vector"))
    return K.tok("vector")



def emit_store_out(K: KB, waits):
    out = K.dout("out", [T, D], F32)
    x_tok = K.tiles["x_tok"]
    src = x_tok[:].rearrange("p (c d) -> p c d", d=D)
    dst = out.rearrange("(c p) d -> p c d", p=128)
    return K.on("sync", lambda e, s=src, d=dst: e.dma_start(out=d, in_=s),
                waits=waits, dma="out")


def emit_store_debug(K: KB, waits, pairs):
    toks = []
    for nm, ap in pairs:
        toks.append(K.on(
            "sync",
            lambda e, d=K.dram[nm], s=ap: e.dma_start(out=d, in_=s),
            waits=waits, dma="dbg"))
    return toks



def build_program():
    K = KB()
    K.din("x", [T, D], F32)
    K.din("c_eye", [128, 128], BF16)
    K.din("wq", [D, D], F32)
    K.din("wk", [D, D], F32)
    K.din("wv", [D, D], F32)
    K.din("wo", [D, D], F32)
    K.din("c_bq_col", [128, 8], F32)
    K.din("c_bk_col", [128, 8], F32)
    K.din("c_bv_rep", [128, D], F32)
    K.din("c_bo_rep", [128, D], F32)
    K.din("c_gamma_rep", [128, D], F32)
    K.din("c_beta_rep", [128, D], F32)

    for nm in ("c_bq_col", "c_bk_col"):
        tile = K.sb("t_" + nm, list(K.dram[nm].shape), F32)
        K.on("sync",
             lambda e, d=tile, s=K.dram[nm]: e.dma_start(out=d[:], in_=s),
             dma="bqk")
    t_gpc = None
    for nm in ("c_bv_rep", "c_bo_rep", "c_gamma_rep", "c_beta_rep"):
        tile = K.sb("t_" + nm, list(K.dram[nm].shape), BF16)
        t_gpc = K.on("gpsimd",
                     lambda e, d=tile, s=K.dram[nm]: e.dma_start(out=d[:],
                                                                 in_=s),
                     dma="rep")

    K.din("wg", [D, 32], F32)
    K.din("w1e", [32, D, 512], F32)
    K.din("w3e", [32, D, 512], F32)
    K.din("w2e", [32, 512, D], F32)
    K.din("ws1", [D, D], F32)
    K.din("ws3", [D, D], F32)
    K.din("ws2", [D, D], F32)

    K.din("wk1", [D, D], F32)
    K.din("wv1", [D, D], F32)
    K.din("wo1", [D, D], F32)
    K.din("rwt", [D, 16], F32)
    K.din("gwe", [16, D, 512], F32)
    K.din("uwe", [16, D, 512], F32)
    K.din("dwe", [16, 512, D], F32)
    K.din("sg", [D, D], F32)
    K.din("su", [D, D], F32)
    K.din("sd", [D, D], F32)
    K.din("c_freqs_col", [128, 8], F32)
    K.din("c_g0_col", [128, 8], F32)
    K.din("c_g1_col", [128, 8], F32)
    K.din("c_rsq_rep", [128, T], BF16)
    K.din("c_ebias_rep", [128, 16], F32)
    K.din("c_tril16", [128, 256], BF16)
    K.din("c_iota512", [128, 512], F32)
    for nm in ("c_freqs_col", "c_g0_col", "c_g1_col", "c_rsq_rep",
               "c_ebias_rep", "c_tril16"):
        dt_ = BF16 if nm in ("c_tril16", "c_rsq_rep") else F32
        tile = K.sb("t_" + nm, list(K.dram[nm].shape), dt_)
        K.on("sync",
             lambda e, d=tile, s=K.dram[nm]: e.dma_start(out=d[:], in_=s),
             dma="bqk")

    K.din("c_eye32", [128, 128], F32)
    K.dscratch("xn_scr", [T, D], BF16)

    t_x = emit_load_x(K)
    t_eye, t_ms = emit_consts(K)
    ensure_shared_tiles(K)
    done0 = emit_layernorm(K, "0", ready=t_x, t_eye=t_eye, store_xn=True)

    t_wrt, router_A = emit_l0_router(K, done0)
    sc_done, t_dst = emit_l0_sparse_prep(K, t_wrt)
    prep_V = K.tok("vector")

    a0_done = emit_l0_attn(K, done0, t_gpc, sq_guard=sc_done,
                           psA_guard=[router_A],
                           psB_guard=[prep_V])

    E0 = 32
    wrt3 = K.tiles["wrt"][:].rearrange("p (c e) -> p c e", e=E0)
    w1d, w3d, w2d = K.dram["w1e"], K.dram["w3e"], K.dram["w2e"]
    ws1, ws3, ws2 = K.dram["ws1"], K.dram["ws3"], K.dram["ws2"]

    def l0_aps(e):
        if e < E0:
            return (w1d[e].rearrange("(kc p) o -> p kc o", p=128),
                    w3d[e].rearrange("(kc p) o -> p kc o", p=128),
                    w2d[e].rearrange("(j p) o -> p j o", p=128))
        half = e - E0
        sl = slice(half * 512, (half + 1) * 512)
        return (ws1[:, sl].rearrange("(kc p) o -> p kc o", p=128),
                ws3[:, sl].rearrange("(kc p) o -> p kc o", p=128),
                ws2[sl, :].rearrange("(j p) o -> p j o", p=128))

    moe0 = MoeEmitter(K, E0, E0 + 2, l0_aps, wrt3, t_wrt, router_A,
                      K.tok("tensor"), ye_mode="wide", first=False)
    f0_done = emit_l0_sparse_experts(K, moe0, sc_done)
    done1 = emit_layernorm(K, "1", ready=f0_done, stat_tag="0")
    emit_l1_attn(K, done1)
    f1_done = emit_l1_moe(K, done1)

    out_toks = [emit_store_out(K, [f1_done])]
    K.on("sync", lambda e: None, waits=out_toks)
    return K.finish()


def host_inputs(inputs):
    f32 = np.float32
    com = {
        "c_eye": np.eye(128, dtype=ml_dtypes.bfloat16),
        "c_eye32": np.eye(128, dtype=np.float32),
        "wq": np.ascontiguousarray(inputs["L0_attn_Wq"].reshape(D, D), f32),
        "wk": np.ascontiguousarray(inputs["L0_attn_Wk"].reshape(D, D), f32),
        "wv": np.ascontiguousarray(inputs["L0_attn_Wv"].reshape(D, D), f32),
        "wo": np.ascontiguousarray(
            np.asarray(inputs["L0_attn_Wo"], f32).reshape(D, D)),
        "c_bq_col": np.ascontiguousarray(
            np.asarray(inputs["L0_attn_bq"], f32).reshape(-1).reshape(8, 128).T),
        "c_bk_col": np.ascontiguousarray(
            np.asarray(inputs["L0_attn_bk"], f32).reshape(-1).reshape(8, 128).T),
        "c_bv_rep": np.ascontiguousarray(np.tile(
            np.asarray(inputs["L0_attn_bv"], f32).reshape(1, -1), (128, 1))),
        "c_bo_rep": np.ascontiguousarray(np.tile(
            np.asarray(inputs["L0_attn_bo"], f32).reshape(1, -1), (128, 1))),
        "c_gamma_rep": np.ascontiguousarray(np.tile(
            np.asarray(inputs["L0_attn_gamma"], f32).reshape(1, -1),
            (128, 1))),
        "c_beta_rep": np.ascontiguousarray(np.tile(
            np.asarray(inputs["L0_attn_beta"], f32).reshape(1, -1),
            (128, 1))),
        "wg": np.ascontiguousarray(inputs["L0_ffn_Wg"], f32),
        "w1e": np.ascontiguousarray(inputs["L0_ffn_W1"], f32),
        "w3e": np.ascontiguousarray(inputs["L0_ffn_W3"], f32),
        "w2e": np.ascontiguousarray(inputs["L0_ffn_W2"], f32),
        "ws1": np.ascontiguousarray(inputs["L0_ffn_Ws1"], f32),
        "ws3": np.ascontiguousarray(inputs["L0_ffn_Ws3"], f32),
        "ws2": np.ascontiguousarray(inputs["L0_ffn_Ws2"], f32),
        "wk1": np.ascontiguousarray(inputs["L1_attn_Wk"], f32),
        "wv1": np.ascontiguousarray(inputs["L1_attn_Wv"], f32),
        "wo1": np.ascontiguousarray(inputs["L1_attn_Wo"], f32),
        "rwt": np.ascontiguousarray(
            np.asarray(inputs["L1_ffn_router_weight"], f32).T),
        "gwe": np.ascontiguousarray(
            np.asarray(inputs["L1_ffn_gate_w"], f32).transpose(0, 2, 1)),
        "uwe": np.ascontiguousarray(
            np.asarray(inputs["L1_ffn_up_w"], f32).transpose(0, 2, 1)),
        "dwe": np.ascontiguousarray(
            np.asarray(inputs["L1_ffn_down_w"], f32).transpose(0, 2, 1)),
        "sg": np.ascontiguousarray(
            np.asarray(inputs["L1_ffn_sh_gate_w"], f32).T),
        "su": np.ascontiguousarray(
            np.asarray(inputs["L1_ffn_sh_up_w"], f32).T),
        "sd": np.ascontiguousarray(
            np.asarray(inputs["L1_ffn_sh_down_w"], f32).T),
        "c_freqs_col": np.ascontiguousarray(
            np.asarray(inputs["L1_attn_freqs"], f32).reshape(-1)
            .reshape(8, 128).T),
        "c_g0_col": np.ascontiguousarray(
            np.repeat(np.asarray(inputs["L1_attn_gate"], f32)[:, 0], 64)
            .reshape(8, 128).T),
        "c_g1_col": np.ascontiguousarray(
            np.repeat(np.asarray(inputs["L1_attn_gate"], f32)[:, 1], 64)
            .reshape(8, 128).T),
        "c_rsq_rep": np.ascontiguousarray(np.tile(
            (1.0 / np.sqrt(np.arange(1, T + 1, dtype=f32))).reshape(1, -1),
            (128, 1)).astype(ml_dtypes.bfloat16)),
        "c_ebias_rep": np.ascontiguousarray(np.tile(
            np.asarray(inputs["L1_ffn_e_bias"], f32).reshape(1, -1),
            (128, 1))),
        "c_tril16": np.ascontiguousarray(np.tile(
            np.tril(np.ones((16, 16)), -1).reshape(1, -1),
            (128, 1)).astype(ml_dtypes.bfloat16)),
        "c_iota512": np.ascontiguousarray(np.tile(
            np.arange(512, dtype=f32).reshape(1, -1), (128, 1))),
    }
    return com


def kernel(**inputs):
    x = np.asarray(inputs["L0_attn_x"], dtype=np.float32)
    com = host_inputs(inputs)
    nc = build_program()
    in_maps = []
    for c in range(NCORES):
        m = dict(com)
        m["x"] = np.ascontiguousarray(x[c], np.float32)
        in_maps.append(m)
    res = run_bass_kernel_spmd(nc, in_maps, core_ids=list(range(NCORES)))
    out = np.zeros((B, T, D), dtype=np.float32)
    R = res.results
    for c in range(NCORES):
        out[c] = np.asarray(R[c]["out"])
    return out


# version 20
# speedup vs baseline: 1.2252x; 1.0059x over previous
import contextlib

import numpy as np
import ml_dtypes

import concourse.bass as bass
import concourse.mybir as mybir
from concourse.bass_utils import run_bass_kernel_spmd

F32 = mybir.dt.float32
BF16 = mybir.dt.bfloat16
I32 = mybir.dt.int32
AF = mybir.ActivationFunctionType
ALU = mybir.AluOpType
AX = mybir.AxisListType

B, T, D = 8, 2048, 1024
NCORES = 8
TCH = T // 128
DCH = D // 128
PI = float(np.pi)
TWO_PI = float(2 * np.pi)

ENGINES = ("tensor", "vector", "scalar", "gpsimd", "sync")
SERIAL = ("vector", "scalar", "gpsimd")

DEBUG_OUTS = True


class Sem:
    __slots__ = ("h", "n", "name")

    def __init__(self, h, name):
        self.h = h
        self.n = 0
        self.name = name


class KB:

    def __init__(self):
        self.nc = bass.Bass("TRN2", target_bir_lowering=False)
        self.ctx = contextlib.ExitStack()
        self.queues = {e: [] for e in ENGINES}
        self.tiles = {}
        self.sems = {}
        self.dram = {}

    def din(self, name, shape, dtype=F32):
        ap = self.nc.dram_tensor(name, list(shape), dtype,
                                 kind="ExternalInput").ap()
        self.dram[name] = ap
        return ap

    def dout(self, name, shape, dtype=F32):
        ap = self.nc.dram_tensor(name, list(shape), dtype,
                                 kind="ExternalOutput").ap()
        self.dram[name] = ap
        return ap

    def dscratch(self, name, shape, dtype=F32):
        ap = self.nc.dram_tensor(name, list(shape), dtype,
                                 kind="Internal").ap()
        self.dram[name] = ap
        return ap

    def sb(self, name, shape, dtype=F32):
        if name not in self.tiles:
            h = self.ctx.enter_context(
                self.nc.sbuf_tensor(name, list(shape), dtype))
            self.tiles[name] = h
        return self.tiles[name]

    def psum(self, name, shape, dtype=F32):
        if name not in self.tiles:
            h = self.ctx.enter_context(
                self.nc.psum_tensor(name, list(shape), dtype))
            self.tiles[name] = h
        return self.tiles[name]

    def sem(self, name):
        if name not in self.sems:
            h = self.ctx.enter_context(self.nc.semaphore(name))
            self.sems[name] = Sem(h, name)
        return self.sems[name]

    def on(self, eng, fn, waits=(), dma=False):
        wlist = []
        for w in waits:
            s, v = w
            if v > 0:
                wlist.append((s.h, v, s.name))
        if dma:
            s = self.sem(f"dma_{eng}_{dma}")
            k = 16
        else:
            s = self.sem("ser_" + eng)
            k = 1
            if eng in SERIAL and s.n > 0:
                wlist.append((s.h, s.n, s.name))
        s.n += k
        token = (s, s.n)

        def body(e, _w=tuple(wlist), _fn=fn, _sh=s.h, _k=k):
            for (h, v, _nm) in _w:
                e.wait_ge(h, v)
            inst = _fn(e)
            if inst is None:
                inst = e.nop()
            inst.then_inc(_sh, _k)

        self.queues[eng].append(body)
        return token

    def tok(self, eng, dma=None):
        s = self.sem(f"dma_{eng}_{dma}" if dma else "ser_" + eng)
        return (s, s.n)

    def view(self, src, shape, dtype, byte_off=0):
        h = self.tiles[src]
        ap = h[:].bitcast(dtype)
        esz = 2 if dtype == BF16 else 4
        off = byte_off // esz
        return ap[0:shape[0], off:off + shape[1]]

    def finish(self):
        block = self.ctx.enter_context(self.nc.Block())
        for eng in ENGINES:
            q = self.queues[eng]
            if not q:
                continue

            def run_all(e, _q=tuple(q)):
                for fn in _q:
                    fn(e)

            getattr(block, eng)(run_all)
        self.ctx.close()
        return self.nc


class Track:

    def __init__(self):
        self.d = {}

    def wait(self, key):
        return [self.d[key]] if key in self.d else []

    def set(self, key, token):
        self.d[key] = token



def emit_load_x(K: KB):
    x = K.dram["x"]
    x_tok = K.sb("x_tok", [128, TCH * D], F32)
    src = x.rearrange("(c p) d -> p c d", p=128)
    dst = x_tok[:].rearrange("p (c d) -> p c d", d=D)
    return K.on("sync", lambda e, s=src, d=dst: e.dma_start(out=d, in_=s),
                dma="x")


def emit_consts(K: KB):
    eye = K.sb("eye_bf16", [128, 128], BF16)
    t_eye = K.on("sync",
                 lambda e, d=eye: e.dma_start(out=d[:], in_=K.dram["c_eye"]),
                 dma="eye")
    K.sb("c_eps5", [128, 1], F32)
    K.sb("c_eps3", [128, 1], F32)
    K.on("vector", lambda e: e.memset(K.tiles["c_eps5"][:], 1e-5))
    t_ms = K.on("vector", lambda e: e.memset(K.tiles["c_eps3"][:], 1e-3))
    return t_eye, t_ms


def emit_layernorm(K: KB, tag, ready, t_eye=None, eps_name="c_eps5",
                   src="x_tok", stat_tag=None, store_xn=False):
    x_tok = K.tiles[src]
    stat_tag = tag if stat_tag is None else stat_tag
    mu = K.sb(f"mu{stat_tag}", [128, TCH], F32)
    nmr = K.sb(f"nmr{stat_tag}", [128, TCH], F32)
    rstd = K.sb(f"rstd{stat_tag}", [128, TCH], F32)
    ssq = K.sb(f"ssq{stat_tag}", [128, TCH], F32)
    sq_scr = K.sb("sq_scr", [128, D], F32)
    xn_chunk = K.sb("xn_chunk", [128, D], BF16)
    xnT = [K.sb(f"xnT{k}", [128, T], BF16) for k in range(DCH)]
    psA = K.psum("psA", [128, 2048], F32)
    pT2 = [psA[:, 0:512].bitcast(BF16), psA[:, 512:1024].bitcast(BF16)]
    eps = K.tiles[eps_name]
    eye = K.tiles["eye_bf16"]

    x3 = x_tok[:].rearrange("p (c d) -> p c d", d=D)
    K.on("vector",
         lambda e: e.reduce_sum(mu[:].rearrange("p (c o) -> p c o", o=1),
                                x3, axis=AX.X), waits=[ready])
    K.on("vector", lambda e: e.tensor_scalar_mul(mu[:], mu[:], 1.0 / D))
    for c in range(TCH):
        K.on("scalar",
             lambda e, c=c: e.activation(
                 sq_scr[:], x_tok[:, c * D:(c + 1) * D], AF.Square,
                 bias=mu[:, c:c + 1], scale=-1.0,
                 accum_out=ssq[:, c:c + 1]),
             waits=[K.tok("vector")] if c == 0 else [])
    K.on("scalar",
         lambda e: e.activation(rstd[:], ssq[:], AF.Ln,
                                bias=eps[:, 0:1], scale=1.0 / D))
    K.on("scalar",
         lambda e: e.activation(rstd[:], rstd[:], AF.Exp, scale=-0.5))
    K.on("vector",
         lambda e: e.tensor_tensor(nmr[:], mu[:], rstd[:], ALU.mult),
         waits=[K.tok("scalar")])
    t_stats = K.on("vector",
                   lambda e: e.tensor_scalar_mul(nmr[:], nmr[:], -1.0))

    tr_toks = {}
    ev_toks = {}
    norm_toks = {}
    st_toks = {}
    for c in range(TCH):
        pT = pT2[c % 2]
        w = [t_stats]
        if c >= 1:
            w.append(tr_toks[c - 1])
        if store_xn and c >= 1:
            w.append(st_toks[c - 1])
        norm_toks[c] = K.on(
            "scalar",
            lambda e, c=c: e.activation(
                xn_chunk[:], x_tok[:, c * D:(c + 1) * D], AF.Identity,
                bias=nmr[:, c:c + 1], scale=rstd[:, c:c + 1]),
            waits=w)
        if store_xn:
            st_toks[c] = K.on(
                "sync",
                lambda e, c=c: e.dma_start(
                    out=K.dram["xn_scr"][c * 128:(c + 1) * 128, :],
                    in_=xn_chunk[:]),
                waits=[norm_toks[c]], dma="xnst")
        for k in range(DCH):
            w = []
            if k == 0:
                w = [norm_toks[c]]
                if c == 0 and t_eye is not None:
                    w.append(t_eye)
                if c >= 2:
                    w.append(ev_toks[c - 2])
            tr_toks[c] = K.on(
                "tensor",
                lambda e, c=c, k=k, pT=pT: e.transpose(
                    pT[:, k * 128:(k + 1) * 128],
                    xn_chunk[:, k * 128:(k + 1) * 128], eye[:]),
                waits=w)
        for k in range(DCH):
            ev_toks[c] = K.on(
                "vector",
                lambda e, c=c, k=k, pT=pT: e.tensor_copy(
                    xnT[k][:, c * 128:(c + 1) * 128],
                    pT[:, k * 128:(k + 1) * 128]),
                waits=[tr_toks[c]] if k == 0 else [])
    return ev_toks[TCH - 1]


def emit_l0_attn(K: KB, xnT_ready, t_gpc, sq_guard=(), psA_guard=(),
                 psB_guard=()):
    H = 16
    xnT = [K.tiles[f"xnT{k}"] for k in range(DCH)]
    x_tok = K.tiles["x_tok"]
    psA = K.psum("psA", [128, 2048], F32)
    psB = K.psum("psB", [128, 2048], F32)

    v65 = K.sb("v65", [128, TCH * H * 65], BF16)
    v65v = v65[:].rearrange("p (c h o) -> p c h o", h=H, o=65)
    qT = K.sb("qT", [128, T], BF16)
    kTt = K.sb("kTt", [128, T], BF16)
    Ea = K.sb("Ea", [128, T], BF16)
    Eb = K.sb("Eb", [128, T], BF16)
    r4 = K.sb("r4", [128, 512], F32)
    ones64 = K.sb("ones64", [1, 64], F32)
    rrep = K.sb("rrep", [128, 512], F32)
    ctx_stg = K.sb("ctx_stg", [64, T], BF16)
    ctx_rd = [K.sb("ctx_rd0", [128, 1024], BF16),
              K.sb("ctx_rd1", [128, 1024], BF16)]
    wbig = K.sb("wbig", [128, DCH * 1024], BF16)
    wbigv = wbig[:].rearrange("p (kc o) -> p kc o", o=1024)
    wqp = K.sb("wqp", [128, 1024], BF16)
    wkp = K.sb("wkp", [128, 1024], BF16)
    xn0_scr = K.sb("xn0_scr", [128, D], F32)
    y_sb = K.sb("y_sb", [128, D], F32)
    yn_sb = y_sb
    a0_sb = xn0_scr
    sq_scr = K.sb("sq_scr", [128, D], F32)
    muy = K.sb("muy", [128, 1], F32)
    ssqy = K.sb("ssqy", [128, 1], F32)
    rstdy = K.sb("rstdy", [128, 1], F32)
    nmry = K.sb("nmry", [128, 1], F32)
    c_n16 = K.sb("c_n16", [128, 1], F32)
    ctx_scr = K.dscratch("ctx_scr", [H, 64, T], BF16)

    trk = Track()
    K.on("vector", lambda e: e.memset(ones64[:], 1.0))
    K.on("vector", lambda e: e.memset(c_n16[:], -16.0))
    t_ones = K.on("vector", lambda e: e.memset(v65v[:, :, :, 64:65], 1.0))

    wv = K.dram["wv"]
    t_wv = K.on("gpsimd",
                lambda e: e.dma_start(
                    out=wbigv, in_=wv.rearrange("(kc p) o -> p kc o", p=128)),
                dma="wbig")

    bv_rep = K.tiles["t_c_bv_rep"]
    bv3 = bv_rep[:].rearrange("p (h o) -> p h o", o=64)
    for c in range(TCH):
        for n in range(2):
            slot = (c * 2 + n) % 4
            pv = psA[:, slot * 512:(slot + 1) * 512]

            def mm(e, c=c, n=n, pv=pv):
                i = None
                for kc in range(DCH):
                    i = e.matmul(pv, xnT[kc][:, c * 128:(c + 1) * 128],
                                 wbigv[:, kc, n * 512:(n + 1) * 512],
                                 start=(kc == 0), stop=(kc == DCH - 1))
                return i
            w = list([t_wv, xnT_ready, t_ones] + list(psA_guard)) \
                if (c == 0 and n == 0) else []
            w = list(w) + trk.wait(("psA", slot))
            t_mm = K.on("tensor", mm, waits=w)

            def ev(e, c=c, n=n, pv=pv):
                return e.scalar_tensor_tensor(
                    v65v[:, c, n * 8:(n + 1) * 8, 0:64],
                    pv.rearrange("p (h o) -> p h o", o=64),
                    1.0, bv3[:, n * 8:(n + 1) * 8, :],
                    ALU.mult, ALU.add)
            wev = [t_mm]
            if c == 0 and n == 0:
                wev.append(t_gpc)
            t_ev = K.on("vector", ev, waits=wev)
            trk.set(("psA", slot), t_ev)
    v_prod = K.tok("tensor")

    wo = K.dram["wo"]
    t_wo = K.on("gpsimd",
                lambda e: e.dma_start(
                    out=wbigv, in_=wo.rearrange("(kc p) o -> p kc o", p=128)),
                waits=[v_prod], dma="wbig")

    bqc = K.tiles["t_c_bq_col"]
    bkc = K.tiles["t_c_bk_col"]

    for p in range(8):
        wq = K.dram["wq"]
        wk = K.dram["wk"]
        wqv = wq[:, p * 128:(p + 1) * 128].rearrange(
            "(kc q) o -> q kc o", q=128)
        wkv = wk[:, p * 128:(p + 1) * 128].rearrange(
            "(kc q) o -> q kc o", q=128)
        t_lq = K.on("gpsimd",
                    lambda e, s=wqv: e.dma_start(
                        out=wqp[:].rearrange("p (kc o) -> p kc o", o=128),
                        in_=s),
                    waits=trk.wait("wqp"), dma="wq")
        t_lk = K.on("gpsimd",
                    lambda e, s=wkv: e.dma_start(
                        out=wkp[:].rearrange("p (kc o) -> p kc o", o=128),
                        in_=s),
                    waits=trk.wait("wkp"), dma="wk")

        for (wtile, otile, bcol, t_l) in ((wqp, qT, bqc, t_lq),
                                          (wkp, kTt, bkc, t_lk)):
            for n in range(4):
                slot = n
                pv = psA[:, slot * 512:(slot + 1) * 512]

                def mm(e, wtile=wtile, n=n, pv=pv):
                    wv3 = wtile[:].rearrange("p (kc o) -> p kc o", o=128)
                    i = None
                    for kc in range(DCH):
                        i = e.matmul(pv, wv3[:, kc, :],
                                     xnT[kc][:, n * 512:(n + 1) * 512],
                                     start=(kc == 0), stop=(kc == DCH - 1))
                    return i
                w = [t_l] + trk.wait(("psA", slot)) + trk.wait("psA_free")
                t_mm = K.on("tensor", mm, waits=w)

                wev2 = [t_mm]
                if p == 0 and n == 0:
                    wev2.append(K.tok("sync", dma="bqk"))
                t_ev = K.on(
                    "scalar",
                    lambda e, otile=otile, bcol=bcol, n=n, pv=pv, p=p:
                    e.activation(otile[:, n * 512:(n + 1) * 512], pv,
                                 AF.Identity, bias=bcol[:, p:p + 1]),
                    waits=wev2)
                trk.set(("psA", slot), t_ev)
        trk.set("wqp", K.tok("scalar"))
        trk.set("wkp", K.tok("scalar"))
        qk_done = K.tok("scalar")

        for hh in range(2):
            h = 2 * p + hh
            sT = K.sem("ser_tensor")
            sA = K.sem("ser_scalar")
            t0 = sT.n
            a0 = sA.n
            for c in range(TCH):
                Ebuf = Ea if c % 2 == 0 else Eb
                for half in range(2):
                    def smm(e, c=c, hh=hh, half=half):
                        lhs = kTt[hh * 64:(hh + 1) * 64,
                                  c * 128:(c + 1) * 128]
                        i = None
                        for n in (2 * half, 2 * half + 1):
                            i = e.matmul(psA[:, n * 512:(n + 1) * 512], lhs,
                                         qT[hh * 64:(hh + 1) * 64,
                                            n * 512:(n + 1) * 512],
                                         start=True, stop=True)
                        return i
                    if c == 0:
                        w = ([qk_done] + trk.wait("psA_free")) \
                            if half == 0 else []
                    else:
                        w = [(sA, a0 + 2 * (c - 1) + half + 1)]
                    K.on("tensor", smm, waits=w)
                for half in range(2):
                    wex = [(sT, t0 + 4 * c + half + 1)]
                    K.on("scalar",
                         lambda e, half=half, Ebuf=Ebuf: e.activation(
                             Ebuf[:, half * 1024:(half + 1) * 1024],
                             psA[:, half * 1024:(half + 1) * 1024],
                             AF.Exp, bias=c_n16[:, 0:1], scale=0.125),
                         waits=wex)
                for half in range(2):
                    def cmm(e, c=c, h=h, half=half, Ebuf=Ebuf):
                        i = None
                        for n in (2 * half, 2 * half + 1):
                            i = e.matmul(psB[0:65, n * 512:(n + 1) * 512],
                                         v65v[:, c, h, :],
                                         Ebuf[:, n * 512:(n + 1) * 512],
                                         start=(c == 0), stop=(c == TCH - 1))
                        return i
                    w = [(sA, a0 + 2 * c + half + 1)]
                    if c == 0 and half == 0:
                        w += trk.wait("psB_free")
                        if p == 0 and hh == 0:
                            w += list(psB_guard)
                    K.on("tensor", cmm, waits=w)
            ctx_done = K.tok("tensor")

            bc_prev = None
            for n in range(4):
                w = [ctx_done] if n == 0 else [bc_prev]
                t_rc = K.on(
                    "vector",
                    lambda e, n=n: e.reciprocal(
                        r4[0:1, 0:512],
                        psB[64:65, n * 512:(n + 1) * 512]),
                    waits=w)
                w = [t_rc]
                if n == 0:
                    w.append((sA, a0 + 2 * TCH))
                bc_prev = K.on(
                    "tensor",
                    lambda e, n=n: e.matmul(
                        psA[0:64, n * 512:(n + 1) * 512], ones64[0:1, :],
                        r4[0:1, 0:512],
                        start=True, stop=True),
                    waits=w)
                t_cp = K.on(
                    "scalar",
                    lambda e, n=n: e.copy(rrep[0:64, 0:512],
                                          psA[0:64, n * 512:(n + 1) * 512]),
                    waits=[bc_prev] + trk.wait("rrep"))
                t_ml = K.on(
                    "vector",
                    lambda e, n=n: e.tensor_tensor(
                        ctx_stg[:, n * 512:(n + 1) * 512],
                        psB[0:64, n * 512:(n + 1) * 512],
                        rrep[0:64, 0:512], ALU.mult),
                    waits=[t_cp] + (trk.wait("ctx_stg") if n == 0 else []))
                trk.set("rrep", t_ml)
            trk.set("psB_free", K.tok("vector"))
            t_st = K.on("sync",
                        lambda e, h=h: e.dma_start(out=ctx_scr[h],
                                                   in_=ctx_stg[:]),
                        waits=[K.tok("vector")], dma="ctxst")
            trk.set("ctx_stg", t_st)
            trk.set("psA_free", K.tok("scalar"))

    g_rep = K.tiles["t_c_gamma_rep"]
    be_rep = K.tiles["t_c_beta_rep"]
    bo_rep = K.tiles["t_c_bo_rep"]
    rstd0 = K.tiles["rstd0"]
    nmr0 = K.tiles["nmr0"]
    c_eps3 = K.tiles["c_eps3"]
    for c in range(TCH):
        buf = ctx_rd[c % 2]
        src = ctx_scr.rearrange("(kc h2) dv t -> (h2 dv) kc t", h2=2)[
            :, :, c * 128:(c + 1) * 128]
        dst = buf[:].rearrange("p (kc t) -> p kc t", t=128)
        t_ld = K.on("sync",
                    lambda e, s=src, d=dst: e.dma_start(out=d, in_=s),
                    waits=[K.tok("sync", dma="ctxst")] +
                    trk.wait(("ctx_rd", c % 2)), dma=f"ctxrd{c % 2}")

        pao = psB[:, (c % 2) * 1024:((c % 2) + 1) * 1024]

        def mm(e, buf=buf, pao=pao):
            b3 = buf[:].rearrange("p (kc t) -> p kc t", t=128)
            i = None
            for n in range(2):
                for kc in range(DCH):
                    i = e.matmul(pao[:, n * 512:(n + 1) * 512], b3[:, kc, :],
                                 wbigv[:, kc, n * 512:(n + 1) * 512],
                                 start=(kc == 0), stop=(kc == DCH - 1))
            return i
        w = [t_ld, t_wo] + trk.wait(("psB_ao", c % 2))
        if c < 2:
            w += trk.wait("psB_free")
        t_mm = K.on("tensor", mm, waits=w)
        trk.set(("ctx_rd", c % 2), t_mm)

        t_xn0 = K.on("scalar",
                     lambda e, c=c: e.activation(
                         xn0_scr[:], x_tok[:, c * D:(c + 1) * D], AF.Identity,
                         bias=nmr0[:, c:c + 1], scale=rstd0[:, c:c + 1]),
                     waits=trk.wait("finchunk"))
        wy = [t_mm, t_xn0]
        if c == 0:
            wy.append(t_gpc)
        t_y = K.on("vector",
                   lambda e, pao=pao: e.scalar_tensor_tensor(
                       y_sb[:], pao, 1.0, xn0_scr[:], ALU.mult, ALU.add),
                   waits=wy)
        trk.set(("psB_ao", c % 2), t_y)
        K.on("vector",
             lambda e: e.tensor_tensor(y_sb[:], y_sb[:], bo_rep[:], ALU.add))
        K.on("vector",
             lambda e: e.reduce_sum(muy[:].rearrange("p (c o) -> p c o", o=1),
                                    y_sb[:].rearrange("p (c d) -> p c d", c=1),
                                    axis=AX.X))
        t_mu = K.on("vector",
                    lambda e: e.tensor_scalar_mul(muy[:], muy[:], 1.0 / D))
        wsq = [t_mu]
        if c == 0:
            wsq += list(sq_guard)
        K.on("scalar",
             lambda e: e.activation(sq_scr[:], y_sb[:], AF.Square,
                                    bias=muy[:, 0:1], scale=-1.0,
                                    accum_out=ssqy[:, 0:1]),
             waits=wsq)
        K.on("scalar",
             lambda e: e.activation(rstdy[:], ssqy[:], AF.Ln,
                                    bias=c_eps3[:, 0:1], scale=1.0 / D))
        t_rs = K.on("scalar",
                    lambda e: e.activation(rstdy[:], rstdy[:], AF.Exp,
                                           scale=-0.5))
        K.on("vector",
             lambda e: e.tensor_tensor(nmry[:], muy[:], rstdy[:], ALU.mult),
             waits=[t_rs])
        t_nm = K.on("vector",
                    lambda e: e.tensor_scalar_mul(nmry[:], nmry[:], -1.0))
        t_yn = K.on("scalar",
                    lambda e: e.activation(yn_sb[:], y_sb[:], AF.Identity,
                                           bias=nmry[:, 0:1],
                                           scale=rstdy[:, 0:1]),
                    waits=[t_nm])
        K.on("vector",
             lambda e: e.tensor_tensor(a0_sb[:], yn_sb[:], g_rep[:],
                                       ALU.mult),
             waits=[t_yn])
        K.on("vector",
             lambda e: e.tensor_tensor(a0_sb[:], a0_sb[:], be_rep[:],
                                       ALU.add))
        t_fin = K.on("vector",
                     lambda e, c=c: e.tensor_tensor(
                         x_tok[:, c * D:(c + 1) * D],
                         x_tok[:, c * D:(c + 1) * D],
                         a0_sb[:], ALU.add))
        trk.set("finchunk", t_fin)
    return K.tok("vector")


def emit_l0_router(K: KB, xnT_ready):
    E = 32
    xnT = [K.tiles[f"xnT{k}"] for k in range(DCH)]
    psA = K.psum("psA", [128, 2048], F32)
    s_all = K.view("ctx_rd0", [128, 512], F32, 0)
    s3 = s_all.rearrange("p (c e) -> p c e", e=E)
    rt_scr = K.view("r4", [128, 512], F32, 0)
    rt3 = rt_scr.rearrange("p (c e) -> p c e", e=E)
    wrt = K.sb("wrt", [128, 512], F32)
    ge_bf = K.sb("ge_bf", [128, 512], BF16)
    ge3 = ge_bf[:].rearrange("p (c e) -> p c e", e=E)
    rt_r = K.sb("rt_r", [128, TCH], F32)
    rt_r3 = rt_r[:].rearrange("p (c o) -> p c o", o=1)
    rt_m = K.tiles["muy"]
    rt_sum = K.tiles["ssqy"]
    trk = Track()
    bA = K.tok("scalar")
    bV = K.tok("vector")

    wg = K.dram["wg"]
    wg_sb = K.sb("wrt_wg", [128, DCH * 32], BF16)
    wgv = wg_sb[:].rearrange("p (kc o) -> p kc o", o=32)
    t_wg = K.on("gpsimd",
                lambda e: e.dma_start(
                    out=wgv, in_=wg.rearrange("(kc p) o -> p kc o", p=128)),
                dma="wg")
    for c in range(TCH):
        slot = c % 4
        pv = psA[:, slot * 512:slot * 512 + 32]

        def mm(e, c=c, pv=pv):
            i = None
            for kc in range(DCH):
                i = e.matmul(pv, xnT[kc][:, c * 128:(c + 1) * 128],
                             wgv[:, kc, :],
                             start=(kc == 0), stop=(kc == DCH - 1))
            return i
        w = [t_wg, xnT_ready, bA, bV] if c == 0 else []
        w = list(w) + trk.wait(("rps", slot))
        t_mm = K.on("tensor", mm, waits=w)

        K.on("vector",
             lambda e, pv=pv: e.reduce_max(
                 rt_m[:].rearrange("p (c o) -> p c o", o=1),
                 pv.rearrange("p (c e) -> p c e", c=1), axis=AX.X),
             waits=[t_mm])
        K.on("vector",
             lambda e: e.tensor_scalar_mul(rt_m[:], rt_m[:], -1.0))
        t_exp = K.on("scalar",
                     lambda e, c=c, pv=pv: e.activation(
                         s_all[:, c * E:(c + 1) * E], pv, AF.Exp,
                         bias=rt_m[:, 0:1], accum_out=rt_sum[:, 0:1]),
                     waits=[K.tok("vector")])
        trk.set(("rps", slot), t_exp)
        K.on("vector", lambda e: e.reciprocal(rt_sum[:], rt_sum[:]),
             waits=[t_exp])
        K.on("vector",
             lambda e, c=c: e.tensor_scalar_mul(
                 s_all[:, c * E:(c + 1) * E], s_all[:, c * E:(c + 1) * E],
                 rt_sum[:, 0:1]))
    K.on("vector", lambda e: e.tensor_copy(rt_scr[:], s_all[:]))
    for rnd in range(3):
        K.on("vector",
             lambda e: e.reduce_max(rt_r3, rt3, axis=AX.X))
        K.on("vector",
             lambda e: e.tensor_tensor(
                 ge3, rt3, rt_r3.broadcast_to([128, TCH, E]), ALU.is_ge))
        K.on("vector",
             lambda e: e.scalar_tensor_tensor(
                 rt_scr[:], ge_bf[:], -1e30, rt_scr[:], ALU.mult, ALU.add))
    K.on("vector", lambda e: e.reduce_max(rt_r3, rt3, axis=AX.X))
    K.on("vector",
         lambda e: e.tensor_tensor(
             ge3, s3, rt_r3.broadcast_to([128, TCH, E]), ALU.is_ge))
    t_wrt = K.on("vector",
                 lambda e: e.tensor_tensor(wrt[:], s_all[:], ge_bf[:],
                                           ALU.mult))
    router_A = K.tok("scalar")
    return t_wrt, router_A


def ensure_shared_tiles(K: KB):
    K.sb("v65", [128, TCH * 16 * 65], BF16)
    K.sb("qT", [128, T], BF16)
    K.sb("kTt", [128, T], BF16)
    K.sb("Ea", [128, T], BF16)
    K.sb("Eb", [128, T], BF16)
    K.sb("r4", [128, 512], F32)
    K.sb("ones64", [1, 64], F32)
    K.sb("rrep", [128, 512], F32)
    K.sb("ctx_stg", [64, T], BF16)
    K.sb("ctx_rd0", [128, 1024], BF16)
    K.sb("ctx_rd1", [128, 1024], BF16)
    K.sb("wbig", [128, DCH * 1024], BF16)
    K.sb("wqp", [128, 1024], BF16)
    K.sb("wkp", [128, 1024], BF16)
    K.sb("xn0_scr", [128, D], F32)
    K.sb("y_sb", [128, D], F32)
    K.sb("muy", [128, 1], F32)
    K.sb("ssqy", [128, 1], F32)
    K.sb("rstdy", [128, 1], F32)
    K.sb("nmry", [128, 1], F32)
    K.sb("c_n16", [128, 1], F32)


def emit_l0_sparse_prep(K: KB, t_wrt):
    C = 512
    E = 32
    NR = E * C
    mask = K.tiles["ge_bf"]
    mask3 = mask[:].rearrange("p (c e) -> p c e", e=E)
    wrt = K.tiles["wrt"]
    wrt3 = wrt[:].rearrange("p (c e) -> p c e", e=E)
    psB = K.psum("psB", [128, 2048], F32)
    pTb16 = [psB[:, 0:512].bitcast(BF16), psB[:, 512:1024].bitcast(BF16)]
    pTf32 = [psB[:, 1024:1536], psB[:, 1536:2048]]
    eyeb = K.tiles["eye_bf16"]
    eye32 = K.view("wrt_wg", [128, 128], F32, 0)
    maskT = K.view("y_sb", [32, 2048], BF16, 0)
    pos_exp = K.view("v65", [32, 2048], F32, 0)
    pos_tok = K.view("xn0_scr", [128, 512], F32, 0)
    pos3 = pos_tok.rearrange("p (c e) -> p c e", e=E)
    cum_all = K.view("r4", [128, 512], F32, 0)
    cum3 = cum_all.rearrange("p (c e) -> p c e", e=E)
    sel = K.view("ctx_rd1", [128, 512], BF16, 0)
    sel3 = sel.rearrange("p (c e) -> p c e", e=E)
    tmul = K.view("sq_scr", [128, 512], F32, 0)
    tm3 = tmul.rearrange("p (c e) -> p c e", e=E)
    eio = K.sb("xn_chunk", [128, D], BF16)[:].bitcast(F32)[0:128, 0:32]
    spdst = K.view("xn_chunk", [128, 64], I32, 128)
    spdf = K.view("xn_chunk", [128, 64], F32, 384)
    spdf3 = spdf[:].rearrange("p (c k) -> p c k", k=4)
    spw = K.view("xn_chunk", [128, 64], F32, 640)
    spw3 = spw.rearrange("p (c k) -> p c k", k=4)
    spe = K.view("xn_chunk", [128, 64], F32, 896)
    spe3 = spe.rearrange("p (c k) -> p c k", k=4)
    ones1 = K.sb("ones1", [128, 1], F32)
    gx = K.dscratch("gx", [NR, 1040], BF16)
    xn_dram = K.dram["xn_scr"]
    xgw = K.view("sq_scr", [128, 1040], BF16, 0)
    xgw_w = K.view("sq_scr", [128, 8], F32, 2048)
    trk = Track()

    def mkbc(e):
        r = e.alloc_register("sp_bc")
        K.cache = getattr(K, "cache", {})
        K.cache["sp_bc"] = r
        return e.reg_mov(r, NR - 1)
    K.on("gpsimd", mkbc)
    t_xnst = K.tok("sync", dma="xnst")
    t_lnT = K.tok("tensor")
    t_eio = K.on("sync",
                 lambda e: e.dma_start(out=eio,
                                       in_=K.dram["c_iota512"][:, 0:32]),
                 waits=[t_wrt, t_xnst, t_lnT], dma="eio")
    t_e32 = K.on("sync",
                 lambda e: e.dma_start(out=eye32, in_=K.dram["c_eye32"]),
                 waits=[K.tok("tensor")], dma="e32")
    K.on("vector", lambda e: e.memset(ones1[:], 1.0),
         waits=[t_wrt, t_xnst, t_eio])

    ev_toks = []
    for c in range(TCH):
        pT = pTb16[c % 2]
        w = []
        if c < 2:
            w = [t_wrt]
        else:
            w = [ev_toks[c - 2]]
        t_tr = K.on("tensor",
                    lambda e, c=c, pT=pT: e.transpose(
                        pT[0:32, 0:128], mask[:, c * 32:(c + 1) * 32],
                        eyeb[:]),
                    waits=w)
        ev_toks.append(K.on("vector",
                            lambda e, c=c, pT=pT: e.tensor_copy(
                                maskT[:, c * 128:(c + 1) * 128],
                                pT[0:32, 0:128]),
                            waits=[t_tr]))
    t_pos = K.on("vector",
                 lambda e: e.tensor_tensor_scan(
                     pos_exp, ones1[0:32, 0:1].broadcast_to([32, 2048]),
                     maskT, 0.0, ALU.mult, ALU.add))
    ev2 = []
    for c in range(TCH):
        pT = pTf32[c % 2]
        w = [t_pos, t_e32] if c < 2 else [ev2[c - 2]]
        t_tr = K.on("tensor",
                    lambda e, c=c, pT=pT: e.transpose(
                        pT[0:128, 0:32],
                        pos_exp[:, c * 128:(c + 1) * 128],
                        eye32[0:32, 0:32]),
                    waits=w)
        ev2.append(K.on("vector",
                        lambda e, c=c, pT=pT: e.tensor_copy(
                            pos3[:, c, :], pT[0:128, 0:32]),
                        waits=[t_tr]))
    for c in range(TCH):
        K.on("vector",
             lambda e, c=c: e.tensor_tensor_scan(
                 cum3[:, c, :], ones1[:, 0:1].broadcast_to([128, 32]),
                 mask3[:, c, :], 0.0, ALU.mult, ALU.add),
             waits=[ev2[TCH - 1]] if c == 0 else [])
    for k in range(4):
        K.on("vector",
             lambda e, k=k: e.tensor_single_scalar(sel, cum_all,
                                                   float(k + 1),
                                                   ALU.is_equal))
        K.on("vector",
             lambda e: e.tensor_tensor(sel, sel, mask[:], ALU.mult))
        K.on("vector",
             lambda e: e.tensor_tensor(
                 tm3, sel3,
                 eio[:].unsqueeze(1).broadcast_to([128, TCH, 32]),
                 ALU.mult))
        K.on("vector",
             lambda e, k=k: e.reduce_sum(spe3[:, :, k:k + 1], tm3,
                                         axis=AX.X))
        K.on("vector",
             lambda e: e.tensor_tensor(tm3, sel3, pos3, ALU.mult))
        K.on("vector",
             lambda e, k=k: e.reduce_sum(spdf3[:, :, k:k + 1], tm3,
                                         axis=AX.X))
        K.on("vector",
             lambda e: e.tensor_tensor(tm3, sel3, wrt3, ALU.mult))
        K.on("vector",
             lambda e, k=k: e.reduce_sum(spw3[:, :, k:k + 1], tm3,
                                         axis=AX.X))
    K.on("vector",
         lambda e: e.scalar_tensor_tensor(spdf[:], spe[:], float(C),
                                          spdf[:], ALU.mult, ALU.add))
    K.on("vector",
         lambda e: e.tensor_scalar_add(spdf[:], spdf[:], -1.0))
    K.on("vector",
         lambda e: e.scalar_tensor_tensor(tmul[:, 64:128], spe[:],
                                          float(-C), spdf[:],
                                          ALU.mult, ALU.add))
    K.on("vector",
         lambda e: e.tensor_single_scalar(tmul[:, 0:64], tmul[:, 64:128],
                                          float(C - 1), ALU.is_gt))
    K.on("vector",
         lambda e: e.scalar_tensor_tensor(spdf[:], tmul[:, 0:64], 1e9,
                                          spdf[:], ALU.mult, ALU.add))
    t_dst = K.on("vector", lambda e: e.tensor_copy(spdst[:], spdf[:]))

    spdsti = spdst.rearrange("p (c k) -> p c k", k=4)
    idxp = [K.view("rt_r", [128, 8], I32, 0)[:, k:k + 1] for k in range(4)]
    t_sc_last = None
    for c in range(TCH):
        wx = sum((trk.wait(("xgw", k)) for k in range(4)), [])
        if c == 0:
            wx = wx + [t_dst]
        t_x = K.on("sync",
                   lambda e, c=c: e.dma_start(
                       out=xgw[:, 0:1024],
                       in_=xn_dram[c * 128:(c + 1) * 128, :]),
                   waits=wx, dma="xg")
        t_m1 = K.on("vector",
                    lambda e, c=c: e.tensor_copy(xgw_w[:, 0:4],
                                                 spw3[:, c, :]),
                    waits=[t_dst] + wx)
        t_m2 = K.on("vector",
                    lambda e, c=c: e.tensor_copy(xgw_w[:, 4:8],
                                                 spe3[:, c, :]))
        sc_toks = []
        for k in range(4):
            idxt = idxp[k]
            t_cp = K.on("vector",
                        lambda e, c=c, k=k, idxt=idxt: e.tensor_copy(
                            idxt, spdsti[:, c, k:k + 1]),
                        waits=trk.wait(("scidx", k)))
            sc_toks.append(K.on(
                "gpsimd",
                lambda e, idxt=idxt: e.indirect_dma_start(
                    out=gx,
                    out_offset=bass.IndirectOffsetOnAxis(ap=idxt, axis=0),
                    in_=xgw, in_offset=None,
                    bounds_check=K.cache["sp_bc"], oob_is_err=False),
                waits=([t_x, t_m2] if k == 0 else []) + [t_cp],
                dma=f"sc{k}"))
            trk.set(("scidx", k), sc_toks[-1])
            trk.set(("xgw", k), sc_toks[-1])
        t_sc_last = sc_toks
    done = [K.tok("gpsimd", dma=f"sc{k}") for k in range(4)]
    return done, t_dst


def emit_l0_sparse_experts(K: KB, moe: "MoeEmitter", sc_done,
                           pc_toks=None):
    C = 512
    E = 32
    NR = E * C
    psA = K.psum("psA", [128, 2048], F32)
    psB = K.psum("psB", [128, 2048], F32)
    pT16 = [psB[:, 1024:1536].bitcast(BF16), psB[:, 1536:2048].bitcast(BF16)]
    gx = K.dram["gx"]
    y_gx = K.dscratch("y_gx", [NR, 1024], BF16)
    gxin = [K.view("Ea", [128, 1024], BF16, 0),
            K.view("Ea", [128, 1024], BF16, 2048),
            K.view("Eb", [128, 1024], BF16, 0),
            K.view("Eb", [128, 1024], BF16, 2048)]
    gxmeta = [K.view("xn_chunk", [128, 16], BF16, 1216),
              K.view("xn_chunk", [128, 16], BF16, 1248)]
    gxwm = [K.view("xn_chunk", [128, 8], F32, 1216),
            K.view("xn_chunk", [128, 8], F32, 1248)]
    gxTs = [[K.view("v65", [128, 512], BF16, 24576 + kc * 1024)
             for kc in range(8)],
            [K.view("kTt", [128, 512], BF16, (kc % 4) * 1024)
             if kc < 4 else
             (K.view("rrep", [128, 512], BF16, (kc - 4) * 1024)
              if kc < 6 else
              K.view("r4", [128, 512], BF16, (kc - 6) * 1024))
             for kc in range(8)]]
    heT = [K.view("qT", [128, 512], BF16, j * 1024) for j in range(4)]
    tmp = K.view("sq_scr", [128, 2048], BF16, 0)
    ywst = [K.view("y_sb", [128, 1024], BF16, 0),
            K.view("y_sb", [128, 1024], BF16, 2048)]
    gsel = K.view("xn_chunk", [128, 8], F32, 1152)
    gwall = [K.view("xn_chunk", [128, 4], F32, 1184),
             K.view("xn_chunk", [128, 4], F32, 1200)]
    trk = Track()
    x_tok = K.tiles["x_tok"]
    spdsti = K.view("xn_chunk", [128, 64], I32, 128).rearrange(
        "p (c k) -> p c k", k=4)

    bV = K.tok("vector")
    bA = K.tok("scalar")

    def emit_gather_transpose(e_i):
        si = e_i % 2
        gset = gxTs[si]
        wgxT = trk.wait(("gxT", si)) + trk.wait(("gxTA", si)) + \
            trk.wait(("gxTV", si))
        wsel_t = None
        for rc in range(4):
            gpar = (e_i * 4 + rc) % 4
            mpar = rc % 2
            gi = gxin[gpar]
            t_g = K.on("sync",
                       lambda e, e_i=e_i, rc=rc, gi=gi: e.dma_start(
                           out=gi[:, 0:1024],
                           in_=gx[e_i * C + rc * 128:
                                  e_i * C + (rc + 1) * 128, 0:1024]),
                       waits=(sc_done if e_i == 0 and rc < 2 else []) +
                       trk.wait(("gxin", gpar)), dma=f"gt{gpar}")
            t_m = K.on("sync",
                       lambda e, e_i=e_i, rc=rc, mpar=mpar: e.dma_start(
                           out=gxmeta[mpar],
                           in_=gx[e_i * C + rc * 128:
                                  e_i * C + (rc + 1) * 128, 1024:1040]),
                       waits=(sc_done if e_i == 0 and rc < 2 else []) +
                       trk.wait(("gxm", mpar)), dma=f"gxm{mpar}")
            K.on("vector",
                 lambda e, e_i=e_i, mpar=mpar: e.tensor_single_scalar(
                     gsel[:, 0:4], gxwm[mpar][:, 4:8], float(e_i),
                     ALU.is_equal),
                 waits=[t_m] + ([bV] if e_i == 0 and rc == 0 else []))
            K.on("vector",
                 lambda e, mpar=mpar: e.tensor_tensor(
                     gsel[:, 4:8], gsel[:, 0:4], gxwm[mpar][:, 0:4],
                     ALU.mult))
            wsel_t = K.on("vector",
                          lambda e, e_i=e_i, rc=rc: e.reduce_sum(
                              gwall[e_i % 2][:].rearrange(
                                  "p (a b) -> p a b", a=4)[:, rc:rc + 1, :],
                              gsel[:, 4:8].rearrange("p (a e) -> p a e", a=1),
                              axis=AX.X))
            trk.set(("gxm", mpar), wsel_t)
            for kc in range(8):
                s4 = (rc * 8 + kc) % 2
                pT = pT16[s4]
                w = [t_g] + trk.wait(("ptr", s4))
                if rc == 0 and kc == 0:
                    w = w + wgxT + ([bA] if e_i == 0 else [])
                t_tr = K.on("tensor",
                            lambda e, kc=kc, pT=pT, gi=gi: e.transpose(
                                pT[0:128, 0:128],
                                gi[:, kc * 128:(kc + 1) * 128],
                                K.tiles["eye_bf16"][:]),
                            waits=w)
                eng = "scalar" if kc % 2 == 0 else "vector"
                if eng == "scalar":
                    t_ev = K.on(eng,
                                lambda e, rc=rc, kc=kc, pT=pT, gset=gset:
                                e.copy(
                                    gset[kc][:, rc * 128:(rc + 1) * 128],
                                    pT[0:128, 0:128]),
                                waits=[t_tr])
                else:
                    t_ev = K.on(eng,
                                lambda e, rc=rc, kc=kc, pT=pT, gset=gset:
                                e.tensor_copy(
                                    gset[kc][:, rc * 128:(rc + 1) * 128],
                                    pT[0:128, 0:128]),
                                waits=[t_tr])
                trk.set(("ptr", s4), t_ev)
            trk.set(("gxin", gpar), K.tok("tensor"))
        return si, [K.tok("scalar"), K.tok("vector")], wsel_t

    moe.issue_w13(0)
    pending = {0: emit_gather_transpose(0)}
    for e_i in range(E):
        par = e_i % 2
        w1v = moe.w1vs[par]
        w3v = moe.w3vs[par]
        if e_i + 1 < E + 2:
            moe.issue_w13(e_i + 1)
        t_w1, t_w3 = moe.w13_toks[e_i]
        _, _, a2 = moe.expert_weight_aps(e_i)
        t_w2 = K.on("gpsimd",
                    lambda e, s=a2: e.dma_start(out=moe.w2v, in_=s),
                    waits=moe.trk.wait("w2") + ([moe.bT] if e_i == 0 else []),
                    dma="w2")
        si, gxT_done, t_wsel = pending.pop(e_i)
        gset = gxTs[si]
        gw1 = gwall[si]
        if e_i + 1 < E:
            pending[e_i + 1] = emit_gather_transpose(e_i + 1)

        for j in range(4):
            p1 = psA[:, (j % 2) * 512:(1 + j % 2) * 512]
            p2 = psA[:, (2 + j % 2) * 512:(3 + j % 2) * 512]

            def mh1(e, j=j, p1=p1, w1v=w1v, gset=gset):
                i = None
                for kc in range(DCH):
                    i = e.matmul(p1, w1v[:, kc, j * 128:(j + 1) * 128],
                                 gset[kc][:, 0:512],
                                 start=(kc == 0), stop=(kc == DCH - 1))
                return i
            w = list(gxT_done) if j == 0 else []
            w = w + ([t_w1] if j == 0 else []) + trk.wait(("m1", j % 2))
            t_m1 = K.on("tensor", mh1, waits=w)
            t_si = K.on("scalar",
                        lambda e, j=j, p1=p1: e.activation(
                            tmp[:, j * 512:(j + 1) * 512], p1, AF.Silu),
                        waits=[t_m1])
            trk.set(("m1", j % 2), t_si)

            def mh3(e, j=j, p2=p2, w3v=w3v, gset=gset):
                i = None
                for kc in range(DCH):
                    i = e.matmul(p2, w3v[:, kc, j * 128:(j + 1) * 128],
                                 gset[kc][:, 0:512],
                                 start=(kc == 0), stop=(kc == DCH - 1))
                return i
            w = ([t_w3] if j == 0 else []) + trk.wait(("m3", j % 2))
            t_m3 = K.on("tensor", mh3, waits=w)
            t_he = K.on("vector",
                        lambda e, j=j, p2=p2: e.tensor_tensor(
                            heT[j][:, 0:512], p2,
                            tmp[:, j * 512:(j + 1) * 512], ALU.mult),
                        waits=[t_m3, t_si])
            trk.set(("m3", j % 2), t_he)
        he_done = K.tok("vector")
        trk.set(("gxT", si), K.tok("tensor"))
        trk.set(("gxTA", si), K.tok("scalar"))
        trk.set(("gxTV", si), K.tok("vector"))

        for rc in range(4):
            ypar = rc % 2
            yst = ywst[ypar]
            evs = []
            for n in range(2):
                yp2 = (rc * 2 + n) % 2
                pye = psB[:, yp2 * 512:(yp2 + 1) * 512]

                def mye(e, rc=rc, n=n, pye=pye):
                    i = None
                    for j in range(4):
                        i = e.matmul(pye,
                                     heT[j][:, rc * 128:(rc + 1) * 128],
                                     moe.w2v[:, j, n * 512:(n + 1) * 512],
                                     start=(j == 0), stop=(j == 3))
                    return i
                w = [he_done, t_w2] if (rc == 0 and n == 0) else []
                w = list(w) + trk.wait(("yps", yp2))
                t_ye = K.on("tensor", mye, waits=w)
                t_ev = K.on("scalar",
                            lambda e, n=n, pye=pye, yst=yst, gw1=gw1, rc=rc:
                            e.activation(
                                yst[:, n * 512:(n + 1) * 512], pye,
                                AF.Identity, scale=gw1[:, rc:rc + 1]),
                            waits=[t_ye, t_wsel] +
                            (trk.wait(("yst", ypar)) if n == 0 else []))
                trk.set(("yps", yp2), t_ev)
                evs.append(t_ev)
            t_st = K.on("sync",
                        lambda e, e_i=e_i, rc=rc, yst=yst: e.dma_start(
                            out=y_gx[e_i * C + rc * 128:
                                     e_i * C + (rc + 1) * 128, :],
                            in_=yst[:, :]),
                        waits=evs, dma=f"ygx{ypar}")
            trk.set(("yst", ypar), t_st)
        moe.trk.set(("w13", par), K.tok("tensor"))
        moe.trk.set("w2", K.tok("tensor"))
        y_done = [K.tok("sync", dma="ygx0"), K.tok("sync", dma="ygx1")]

    gcmb = [K.view("y_sb", [128, 1024], BF16, 0),
            K.view("y_sb", [128, 1024], BF16, 2048),
            K.view("xn0_scr", [128, 1024], BF16, 0),
            K.view("xn0_scr", [128, 1024], BF16, 2048)]
    idxp = [K.view("rt_r", [128, 8], I32, 0)[:, 4 + k:5 + k]
            for k in range(4)]

    tV = K.tok("vector")
    tT = K.tok("tensor")
    for s in range(4):
        moe.trk.set(("mps", s), tV)
    for ypar in range(2):
        moe.trk.set(("yps", ypar), tV)
    for j in range(4):
        moe.trk.set(("he", j), tT)
    moe.emit_expert(E)
    moe.emit_expert(E + 1)

    for c in range(TCH):
        for k in range(4):
            gpar = (c * 4 + k) % 4
            gb = gcmb[gpar]
            idxt = idxp[gpar]
            t_cp = K.on("vector",
                        lambda e, c=c, k=k, idxt=idxt: e.tensor_copy(
                            idxt, spdsti[:, c, k:k + 1]),
                        waits=trk.wait(("gcidx", gpar)))
            t_g = K.on("gpsimd",
                       lambda e, gb=gb, idxt=idxt: e.indirect_dma_start(
                           out=gb, out_offset=None,
                           in_=y_gx,
                           in_offset=bass.IndirectOffsetOnAxis(
                               ap=idxt, axis=0),
                           bounds_check=K.cache["sp_bc"], oob_is_err=False),
                       waits=(y_done if c == 0 and k < 2 else []) +
                       trk.wait(("gcmb", gpar)) + [t_cp], dma=f"gc{gpar}")
            trk.set(("gcidx", gpar), t_g)
            t_ad = K.on("vector",
                        lambda e, c=c, gb=gb: e.tensor_tensor(
                            x_tok[:, c * D:(c + 1) * D],
                            x_tok[:, c * D:(c + 1) * D], gb, ALU.add),
                        waits=[t_g])
            trk.set(("gcmb", gpar), t_ad)

    return K.tok("vector")



class MoeEmitter:

    def __init__(self, K, n_routed, n_iter, expert_weight_aps, wrt3,
                 t_wrt, router_A, bT, ye_mode="wide", first=True):
        self.K = K
        self.ye_mode = ye_mode
        self.n_routed = n_routed
        self.n_iter = n_iter
        self.expert_weight_aps = expert_weight_aps
        self.wrt3 = wrt3
        self.t_wrt = t_wrt
        self.router_A = router_A
        self.bT = bT
        self.xnT = [K.tiles[f"xnT{k}"] for k in range(DCH)]
        self.x_tok = K.tiles["x_tok"]
        self.psA = K.psum("psA", [128, 2048], F32)
        self.psB = K.psum("psB", [128, 2048], F32)
        self.w1vs = [K.view("v65", [128, 8 * 512], BF16, 0)
                     .rearrange("p (kc o) -> p kc o", o=512),
                     K.view("wbig", [128, 8 * 512], BF16, 0)
                     .rearrange("p (kc o) -> p kc o", o=512)]
        self.w3vs = [K.view("v65", [128, 8 * 512], BF16, 8192)
                     .rearrange("p (kc o) -> p kc o", o=512),
                     K.view("wbig", [128, 8 * 512], BF16, 8192)
                     .rearrange("p (kc o) -> p kc o", o=512)]
        self.w2v = K.view("v65", [128, 4 * 1024], BF16, 16384).rearrange(
            "p (j o) -> p j o", o=1024)
        self.heT = [K.tiles[nm] for nm in ("qT", "kTt", "Ea", "Eb")]
        self.tmp = K.view("sq_scr", [128, 2048], BF16, 0)
        self.trk = Track()
        self.bV = K.tok("vector")
        self.w13_toks = {}
        if first:
            self.issue_w13(0)

    def issue_w13(self, e_i):
        K, trk = self.K, self.trk
        par = e_i % 2
        a1, a3, _ = self.expert_weight_aps(e_i)
        guard = [self.bT, self.bV] if e_i < 2 else trk.wait(("w13", par))
        t1 = K.on("gpsimd",
                  lambda e, s=a1, par=par: e.dma_start(out=self.w1vs[par],
                                                       in_=s),
                  waits=list(guard), dma=f"w1{par}")
        t3 = K.on("gpsimd",
                  lambda e, s=a3, par=par: e.dma_start(out=self.w3vs[par],
                                                       in_=s),
                  dma=f"w3{par}")
        self.w13_toks[e_i] = (t1, t3)

    def emit_expert(self, e_i):
        for _ in self.expert_parts(e_i):
            pass

    def expert_parts(self, e_i):
        K, trk = self.K, self.trk
        psA, psB = self.psA, self.psB
        xnT, heT, tmp, w2v = self.xnT, self.heT, self.tmp, self.w2v
        x_tok = self.x_tok
        par = e_i % 2
        w1v = self.w1vs[par]
        w3v = self.w3vs[par]
        if e_i + 1 < self.n_iter:
            self.issue_w13(e_i + 1)
        t_w1, t_w3 = self.w13_toks[e_i]
        _, _, a2 = self.expert_weight_aps(e_i)
        t_w2 = K.on("gpsimd",
                    lambda e, s=a2: e.dma_start(out=self.w2v, in_=s),
                    waits=trk.wait("w2") + ([self.bT] if e_i == 0 else []),
                    dma="w2")

        for j in range(4):
            for n in range(4):
                s1 = n % 2
                s2 = 2 + n % 2
                p1 = psA[:, s1 * 512:(s1 + 1) * 512]
                p2 = psA[:, s2 * 512:(s2 + 1) * 512]

                def mh1(e, j=j, n=n, p1=p1, w1v=w1v):
                    i = None
                    for kc in range(DCH):
                        i = e.matmul(p1, w1v[:, kc, j * 128:(j + 1) * 128],
                                     xnT[kc][:, n * 512:(n + 1) * 512],
                                     start=(kc == 0), stop=(kc == DCH - 1))
                    return i
                w = [t_w1] if (j == 0 and n == 0) else []
                if e_i == 0 and j == 0 and n == 0 and \
                        self.router_A is not None:
                    w = list(w) + [self.router_A]
                w = list(w) + trk.wait(("mps", s1))
                t_m1 = K.on("tensor", mh1, waits=w)
                t_si = K.on("scalar",
                            lambda e, n=n, p1=p1: e.activation(
                                tmp[:, n * 512:(n + 1) * 512], p1, AF.Silu),
                            waits=[t_m1])
                trk.set(("mps", s1), t_si)

                def mh3(e, j=j, n=n, p2=p2, w3v=w3v):
                    i = None
                    for kc in range(DCH):
                        i = e.matmul(p2, w3v[:, kc, j * 128:(j + 1) * 128],
                                     xnT[kc][:, n * 512:(n + 1) * 512],
                                     start=(kc == 0), stop=(kc == DCH - 1))
                    return i
                w = [t_w3] if (j == 0 and n == 0) else []
                w = list(w) + trk.wait(("mps", s2))
                t_m3 = K.on("tensor", mh3, waits=w)
                t_he = K.on("vector",
                            lambda e, j=j, n=n, p2=p2: e.tensor_tensor(
                                heT[j][:, n * 512:(n + 1) * 512], p2,
                                tmp[:, n * 512:(n + 1) * 512], ALU.mult),
                            waits=[t_m3, t_si] +
                            (trk.wait(("he", j)) if n == 0 else []))
                trk.set(("mps", s2), t_he)
            yield
        he_done = K.tok("vector")

        if self.ye_mode == "wide":
            for c in range(TCH):
                ypar = c % 2
                pye = psB[:, ypar * 1024:(ypar + 1) * 1024]

                def mye(e, c=c, pye=pye):
                    i = None
                    for n in range(2):
                        for j in range(4):
                            i = e.matmul(pye[:, n * 512:(n + 1) * 512],
                                         heT[j][:, c * 128:(c + 1) * 128],
                                         w2v[:, j, n * 512:(n + 1) * 512],
                                         start=(j == 0), stop=(j == 3))
                    return i
                w = [he_done, t_w2] if c == 0 else []
                w = list(w) + trk.wait(("yps", ypar))
                t_ye = K.on("tensor", mye, waits=w)
                if e_i < self.n_routed:
                    def acc(e, c=c, e_i=e_i, pye=pye):
                        return e.scalar_tensor_tensor(
                            x_tok[:, c * D:(c + 1) * D], pye,
                            self.wrt3[:, c, e_i:e_i + 1],
                            x_tok[:, c * D:(c + 1) * D], ALU.mult, ALU.add)
                else:
                    def acc(e, c=c, pye=pye):
                        return e.scalar_tensor_tensor(
                            x_tok[:, c * D:(c + 1) * D], pye, 1.0,
                            x_tok[:, c * D:(c + 1) * D], ALU.mult, ALU.add)
                wacc = [t_ye] + ([self.t_wrt]
                                 if (e_i == 0 and c == 0) else [])
                t_acc = K.on("vector", acc, waits=wacc)
                trk.set(("yps", ypar), t_acc)
                if c % 4 == 3:
                    yield
        else:
            k = 0
            for c in range(TCH):
                for n in range(2):
                    ypar = k % 3
                    k += 1
                    pye = psB[:, ypar * 512:(ypar + 1) * 512]

                    def mye(e, c=c, n=n, pye=pye):
                        i = None
                        for j in range(4):
                            i = e.matmul(pye,
                                         heT[j][:, c * 128:(c + 1) * 128],
                                         w2v[:, j, n * 512:(n + 1) * 512],
                                         start=(j == 0), stop=(j == 3))
                        return i
                    w = [he_done, t_w2] if (c == 0 and n == 0) else []
                    w = list(w) + trk.wait(("yps", ypar))
                    t_ye = K.on("tensor", mye, waits=w)
                    sl = slice(c * D + n * 512, c * D + (n + 1) * 512)
                    if e_i < self.n_routed:
                        def acc(e, sl=sl, c=c, e_i=e_i, pye=pye):
                            return e.scalar_tensor_tensor(
                                x_tok[:, sl], pye,
                                self.wrt3[:, c, e_i:e_i + 1],
                                x_tok[:, sl], ALU.mult, ALU.add)
                    else:
                        def acc(e, sl=sl, pye=pye):
                            return e.scalar_tensor_tensor(
                                x_tok[:, sl], pye, 1.0,
                                x_tok[:, sl], ALU.mult, ALU.add)
                    wacc = [t_ye] + ([self.t_wrt]
                                     if (e_i == 0 and c == 0 and n == 0)
                                     else [])
                    t_acc = K.on("vector", acc, waits=wacc)
                    trk.set(("yps", ypar), t_acc)
                if c % 4 == 3:
                    yield
        trk.set(("w13", par), K.tok("tensor"))
        trk.set("w2", K.tok("tensor"))
        for j in range(4):
            trk.set(("he", j), K.tok("tensor"))


def emit_moe_experts(K: KB, n_routed, n_iter, expert_weight_aps, wrt3,
                     t_wrt, router_A, bT):
    me = MoeEmitter(K, n_routed, n_iter, expert_weight_aps, wrt3,
                    t_wrt, router_A, bT)
    for e_i in range(n_iter):
        me.emit_expert(e_i)
    return K.tok("vector")



def emit_l0_moe(K: KB, xnT_ready):
    E = 32
    t_wrt, router_A = emit_l0_router(K, xnT_ready)
    wrt3 = K.tiles["wrt"][:].rearrange("p (c e) -> p c e", e=E)
    w1d, w3d, w2d = K.dram["w1e"], K.dram["w3e"], K.dram["w2e"]
    ws1, ws3, ws2 = K.dram["ws1"], K.dram["ws3"], K.dram["ws2"]

    def expert_weight_aps(e):
        if e < E:
            return (w1d[e].rearrange("(kc p) o -> p kc o", p=128),
                    w3d[e].rearrange("(kc p) o -> p kc o", p=128),
                    w2d[e].rearrange("(j p) o -> p j o", p=128))
        half = e - E
        sl = slice(half * 512, (half + 1) * 512)
        return (ws1[:, sl].rearrange("(kc p) o -> p kc o", p=128),
                ws3[:, sl].rearrange("(kc p) o -> p kc o", p=128),
                ws2[sl, :].rearrange("(j p) o -> p j o", p=128))

    bT = K.tok("tensor")
    return emit_moe_experts(K, E, E + 2, expert_weight_aps, wrt3,
                            t_wrt, router_A, bT)



def emit_l1_attn(K: KB, xnT_ready):
    xnT = [K.tiles[f"xnT{k}"] for k in range(DCH)]
    x_tok = K.tiles["x_tok"]
    psA = K.psum("psA", [128, 2048], F32)
    psB = K.psum("psB", [128, 2048], F32)

    H1 = [K.view("v65", [128, T], BF16, i * 4096) for i in range(DCH)]
    krT = K.view("qT", [128, 1024], F32, 0)
    vT = K.view("kTt", [128, 1024], F32, 0)
    sth = K.view("Ea", [128, 512], F32, 0)
    cth = K.view("Ea", [128, 512], F32, 2048)
    sfh = K.view("Eb", [128, 512], F32, 0)
    cfh = K.view("Eb", [128, 512], F32, 2048)
    s1 = K.view("xn_chunk", [128, 512], F32, 0)
    i32 = K.view("ctx_rd1", [128, 512], I32, 0)
    mem_re = K.view("sq_scr", [128, 512], F32, 0)
    mem_im = K.view("sq_scr", [128, 512], F32, 2048)
    acc = K.view("y_sb", [128, 512], F32, 0)
    ones512 = K.view("rrep", [128, 512], F32, 0)
    wqp = K.tiles["wqp"]
    wkp = K.tiles["wkp"]
    wbig = K.tiles["wbig"]
    wbigv = wbig[:].rearrange("p (kc o) -> p kc o", o=1024)
    iota512 = K.view("wrt", [128, 512], F32, 0)
    cry = K.sb("cry", [128, 8], F32)
    fcol = K.tiles["t_c_freqs_col"]
    g0c = K.tiles["t_c_g0_col"]
    g1c = K.tiles["t_c_g1_col"]
    rsq = K.tiles["t_c_rsq_rep"]
    f2pi = K.sb("f2pi_col", [128, DCH], F32)
    tcol = K.tiles["muy"]
    tcol2 = K.tiles["ssqy"]
    trk = Track()
    R10 = float(10.0 / TWO_PI)

    bTT = K.tok("tensor")
    t_ctxst = K.tok("sync", dma="ctxst")

    t_iota = K.on("sync",
                  lambda e: e.dma_start(out=iota512,
                                        in_=K.dram["c_iota512"]),
                  waits=[K.tok("vector")], dma="eye")
    t_bqk = K.tok("sync", dma="bqk")
    K.on("vector", lambda e: e.memset(ones512, 1.0),
         waits=[t_ctxst, t_iota])
    t_f2pi = K.on("vector",
                  lambda e: e.tensor_scalar_mul(f2pi[:], fcol[:],
                                                1.0 / TWO_PI),
                  waits=[t_bqk, t_iota])

    wo1 = K.dram["wo1"]
    t_wo = K.on("gpsimd",
                lambda e: e.dma_start(
                    out=wbigv, in_=wo1.rearrange("(kc p) o -> p kc o", p=128)),
                waits=[bTT], dma="wbig")

    wk1 = K.dram["wk1"]
    wv1 = K.dram["wv1"]
    first_h1 = True
    for c2 in range(DCH):
        t_lk = K.on("gpsimd",
                    lambda e, c2=c2: e.dma_start(
                        out=wqp[:].rearrange("p (kc o) -> p kc o", o=128),
                        in_=wk1[:, c2 * 128:(c2 + 1) * 128].rearrange(
                            "(kc q) o -> q kc o", q=128)),
                    waits=trk.wait("wqp"), dma="wk1")
        t_lv = K.on("gpsimd",
                    lambda e, c2=c2: e.dma_start(
                        out=wkp[:].rearrange("p (kc o) -> p kc o", o=128),
                        in_=wv1[:, c2 * 128:(c2 + 1) * 128].rearrange(
                            "(kc q) o -> q kc o", q=128)),
                    waits=trk.wait("wkp"), dma="wv1")
        t_cry = K.on("vector", lambda e: e.memset(cry[:, 0:6], 0.0))
        if c2 == 0:
            prev_cfh = [None]
            prev_pos = [None]
            prev_as = [None]

        for half in range(2):
            ev_k = ev_v = None
            for n2 in range(2):
                nt = half * 2 + n2
                pk = psA[:, n2 * 512:(n2 + 1) * 512]
                pv = psA[:, (2 + n2) * 512:(3 + n2) * 512]

                def mk(e, nt=nt, pk=pk):
                    i = None
                    wv3 = wqp[:].rearrange("p (kc o) -> p kc o", o=128)
                    for kc in range(DCH):
                        i = e.matmul(pk, wv3[:, kc, :],
                                     xnT[kc][:, nt * 512:(nt + 1) * 512],
                                     start=(kc == 0), stop=(kc == DCH - 1))
                    return i
                w = [t_lk] if (half == 0 and n2 == 0) else []
                if half == 0 and n2 == 0:
                    w = list(w) + [xnT_ready]
                w = list(w) + trk.wait(("kps", n2))
                t_mk = K.on("tensor", mk, waits=w)
                ev_k = K.on("scalar",
                            lambda e, n2=n2, pk=pk: e.activation(
                                krT[:, n2 * 512:(n2 + 1) * 512], pk,
                                AF.Identity),
                            waits=[t_mk] +
                            ([K.tok("vector")] if n2 == 0 else []))
                trk.set(("kps", n2), ev_k)

                def mv(e, nt=nt, pv=pv):
                    i = None
                    wv3 = wkp[:].rearrange("p (kc o) -> p kc o", o=128)
                    for kc in range(DCH):
                        i = e.matmul(pv, wv3[:, kc, :],
                                     xnT[kc][:, nt * 512:(nt + 1) * 512],
                                     start=(kc == 0), stop=(kc == DCH - 1))
                    return i
                w = ([t_lv] if (half == 0 and n2 == 0) else []) + \
                    trk.wait(("vps", n2))
                t_mv = K.on("tensor", mv, waits=w)
                ev_v = K.on("scalar",
                            lambda e, n2=n2, pv=pv: e.activation(
                                vT[:, n2 * 512:(n2 + 1) * 512], pv,
                                AF.Identity),
                            waits=[t_mv])
                trk.set(("vps", n2), ev_v)

            for tq in range(2):
                q = half * 2 + tq
                t0 = q * 512
                kq = krT[:, tq * 512:(tq + 1) * 512]
                vq = vT[:, tq * 512:(tq + 1) * 512]
                fc = f2pi[:, c2:c2 + 1]

                K.on("gpsimd",
                     lambda e, fc=fc, t0=t0: e.tensor_scalar_mul(
                         tcol[:], fc, float(t0)),
                     waits=([t_cry] if q == 0 else []) + [t_f2pi] +
                     ([prev_cfh[0]] if prev_cfh[0] is not None else []))
                K.on("gpsimd",
                     lambda e, fc=fc: e.tensor_scalar(
                         s1, iota512, fc, tcol[:, 0:1],
                         ALU.mult, ALU.add))
                K.on("gpsimd", lambda e: e.tensor_copy(i32, s1))
                K.on("gpsimd",
                     lambda e: e.tensor_tensor(s1, s1, i32, ALU.subtract))
                t_sth = K.on("scalar",
                             lambda e: e.activation(sth, s1, AF.Sin,
                                                    scale=TWO_PI),
                             waits=[K.tok("gpsimd")] +
                             ([prev_pos[0]] if prev_pos[0] else []))
                K.on("gpsimd",
                     lambda e: e.tensor_scalar_add(tcol2[:], tcol[:], 0.25),
                     waits=[t_sth])
                K.on("gpsimd",
                     lambda e, fc=fc: e.tensor_scalar(
                         s1, iota512, fc, tcol2[:, 0:1],
                         ALU.mult, ALU.add))
                K.on("gpsimd", lambda e: e.tensor_copy(i32, s1))
                K.on("gpsimd",
                     lambda e: e.tensor_tensor(s1, s1, i32, ALU.subtract))
                t_cth = K.on("scalar",
                             lambda e: e.activation(cth, s1, AF.Sin,
                                                    scale=TWO_PI),
                             waits=[K.tok("gpsimd")])

                K.on("vector",
                     lambda e, vq=vq: e.tensor_tensor(mem_re, vq, cth,
                                                      ALU.mult),
                     waits=[t_cth, ev_v])
                K.on("vector",
                     lambda e, vq=vq: e.tensor_tensor(mem_im, vq, sth,
                                                      ALU.mult))
                K.on("vector",
                     lambda e: e.tensor_tensor_scan(
                         mem_re, ones512, mem_re, cry[:, 0:1],
                         ALU.mult, ALU.add))
                K.on("vector",
                     lambda e: e.tensor_copy(cry[:, 0:1], mem_re[:, 511:512]))
                K.on("vector",
                     lambda e: e.tensor_tensor_scan(
                         mem_im, ones512, mem_im, cry[:, 1:2],
                         ALU.mult, ALU.add))
                K.on("vector",
                     lambda e: e.tensor_copy(cry[:, 1:2], mem_im[:, 511:512]))
                K.on("vector",
                     lambda e: e.tensor_tensor(mem_re, mem_re, cth, ALU.mult))
                K.on("vector",
                     lambda e: e.tensor_tensor(mem_im, mem_im, sth, ALU.mult))
                K.on("vector",
                     lambda e: e.tensor_tensor(mem_re, mem_re, mem_im,
                                               ALU.add))
                K.on("vector",
                     lambda e, c2=c2: e.tensor_scalar_mul(
                         acc, mem_re, g0c[:, c2:c2 + 1]))
                prev_pos[0] = K.tok("vector")

                K.on("gpsimd",
                     lambda e, kq=kq: e.tensor_scalar_mul(s1, kq, R10),
                     waits=[ev_k, t_cth])
                K.on("gpsimd", lambda e: e.tensor_copy(i32, s1))
                K.on("gpsimd",
                     lambda e: e.tensor_tensor(s1, s1, i32, ALU.subtract))
                t_sfh = K.on("scalar",
                             lambda e: e.activation(sfh, s1, AF.Sin,
                                                    scale=TWO_PI),
                             waits=[K.tok("gpsimd")] +
                             ([prev_as[0]] if prev_as[0] else []))
                K.on("gpsimd",
                     lambda e, kq=kq: e.tensor_scalar(
                         s1, kq, R10, 0.25, ALU.mult, ALU.add),
                     waits=[t_sfh])
                K.on("gpsimd", lambda e: e.tensor_copy(i32, s1))
                K.on("gpsimd",
                     lambda e: e.tensor_tensor(s1, s1, i32, ALU.subtract))
                t_cfh = K.on("scalar",
                             lambda e: e.activation(cfh, s1, AF.Sin,
                                                    scale=TWO_PI),
                             waits=[K.tok("gpsimd")])
                prev_cfh[0] = t_cfh

                K.on("vector",
                     lambda e, vq=vq: e.tensor_tensor(
                         mem_re[0:128, 1:512], vq[0:128, 1:512],
                         cfh[0:128, 0:511], ALU.mult),
                     waits=[t_cfh])
                if q == 0:
                    K.on("vector",
                         lambda e: e.memset(mem_re[0:128, 0:1], 0.0))
                else:
                    K.on("vector",
                         lambda e, vq=vq: e.tensor_tensor(
                             mem_re[0:128, 0:1], vq[0:128, 0:1],
                             cry[:, 4:5], ALU.mult))
                K.on("vector",
                     lambda e, vq=vq: e.tensor_tensor(
                         mem_im[0:128, 1:512], vq[0:128, 1:512],
                         sfh[0:128, 0:511], ALU.mult))
                if q == 0:
                    K.on("vector",
                         lambda e: e.memset(mem_im[0:128, 0:1], 0.0))
                else:
                    K.on("vector",
                         lambda e, vq=vq: e.tensor_tensor(
                             mem_im[0:128, 0:1], vq[0:128, 0:1],
                             cry[:, 5:6], ALU.mult))
                K.on("vector",
                     lambda e: e.tensor_copy(cry[:, 4:5], cfh[:, 511:512]))
                K.on("vector",
                     lambda e: e.tensor_copy(cry[:, 5:6], sfh[:, 511:512]))
                K.on("vector",
                     lambda e: e.tensor_tensor_scan(
                         mem_re, ones512, mem_re, cry[:, 2:3],
                         ALU.mult, ALU.add))
                K.on("vector",
                     lambda e: e.tensor_copy(cry[:, 2:3], mem_re[:, 511:512]))
                K.on("vector",
                     lambda e: e.tensor_tensor_scan(
                         mem_im, ones512, mem_im, cry[:, 3:4],
                         ALU.mult, ALU.add))
                K.on("vector",
                     lambda e: e.tensor_copy(cry[:, 3:4], mem_im[:, 511:512]))
                K.on("vector",
                     lambda e: e.tensor_tensor(mem_re, mem_re, cfh, ALU.mult))
                K.on("vector",
                     lambda e: e.tensor_tensor(mem_im, mem_im, sfh, ALU.mult))
                K.on("vector",
                     lambda e: e.tensor_tensor(mem_re, mem_re, mem_im,
                                               ALU.add))
                K.on("vector",
                     lambda e, c2=c2: e.scalar_tensor_tensor(
                         acc, mem_re, g1c[:, c2:c2 + 1], acc,
                         ALU.mult, ALU.add))
                prev_as[0] = K.tok("vector")
                wq_ = [bTT] if first_h1 else []
                first_h1 = False
                K.on("vector",
                     lambda e, c2=c2, t0=t0: e.tensor_tensor(
                         H1[c2][:, t0:t0 + 512], acc,
                         rsq[:, t0:t0 + 512], ALU.mult),
                     waits=wq_)
        trk.set("wqp", K.tok("vector"))
        trk.set("wkp", K.tok("vector"))
    h1_done = K.tok("vector")

    for c in range(TCH):
        par = c % 2
        pao = psB[:, par * 1024:(par + 1) * 1024]

        def mm(e, c=c, pao=pao):
            i = None
            for n in range(2):
                for kc in range(DCH):
                    i = e.matmul(pao[:, n * 512:(n + 1) * 512],
                                 H1[kc][:, c * 128:(c + 1) * 128],
                                 wbigv[:, kc, n * 512:(n + 1) * 512],
                                 start=(kc == 0), stop=(kc == DCH - 1))
            return i
        w = ([h1_done, t_wo] if c == 0 else []) + trk.wait(("wops", par))
        t_mm = K.on("tensor", mm, waits=w)
        t_acc = K.on("vector",
                     lambda e, c=c, pao=pao: e.scalar_tensor_tensor(
                         x_tok[:, c * D:(c + 1) * D], pao, 1.0,
                         x_tok[:, c * D:(c + 1) * D], ALU.mult, ALU.add),
                     waits=[t_mm])
        trk.set(("wops", par), t_acc)
    return K.tok("vector")


def emit_l1_router(K: KB, xnT_ready):
    E = 16
    G = 4
    xnT = [K.tiles[f"xnT{k}"] for k in range(DCH)]
    psA = K.psum("psA", [128, 2048], F32)
    s_sig = K.view("ctx_rd0", [128, 256], F32, 0)
    sc = K.view("ctx_rd0", [128, 256], F32, 1024)
    msc = K.view("r4", [128, 256], F32, 0)
    gsc = K.view("r4", [128, 256], F32, 1024)
    wrt = K.sb("wrt", [128, 512], F32)
    ge_bf = K.sb("ge_bf", [128, 512], BF16)
    rt_r = K.sb("rt_r", [128, TCH], F32)
    rt_r3 = rt_r[:].rearrange("p (c o) -> p c o", o=1)
    G1 = K.view("wbig", [128, 4096], BF16, 0)
    G2 = K.view("wbig", [128, 4096], BF16, 8192)
    rnk = K.view("ctx_rd1", [128, 256], F32, 0)
    gs64 = K.view("ctx_rd1", [128, 64], F32, 1024)
    gm64 = K.view("ctx_rd1", [128, 64], F32, 1280)
    tril = K.tiles["t_c_tril16"]
    ebr = K.tiles["t_c_ebias_rep"]
    trk = Track()
    bA = K.tok("scalar")
    bV = K.tok("vector")
    bT = K.tok("tensor")

    rwt = K.dram["rwt"]
    wg_sb = K.sb("wrt_wg", [128, DCH * 32], BF16)
    wgv = wg_sb[:].rearrange("p (kc o) -> p kc o", o=32)
    t_wg = K.on("gpsimd",
                lambda e: e.dma_start(
                    out=wgv[:, :, 0:16],
                    in_=rwt.rearrange("(kc p) o -> p kc o", p=128)),
                waits=[bT], dma="wg")
    for c in range(TCH):
        slot = c % 4
        pv = psA[:, slot * 512:slot * 512 + 16]

        def mm(e, c=c, pv=pv):
            i = None
            for kc in range(DCH):
                i = e.matmul(pv, xnT[kc][:, c * 128:(c + 1) * 128],
                             wgv[:, kc, 0:16],
                             start=(kc == 0), stop=(kc == DCH - 1))
            return i
        w = [t_wg, xnT_ready, bA, bV, bT] if c == 0 else []
        w = list(w) + trk.wait(("rps", slot))
        t_mm = K.on("tensor", mm, waits=w)
        t_sg = K.on("scalar",
                    lambda e, c=c, pv=pv: e.activation(
                        s_sig[:, c * E:(c + 1) * E], pv, AF.Sigmoid),
                    waits=[t_mm] + ([bV] if c == 0 else []))
        trk.set(("rps", slot), t_sg)
    t_sig = K.tok("scalar")
    t_sc = K.on("vector",
                lambda e: e.tensor_tensor(
                    sc.rearrange("p (c e) -> p c e", e=E),
                    s_sig.rearrange("p (c e) -> p c e", e=E),
                    ebr[:, 0:E].rearrange("p (o e) -> p o e", o=1)
                    .broadcast_to([128, TCH, E]), ALU.add),
                waits=[t_sig, bV, bT])
    sc4 = sc.rearrange("p (c g i) -> p (c g) i", g=G, i=4)
    gsc4 = gsc.rearrange("p (c g i) -> p (c g) i", g=G, i=4)
    gs3 = gs64.rearrange("p (x o) -> p x o", o=1)
    K.on("vector", lambda e: e.tensor_copy(gsc, sc))
    K.on("vector", lambda e: e.reduce_max(gs3, gsc4, axis=AX.X))
    K.on("vector",
         lambda e: e.tensor_tensor(
             ge_bf[:, 0:256].rearrange("p (x i) -> p x i", i=4), gsc4,
             gs3.broadcast_to([128, 64, 4]), ALU.is_ge))
    K.on("vector",
         lambda e: e.scalar_tensor_tensor(
             gsc, ge_bf[:, 0:256], -1e30, gsc, ALU.mult, ALU.add))
    K.on("vector", lambda e: e.reduce_max(
        gm64.rearrange("p (x o) -> p x o", o=1), gsc4, axis=AX.X))
    K.on("vector", lambda e: e.tensor_tensor(gs64, gs64, gm64, ALU.add))
    gsg = gs64.rearrange("p (c g) -> p c g", g=G)
    K.on("vector", lambda e: e.tensor_copy(gm64, gs64))
    gmg = gm64.rearrange("p (c g) -> p c g", g=G)
    K.on("vector", lambda e: e.reduce_max(rt_r3, gmg, axis=AX.X))
    K.on("vector",
         lambda e: e.tensor_tensor(
             ge_bf[:, 0:64].rearrange("p (c g) -> p c g", g=G), gmg,
             rt_r3.broadcast_to([128, TCH, G]), ALU.is_ge))
    K.on("vector",
         lambda e: e.scalar_tensor_tensor(
             gm64, ge_bf[:, 0:64], -1e30, gm64, ALU.mult, ALU.add))
    K.on("vector", lambda e: e.reduce_max(rt_r3, gmg, axis=AX.X))
    K.on("vector",
         lambda e: e.tensor_tensor(
             gm64.rearrange("p (c g) -> p c g", g=G), gsg,
             rt_r3.broadcast_to([128, TCH, G]), ALU.is_ge))
    K.on("vector",
         lambda e: e.tensor_tensor(
             msc.rearrange("p (c g i) -> p c g i", g=G, i=4),
             sc.rearrange("p (c g i) -> p c g i", g=G, i=4),
             gm64.rearrange("p (c g) -> p c g", g=G).unsqueeze(3)
             .broadcast_to([128, TCH, G, 4]),
             ALU.mult))
    mA = msc.rearrange("p (c e) -> p c e", e=E).unsqueeze(2).broadcast_to(
        [128, TCH, E, E])
    mB = msc.rearrange("p (c e) -> p c e", e=E).unsqueeze(3).broadcast_to(
        [128, TCH, E, E])
    g1v = G1.rearrange("p (c e f) -> p c e f", e=E, f=E)
    g2v = G2.rearrange("p (c e f) -> p c e f", e=E, f=E)
    K.on("vector", lambda e: e.tensor_tensor(g1v, mA, mB, ALU.is_gt))
    K.on("vector", lambda e: e.tensor_tensor(g2v, mA, mB, ALU.is_equal))
    K.on("vector",
         lambda e: e.tensor_tensor(
             g2v, g2v,
             tril[:, 0:256].rearrange("p (o e f) -> p o e f", e=E, f=E)
             .broadcast_to([128, TCH, E, E]), ALU.mult))
    K.on("vector", lambda e: e.tensor_tensor(g1v, g1v, g2v, ALU.add))
    K.on("vector",
         lambda e: e.reduce_sum(
             rnk.rearrange("p (c e) -> p c e", e=E).unsqueeze(3),
             g1v, axis=AX.X))
    K.on("vector",
         lambda e: e.tensor_single_scalar(rnk, rnk, 3.5, ALU.is_le))
    K.on("vector",
         lambda e: e.tensor_tensor(msc, s_sig, rnk, ALU.mult))
    K.on("vector",
         lambda e: e.reduce_sum(
             rt_r3, msc.rearrange("p (c e) -> p c e", e=E), axis=AX.X))
    K.on("vector",
         lambda e: e.tensor_scalar_add(rt_r[:], rt_r[:], 1e-20))
    K.on("vector", lambda e: e.reciprocal(rt_r[:], rt_r[:]))
    t_w = None
    for c in range(TCH):
        t_w = K.on("vector",
                   lambda e, c=c: e.tensor_scalar(
                       wrt[:, c * E:(c + 1) * E], msc[:, c * E:(c + 1) * E],
                       rt_r[:, c:c + 1], 2.5, ALU.mult, ALU.mult))
    router_A = K.tok("scalar")
    return t_w, router_A


def emit_l1_moe(K: KB, xnT_ready):
    E = 16
    t_wrt, router_A = emit_l1_router(K, xnT_ready)
    wrt3 = K.tiles["wrt"][:].rearrange("p (c e) -> p c e", e=E)
    gwe, uwe, dwe = K.dram["gwe"], K.dram["uwe"], K.dram["dwe"]
    sg, su, sd = K.dram["sg"], K.dram["su"], K.dram["sd"]

    def expert_weight_aps(e):
        if e < E:
            return (gwe[e].rearrange("(kc p) o -> p kc o", p=128),
                    uwe[e].rearrange("(kc p) o -> p kc o", p=128),
                    dwe[e].rearrange("(j p) o -> p j o", p=128))
        half = e - E
        sl = slice(half * 512, (half + 1) * 512)
        return (sg[:, sl].rearrange("(kc p) o -> p kc o", p=128),
                su[:, sl].rearrange("(kc p) o -> p kc o", p=128),
                sd[sl, :].rearrange("(j p) o -> p j o", p=128))

    bT = K.tok("tensor")
    return emit_moe_experts(K, E, E + 2, expert_weight_aps, wrt3,
                            t_wrt, router_A, bT)


class L1AttnEmitter:

    def __init__(self, K: KB, xnT_ready, t_router_V):
        self.K = K
        self.xnT = [K.tiles[f"xnT{k}"] for k in range(DCH)]
        self.x_tok = K.tiles["x_tok"]
        self.psB = K.psum("psB", [128, 2048], F32)
        self.pk = self.psB[:, 1536:2048]
        self.pv = self.psB[:, 1536:2048]
        self.krT = K.view("ctx_rd0", [128, 512], F32, 0)
        self.vT = K.view("r4", [128, 512], F32, 0)
        self.sth = K.view("xn0_scr", [128, 512], F32, 0)
        self.cth = K.view("xn0_scr", [128, 512], F32, 2048)
        self.sfh = K.view("y_sb", [128, 512], F32, 0)
        self.cfh = K.view("y_sb", [128, 512], F32, 2048)
        self.s1 = K.view("rrep", [128, 512], F32, 0)
        self.i32 = K.view("ctx_rd1", [128, 512], I32, 0)
        self.iota = K.view("xn_chunk", [128, 512], F32, 0)
        self.mem_re = K.view("v65", [128, 512], F32, 24576)
        self.mem_im = K.view("v65", [128, 512], F32, 26624)
        self.acc = K.view("v65", [128, 512], F32, 28672)
        self.h1stg = K.view("ge_bf", [128, 512], BF16, 0)
        self.wqp = K.tiles["wqp"]
        self.wkp = K.tiles["wkp"]
        self.cry = K.sb("cry", [128, 8], F32)
        self.ones1 = K.sb("ones1", [128, 1], F32)
        self.fcol = K.tiles["t_c_freqs_col"]
        self.g0c = K.tiles["t_c_g0_col"]
        self.g1c = K.tiles["t_c_g1_col"]
        self.rsq = K.tiles["t_c_rsq_rep"]
        self.f2pi = K.sb("f2pi_col", [128, DCH], F32)
        self.tcol = K.tiles["muy"]
        self.tcol2 = K.tiles["ssqy"]
        self.h1_scr = K.dscratch("h1_scr", [DCH, 128, T], BF16)
        self.trk = Track()
        self.prev = {}
        bV = K.tok("vector")
        bT = K.tok("tensor")
        t_iota = K.on("sync",
                      lambda e: e.dma_start(out=self.iota,
                                            in_=K.dram["c_iota512"]),
                      waits=[bV, bT], dma="eye")
        t_bqk = K.tok("sync", dma="bqk")
        K.on("vector", lambda e: e.memset(self.ones1[:], 1.0),
             waits=[t_router_V])
        self.t_f2pi = K.on("vector",
                           lambda e: e.tensor_scalar_mul(
                               self.f2pi[:], self.fcol[:], 1.0 / TWO_PI),
                           waits=[t_bqk, t_iota])
        self.t_router_V = t_router_V

    def emit_chunk_start(self, c2):
        K, trk = self.K, self.trk
        wk1 = K.dram["wk1"]
        wv1 = K.dram["wv1"]

        t_lk = K.on("gpsimd",
                    lambda e, c2=c2: e.dma_start(
                        out=self.wqp[:].rearrange("p (kc o) -> p kc o",
                                                  o=128),
                        in_=wk1[:, c2 * 128:(c2 + 1) * 128].rearrange(
                            "(kc q) o -> q kc o", q=128)),
                    waits=trk.wait("wqp"), dma="wk1")
        t_lv = K.on("gpsimd",
                    lambda e, c2=c2: e.dma_start(
                        out=self.wkp[:].rearrange("p (kc o) -> p kc o",
                                                  o=128),
                        in_=wv1[:, c2 * 128:(c2 + 1) * 128].rearrange(
                            "(kc q) o -> q kc o", q=128)),
                    waits=trk.wait("wkp"), dma="wv1")
        t_cry = K.on("vector", lambda e: e.memset(self.cry[:, 0:6], 0.0))
        self._chunk_state = (t_lk, t_lv, t_cry)

    def emit_tile(self, c2, q):
        K, trk = self.K, self.trk
        t_lk, t_lv, t_cry = self._chunk_state
        krT, vT = self.krT, self.vT
        sth, cth, sfh, cfh = self.sth, self.cth, self.sfh, self.cfh
        s1, i32, iota = self.s1, self.i32, self.iota
        mem_re, mem_im, acc = self.mem_re, self.mem_im, self.acc
        cry = self.cry
        R10 = float(10.0 / TWO_PI)
        xnT = self.xnT
        if True:
            t0 = q * 512
            nt = q

            def mk(e, nt=nt):
                i = None
                wv3 = self.wqp[:].rearrange("p (kc o) -> p kc o", o=128)
                for kc in range(DCH):
                    i = e.matmul(self.pk, wv3[:, kc, :],
                                 xnT[kc][:, nt * 512:(nt + 1) * 512],
                                 start=(kc == 0), stop=(kc == DCH - 1))
                return i
            w = [t_lk] + trk.wait("pk")
            t_mk = K.on("tensor", mk, waits=w)
            ev_k = K.on("scalar",
                        lambda e: e.activation(krT, self.pk, AF.Identity),
                        waits=[t_mk] + trk.wait("krT") +
                        ([self.t_router_V] if c2 == 0 and q == 0 else []))
            trk.set("pk", ev_k)

            def mv(e, nt=nt):
                i = None
                wv3 = self.wkp[:].rearrange("p (kc o) -> p kc o", o=128)
                for kc in range(DCH):
                    i = e.matmul(self.pv, wv3[:, kc, :],
                                 xnT[kc][:, nt * 512:(nt + 1) * 512],
                                 start=(kc == 0), stop=(kc == DCH - 1))
                return i
            w = [t_lv] + trk.wait("pk")
            t_mv = K.on("tensor", mv, waits=w)
            ev_v = K.on("scalar",
                        lambda e: e.activation(vT, self.pv, AF.Identity),
                        waits=[t_mv] + trk.wait("vT") +
                        ([self.t_router_V] if c2 == 0 and q == 0 else []))
            trk.set("pk", ev_v)

            fc = self.f2pi[:, c2:c2 + 1]
            tcol, tcol2 = self.tcol, self.tcol2
            K.on("vector",
                 lambda e, fc=fc, t0=t0: e.tensor_scalar_mul(
                     tcol[:], fc, float(t0)),
                 waits=([t_cry] if q == 0 else []) + [self.t_f2pi])
            t_ts = K.on("vector",
                        lambda e, fc=fc: e.tensor_scalar(
                            s1, iota, fc, tcol[:, 0:1], ALU.mult, ALU.add),
                        waits=self.prev.get("cfh_done", []))
            K.on("gpsimd", lambda e: e.tensor_copy(i32, s1), waits=[t_ts])
            K.on("gpsimd",
                 lambda e: e.tensor_tensor(s1, s1, i32, ALU.subtract))
            t_sth = K.on("scalar",
                         lambda e: e.activation(sth, s1, AF.Sin,
                                                scale=TWO_PI),
                         waits=[K.tok("gpsimd")] +
                         self.prev.get("pos_done", []))
            K.on("vector",
                 lambda e: e.tensor_scalar_add(tcol2[:], tcol[:], 0.25))
            t_ts = K.on("vector",
                        lambda e, fc=fc: e.tensor_scalar(
                            s1, iota, fc, tcol2[:, 0:1], ALU.mult, ALU.add),
                        waits=[t_sth])
            K.on("gpsimd", lambda e: e.tensor_copy(i32, s1), waits=[t_ts])
            K.on("gpsimd",
                 lambda e: e.tensor_tensor(s1, s1, i32, ALU.subtract))
            t_cth = K.on("scalar",
                         lambda e: e.activation(cth, s1, AF.Sin,
                                                scale=TWO_PI),
                         waits=[K.tok("gpsimd")] +
                         self.prev.get("pos_done", []))

            K.on("vector",
                 lambda e: e.tensor_tensor(mem_re, vT, cth, ALU.mult),
                 waits=[t_cth, ev_v])
            K.on("vector",
                 lambda e: e.tensor_tensor(mem_im, vT, sth, ALU.mult),
                 waits=[t_sth])
            K.on("vector",
                 lambda e: e.tensor_tensor_scan(
                     mem_re, self.ones1[:].broadcast_to([128, 512]), mem_re,
                     cry[:, 0:1], ALU.mult, ALU.add))
            K.on("vector",
                 lambda e: e.tensor_copy(cry[:, 0:1], mem_re[:, 511:512]))
            K.on("vector",
                 lambda e: e.tensor_tensor_scan(
                     mem_im, self.ones1[:].broadcast_to([128, 512]), mem_im,
                     cry[:, 1:2], ALU.mult, ALU.add))
            K.on("vector",
                 lambda e: e.tensor_copy(cry[:, 1:2], mem_im[:, 511:512]))
            K.on("vector",
                 lambda e: e.tensor_tensor(mem_re, mem_re, cth, ALU.mult))
            K.on("vector",
                 lambda e: e.tensor_tensor(mem_im, mem_im, sth, ALU.mult))
            K.on("vector",
                 lambda e: e.tensor_tensor(mem_re, mem_re, mem_im, ALU.add))
            t_pc = K.on("vector",
                        lambda e, c2=c2: e.tensor_scalar_mul(
                            acc, mem_re, self.g0c[:, c2:c2 + 1]))
            self.prev["pos_done"] = [t_pc]

            t_ts = K.on("vector",
                        lambda e: e.tensor_scalar_mul(s1, krT, R10),
                        waits=[ev_k, t_cth])
            K.on("gpsimd", lambda e: e.tensor_copy(i32, s1), waits=[t_ts])
            K.on("gpsimd",
                 lambda e: e.tensor_tensor(s1, s1, i32, ALU.subtract))
            t_sfh = K.on("scalar",
                         lambda e: e.activation(sfh, s1, AF.Sin,
                                                scale=TWO_PI),
                         waits=[K.tok("gpsimd")] +
                         self.prev.get("as_done", []))
            t_ts = K.on("vector",
                        lambda e: e.tensor_scalar(
                            s1, krT, R10, 0.25, ALU.mult, ALU.add),
                        waits=[t_sfh])
            K.on("gpsimd", lambda e: e.tensor_copy(i32, s1), waits=[t_ts])
            K.on("gpsimd",
                 lambda e: e.tensor_tensor(s1, s1, i32, ALU.subtract))
            t_cfh = K.on("scalar",
                         lambda e: e.activation(cfh, s1, AF.Sin,
                                                scale=TWO_PI),
                         waits=[K.tok("gpsimd")] +
                         self.prev.get("as_done", []))
            self.prev["cfh_done"] = [t_cfh]

            K.on("vector",
                 lambda e: e.tensor_tensor(
                     mem_re[0:128, 1:512], vT[0:128, 1:512],
                     cfh[0:128, 0:511], ALU.mult),
                 waits=[t_cfh])
            if q == 0:
                K.on("vector", lambda e: e.memset(mem_re[0:128, 0:1], 0.0))
            else:
                K.on("vector",
                     lambda e: e.tensor_tensor(
                         mem_re[0:128, 0:1], vT[0:128, 0:1],
                         cry[:, 4:5], ALU.mult))
            K.on("vector",
                 lambda e: e.tensor_tensor(
                     mem_im[0:128, 1:512], vT[0:128, 1:512],
                     sfh[0:128, 0:511], ALU.mult))
            if q == 0:
                K.on("vector", lambda e: e.memset(mem_im[0:128, 0:1], 0.0))
            else:
                K.on("vector",
                     lambda e: e.tensor_tensor(
                         mem_im[0:128, 0:1], vT[0:128, 0:1],
                         cry[:, 5:6], ALU.mult))
            t_lastv = K.on("vector",
                           lambda e: e.tensor_copy(cry[:, 4:5],
                                                   cfh[:, 511:512]))
            trk.set("vT", t_lastv)
            K.on("vector",
                 lambda e: e.tensor_copy(cry[:, 5:6], sfh[:, 511:512]))
            K.on("vector",
                 lambda e: e.tensor_tensor_scan(
                     mem_re, self.ones1[:].broadcast_to([128, 512]), mem_re,
                     cry[:, 2:3], ALU.mult, ALU.add))
            K.on("vector",
                 lambda e: e.tensor_copy(cry[:, 2:3], mem_re[:, 511:512]))
            K.on("vector",
                 lambda e: e.tensor_tensor_scan(
                     mem_im, self.ones1[:].broadcast_to([128, 512]), mem_im,
                     cry[:, 3:4], ALU.mult, ALU.add))
            K.on("vector",
                 lambda e: e.tensor_copy(cry[:, 3:4], mem_im[:, 511:512]))
            K.on("vector",
                 lambda e: e.tensor_tensor(mem_re, mem_re, cfh, ALU.mult))
            K.on("vector",
                 lambda e: e.tensor_tensor(mem_im, mem_im, sfh, ALU.mult))
            K.on("vector",
                 lambda e: e.tensor_tensor(mem_re, mem_re, mem_im, ALU.add))
            t_ac = K.on("vector",
                        lambda e, c2=c2: e.scalar_tensor_tensor(
                            acc, mem_re, self.g1c[:, c2:c2 + 1], acc,
                            ALU.mult, ALU.add))
            self.prev["as_done"] = [t_ac]
            trk.set("krT", t_ac)

            t_h1 = K.on("vector",
                        lambda e, t0=t0: e.tensor_tensor(
                            self.h1stg, acc, self.rsq[:, t0:t0 + 512],
                            ALU.mult),
                        waits=trk.wait("h1stg"))
            t_st = K.on("sync",
                        lambda e, c2=c2, t0=t0: e.dma_start(
                            out=self.h1_scr[c2][:, t0:t0 + 512],
                            in_=self.h1stg),
                        waits=[t_h1], dma="h1st")
            trk.set("h1stg", t_st)
        trk.set("wqp", K.tok("tensor"))
        trk.set("wkp", K.tok("tensor"))

    def emit_wo(self, moe_V_done):
        K, trk = self.K, self.trk
        psB = self.psB
        x_tok = self.x_tok
        bT2 = K.tok("tensor")
        wo1 = K.dram["wo1"]
        wbig = K.tiles["wbig"]
        wbigv = wbig[:].rearrange("p (kc o) -> p kc o", o=1024)
        t_wo1 = K.on("gpsimd",
                     lambda e: e.dma_start(
                         out=wbigv,
                         in_=wo1.rearrange("(kc p) o -> p kc o", p=128)),
                     waits=[bT2], dma="wbig")
        t_h1all = K.tok("sync", dma="h1st")
        bufs = [self.wqp, self.wkp]
        for c in range(TCH):
            par = c % 2
            buf = bufs[par]
            src = self.h1_scr.rearrange("kc p t -> p kc t")[
                :, :, c * 128:(c + 1) * 128]
            dst = buf[:].rearrange("p (kc t) -> p kc t", t=128)
            t_ld = K.on("sync",
                        lambda e, s=src, d=dst: e.dma_start(out=d, in_=s),
                        waits=[t_h1all, bT2] + trk.wait(("wobuf", par)),
                        dma=f"word{par}")
            pao = psB[:, par * 1024:(par + 1) * 1024]

            def mm(e, buf=buf, pao=pao):
                b3 = buf[:].rearrange("p (kc t) -> p kc t", t=128)
                i = None
                for n in range(2):
                    for kc in range(DCH):
                        i = e.matmul(pao[:, n * 512:(n + 1) * 512],
                                     b3[:, kc, :],
                                     wbigv[:, kc, n * 512:(n + 1) * 512],
                                     start=(kc == 0), stop=(kc == DCH - 1))
                return i
            w = [t_ld, t_wo1] + trk.wait(("wops", par))
            if c < 2:
                w += [moe_V_done] + trk.wait("pk") + trk.wait("pv")
            t_mm = K.on("tensor", mm, waits=w)
            trk.set(("wobuf", par), t_mm)
            t_acc = K.on("vector",
                         lambda e, c=c, pao=pao: e.scalar_tensor_tensor(
                             x_tok[:, c * D:(c + 1) * D], pao, 1.0,
                             x_tok[:, c * D:(c + 1) * D],
                             ALU.mult, ALU.add),
                         waits=[t_mm])
            trk.set(("wops", par), t_acc)


def emit_l1_fused(K: KB, xnT_ready):
    E = 16
    t_wrt, router_A = emit_l1_router(K, xnT_ready)
    t_router_V = K.tok("vector")
    wrt3 = K.tiles["wrt"][:].rearrange("p (c e) -> p c e", e=E)
    gwe, uwe, dwe = K.dram["gwe"], K.dram["uwe"], K.dram["dwe"]
    sg, su, sd = K.dram["sg"], K.dram["su"], K.dram["sd"]

    def expert_weight_aps(e):
        if e < E:
            return (gwe[e].rearrange("(kc p) o -> p kc o", p=128),
                    uwe[e].rearrange("(kc p) o -> p kc o", p=128),
                    dwe[e].rearrange("(j p) o -> p j o", p=128))
        half = e - E
        sl = slice(half * 512, (half + 1) * 512)
        return (sg[:, sl].rearrange("(kc p) o -> p kc o", p=128),
                su[:, sl].rearrange("(kc p) o -> p kc o", p=128),
                sd[sl, :].rearrange("(j p) o -> p j o", p=128))

    bT = K.tok("tensor")
    moe = MoeEmitter(K, E, E + 2, expert_weight_aps, wrt3,
                     t_wrt, router_A, bT, ye_mode="narrow")
    la = L1AttnEmitter(K, xnT_ready, t_router_V)

    def moe_parts():
        for e_i in range(E + 2):
            for _ in moe.expert_parts(e_i):
                yield

    gen = moe_parts()
    for c2 in range(DCH):
        la.emit_chunk_start(c2)
        for q in range(4):
            la.emit_tile(c2, q)
            for _ in range(4):
                next(gen, None)
    for _ in gen:
        pass
    la.emit_wo(K.tok("vector"))
    return K.tok("vector")



def emit_store_out(K: KB, waits):
    out = K.dout("out", [T, D], F32)
    x_tok = K.tiles["x_tok"]
    src = x_tok[:].rearrange("p (c d) -> p c d", d=D)
    dst = out.rearrange("(c p) d -> p c d", p=128)
    return K.on("sync", lambda e, s=src, d=dst: e.dma_start(out=d, in_=s),
                waits=waits, dma="out")


def emit_store_debug(K: KB, waits, pairs):
    toks = []
    for nm, ap in pairs:
        toks.append(K.on(
            "sync",
            lambda e, d=K.dram[nm], s=ap: e.dma_start(out=d, in_=s),
            waits=waits, dma="dbg"))
    return toks



def build_program():
    K = KB()
    K.din("x", [T, D], F32)
    K.din("c_eye", [128, 128], BF16)
    K.din("wq", [D, D], F32)
    K.din("wk", [D, D], F32)
    K.din("wv", [D, D], F32)
    K.din("wo", [D, D], F32)
    K.din("c_bq_col", [128, 8], F32)
    K.din("c_bk_col", [128, 8], F32)
    K.din("c_bv_rep", [128, D], F32)
    K.din("c_bo_rep", [128, D], F32)
    K.din("c_gamma_rep", [128, D], F32)
    K.din("c_beta_rep", [128, D], F32)

    for nm in ("c_bq_col", "c_bk_col"):
        tile = K.sb("t_" + nm, list(K.dram[nm].shape), F32)
        K.on("sync",
             lambda e, d=tile, s=K.dram[nm]: e.dma_start(out=d[:], in_=s),
             dma="bqk")
    t_gpc = None
    for nm in ("c_bv_rep", "c_bo_rep", "c_gamma_rep", "c_beta_rep"):
        tile = K.sb("t_" + nm, list(K.dram[nm].shape), BF16)
        t_gpc = K.on("gpsimd",
                     lambda e, d=tile, s=K.dram[nm]: e.dma_start(out=d[:],
                                                                 in_=s),
                     dma="rep")

    K.din("wg", [D, 32], F32)
    K.din("w1e", [32, D, 512], F32)
    K.din("w3e", [32, D, 512], F32)
    K.din("w2e", [32, 512, D], F32)
    K.din("ws1", [D, D], F32)
    K.din("ws3", [D, D], F32)
    K.din("ws2", [D, D], F32)

    K.din("wk1", [D, D], F32)
    K.din("wv1", [D, D], F32)
    K.din("wo1", [D, D], F32)
    K.din("rwt", [D, 16], F32)
    K.din("gwe", [16, D, 512], F32)
    K.din("uwe", [16, D, 512], F32)
    K.din("dwe", [16, 512, D], F32)
    K.din("sg", [D, D], F32)
    K.din("su", [D, D], F32)
    K.din("sd", [D, D], F32)
    K.din("c_freqs_col", [128, 8], F32)
    K.din("c_g0_col", [128, 8], F32)
    K.din("c_g1_col", [128, 8], F32)
    K.din("c_rsq_rep", [128, T], BF16)
    K.din("c_ebias_rep", [128, 16], F32)
    K.din("c_tril16", [128, 256], BF16)
    K.din("c_iota512", [128, 512], F32)
    for nm in ("c_freqs_col", "c_g0_col", "c_g1_col", "c_rsq_rep",
               "c_ebias_rep", "c_tril16"):
        dt_ = BF16 if nm in ("c_tril16", "c_rsq_rep") else F32
        tile = K.sb("t_" + nm, list(K.dram[nm].shape), dt_)
        K.on("sync",
             lambda e, d=tile, s=K.dram[nm]: e.dma_start(out=d[:], in_=s),
             dma="bqk")

    K.din("c_eye32", [128, 128], F32)
    K.dscratch("xn_scr", [T, D], BF16)

    t_x = emit_load_x(K)
    t_eye, t_ms = emit_consts(K)
    ensure_shared_tiles(K)
    done0 = emit_layernorm(K, "0", ready=t_x, t_eye=t_eye, store_xn=True)

    t_wrt, router_A = emit_l0_router(K, done0)
    sc_done, t_dst = emit_l0_sparse_prep(K, t_wrt)
    prep_V = K.tok("vector")

    a0_done = emit_l0_attn(K, done0, t_gpc, sq_guard=sc_done,
                           psA_guard=[router_A],
                           psB_guard=[prep_V])

    E0 = 32
    wrt3 = K.tiles["wrt"][:].rearrange("p (c e) -> p c e", e=E0)
    w1d, w3d, w2d = K.dram["w1e"], K.dram["w3e"], K.dram["w2e"]
    ws1, ws3, ws2 = K.dram["ws1"], K.dram["ws3"], K.dram["ws2"]

    def l0_aps(e):
        if e < E0:
            return (w1d[e].rearrange("(kc p) o -> p kc o", p=128),
                    w3d[e].rearrange("(kc p) o -> p kc o", p=128),
                    w2d[e].rearrange("(j p) o -> p j o", p=128))
        half = e - E0
        sl = slice(half * 512, (half + 1) * 512)
        return (ws1[:, sl].rearrange("(kc p) o -> p kc o", p=128),
                ws3[:, sl].rearrange("(kc p) o -> p kc o", p=128),
                ws2[sl, :].rearrange("(j p) o -> p j o", p=128))

    moe0 = MoeEmitter(K, E0, E0 + 2, l0_aps, wrt3, t_wrt, router_A,
                      K.tok("tensor"), ye_mode="wide", first=False)
    f0_done = emit_l0_sparse_experts(K, moe0, sc_done)
    done1 = emit_layernorm(K, "1", ready=f0_done, stat_tag="0")
    emit_l1_attn(K, done1)
    f1_done = emit_l1_moe(K, done1)

    out_toks = [emit_store_out(K, [f1_done])]
    K.on("sync", lambda e: None, waits=out_toks)
    return K.finish()


def host_inputs(inputs):
    f32 = np.float32
    com = {
        "c_eye": np.eye(128, dtype=ml_dtypes.bfloat16),
        "c_eye32": np.eye(128, dtype=np.float32),
        "wq": np.ascontiguousarray(inputs["L0_attn_Wq"].reshape(D, D), f32),
        "wk": np.ascontiguousarray(inputs["L0_attn_Wk"].reshape(D, D), f32),
        "wv": np.ascontiguousarray(inputs["L0_attn_Wv"].reshape(D, D), f32),
        "wo": np.ascontiguousarray(
            np.asarray(inputs["L0_attn_Wo"], f32).reshape(D, D)),
        "c_bq_col": np.ascontiguousarray(
            np.asarray(inputs["L0_attn_bq"], f32).reshape(-1).reshape(8, 128).T),
        "c_bk_col": np.ascontiguousarray(
            np.asarray(inputs["L0_attn_bk"], f32).reshape(-1).reshape(8, 128).T),
        "c_bv_rep": np.ascontiguousarray(np.tile(
            np.asarray(inputs["L0_attn_bv"], f32).reshape(1, -1), (128, 1))),
        "c_bo_rep": np.ascontiguousarray(np.tile(
            np.asarray(inputs["L0_attn_bo"], f32).reshape(1, -1), (128, 1))),
        "c_gamma_rep": np.ascontiguousarray(np.tile(
            np.asarray(inputs["L0_attn_gamma"], f32).reshape(1, -1),
            (128, 1))),
        "c_beta_rep": np.ascontiguousarray(np.tile(
            np.asarray(inputs["L0_attn_beta"], f32).reshape(1, -1),
            (128, 1))),
        "wg": np.ascontiguousarray(inputs["L0_ffn_Wg"], f32),
        "w1e": np.ascontiguousarray(inputs["L0_ffn_W1"], f32),
        "w3e": np.ascontiguousarray(inputs["L0_ffn_W3"], f32),
        "w2e": np.ascontiguousarray(inputs["L0_ffn_W2"], f32),
        "ws1": np.ascontiguousarray(inputs["L0_ffn_Ws1"], f32),
        "ws3": np.ascontiguousarray(inputs["L0_ffn_Ws3"], f32),
        "ws2": np.ascontiguousarray(inputs["L0_ffn_Ws2"], f32),
        "wk1": np.ascontiguousarray(inputs["L1_attn_Wk"], f32),
        "wv1": np.ascontiguousarray(inputs["L1_attn_Wv"], f32),
        "wo1": np.ascontiguousarray(inputs["L1_attn_Wo"], f32),
        "rwt": np.ascontiguousarray(
            np.asarray(inputs["L1_ffn_router_weight"], f32).T),
        "gwe": np.ascontiguousarray(
            np.asarray(inputs["L1_ffn_gate_w"], f32).transpose(0, 2, 1)),
        "uwe": np.ascontiguousarray(
            np.asarray(inputs["L1_ffn_up_w"], f32).transpose(0, 2, 1)),
        "dwe": np.ascontiguousarray(
            np.asarray(inputs["L1_ffn_down_w"], f32).transpose(0, 2, 1)),
        "sg": np.ascontiguousarray(
            np.asarray(inputs["L1_ffn_sh_gate_w"], f32).T),
        "su": np.ascontiguousarray(
            np.asarray(inputs["L1_ffn_sh_up_w"], f32).T),
        "sd": np.ascontiguousarray(
            np.asarray(inputs["L1_ffn_sh_down_w"], f32).T),
        "c_freqs_col": np.ascontiguousarray(
            np.asarray(inputs["L1_attn_freqs"], f32).reshape(-1)
            .reshape(8, 128).T),
        "c_g0_col": np.ascontiguousarray(
            np.repeat(np.asarray(inputs["L1_attn_gate"], f32)[:, 0], 64)
            .reshape(8, 128).T),
        "c_g1_col": np.ascontiguousarray(
            np.repeat(np.asarray(inputs["L1_attn_gate"], f32)[:, 1], 64)
            .reshape(8, 128).T),
        "c_rsq_rep": np.ascontiguousarray(np.tile(
            (1.0 / np.sqrt(np.arange(1, T + 1, dtype=f32))).reshape(1, -1),
            (128, 1)).astype(ml_dtypes.bfloat16)),
        "c_ebias_rep": np.ascontiguousarray(np.tile(
            np.asarray(inputs["L1_ffn_e_bias"], f32).reshape(1, -1),
            (128, 1))),
        "c_tril16": np.ascontiguousarray(np.tile(
            np.tril(np.ones((16, 16)), -1).reshape(1, -1),
            (128, 1)).astype(ml_dtypes.bfloat16)),
        "c_iota512": np.ascontiguousarray(np.tile(
            np.arange(512, dtype=f32).reshape(1, -1), (128, 1))),
    }
    return com


def kernel(**inputs):
    x = np.asarray(inputs["L0_attn_x"], dtype=np.float32)
    com = host_inputs(inputs)
    nc = build_program()
    in_maps = []
    for c in range(NCORES):
        m = dict(com)
        m["x"] = np.ascontiguousarray(x[c], np.float32)
        in_maps.append(m)
    res = run_bass_kernel_spmd(nc, in_maps, core_ids=list(range(NCORES)))
    out = np.zeros((B, T, D), dtype=np.float32)
    R = res.results
    for c in range(NCORES):
        out[c] = np.asarray(R[c]["out"])
    return out
